# version 05
# baseline (speedup 1.0000x reference)
import numpy as np
import ml_dtypes
from contextlib import ExitStack

import concourse.bass as bass
import concourse.mybir as mybir
from concourse.bass_utils import run_bass_kernel_spmd

F32 = mybir.dt.float32
BF16 = mybir.dt.bfloat16
AL = mybir.AluOpType
AF = mybir.ActivationFunctionType

NCORES = 8
T = 2048
D = 2048
HQ, HK, HD, E = 16, 8, 128, 64
DFF = 8192
NT5 = 4
NTT = 16
NDC = 16
NFC = 64
SEQ = 8192
EPS_LN = 1e-5

ESEM = {"tensor": "sT", "vector": "sV", "scalar": "sS", "gpsimd": "sG", "sync": "sY"}
NQ = {"ld": 8, "st": 4, "sc": 2}
QENG = {"ld": "sync", "st": "gpsimd", "sc": "scalar"}
QSEMS = {q: [f"s{q}{i}" for i in range(n)] for q, n in NQ.items()}
ALL_SEMS = list(ESEM.values()) + [s for ss in QSEMS.values() for s in ss]


class Res:
    __slots__ = ("w", "r")

    def __init__(self):
        self.w = None
        self.r = []


class Prog:
    def __init__(self, sems):
        self.sems = sems
        self.cnt = {k: 0 for k in sems}
        self.items = []
        self.fence = []
        self.qidx = {q: 0 for q in NQ}

    def set_fence(self):
        self.fence = [(s, v) for s, v in self.cnt.items() if v > 0]

    def _waits(self, reads, writes, extra):
        waits = {}

        def add(tok):
            if tok is None:
                return
            s, v = tok
            if waits.get(s, -1) < v:
                waits[s] = v

        for r in reads:
            add(r.w)
        for w in writes:
            add(w.w)
            for t in w.r:
                add(t)
        for t in extra:
            add(t)
        for t in self.fence:
            add(t)
        return waits

    def op(self, eng, fn, reads=(), writes=(), extra=()):
        sem = ESEM[eng]
        waits = self._waits(reads, writes, extra)
        self.cnt[sem] += 1
        tok = (sem, self.cnt[sem])
        self.items.append((eng, fn, waits, sem, 1))
        for r in reads:
            r.r.append(tok)
        for w in writes:
            w.w = tok
            w.r = []
        return tok

    def dma(self, queue, dst, src, reads=(), writes=(), extra=()):
        eng = QENG[queue]
        idx = self.qidx[queue]
        self.qidx[queue] = (idx + 1) % NQ[queue]
        sem = QSEMS[queue][idx]
        waits = self._waits(reads, writes, extra)
        if self.cnt[sem] > 0 and waits.get(sem, -1) < self.cnt[sem]:
            waits[sem] = self.cnt[sem]
        self.cnt[sem] += 16
        tok = (sem, self.cnt[sem])
        self.items.append(
            (eng, (lambda E, d=dst, s_=src: E.dma_start(out=d, in_=s_)), waits, sem, 16))
        for r in reads:
            r.r.append(tok)
        for w in writes:
            w.w = tok
            w.r = []
        return tok

    def replay(self, engname, eng):
        waited = {}
        for e, fn, waits, sem, inc in self.items:
            if e != engname:
                continue
            for ws in sorted(waits):
                wv = waits[ws]
                if waited.get(ws, -1) < wv:
                    eng.wait_ge(self.sems[ws], wv)
                    waited[ws] = wv
            instr = fn(eng)
            if instr is not None:
                instr.then_inc(self.sems[sem], inc)


def build_program(debug=0):
    nc = bass.Bass("TRN2", target_bir_lowering=False)

    def din(name, shape, dtype=F32):
        return nc.dram_tensor(name, list(shape), dtype, kind="ExternalInput").ap()

    xbT_d = din("xbT", (D, SEQ))
    xrT_d = din("xrT", (D, T))
    pos_d = din("pos", (1, SEQ))
    iidx_d = din("iidx", (1, E))
    wqT_d = din("wqT", (4, 128, NDC * 128))
    wkT_d = din("wkT", (HK, 128, NDC * 128))
    wvT_d = din("wvT", (NDC, 128, 1024))
    woT_d = din("woT", (NDC, 128, NDC * 128))
    rT_d = din("rT", (128, E))
    r2T_d = din("r2T", (128, E))
    w1T_d = din("w1T", (D, DFF))
    w2T_d = din("w2T", (DFF, D))
    wsrc1_d = din("wsrc1", (1024, 128, 128))
    wsrc2_d = din("wsrc2", (1024, 128, 128))
    bq_d = din("bq", (1, 512))
    bk_d = din("bk", (1, HK * HD))
    bv_d = din("bv", (1, HK * HD))
    bo_d = din("bo", (1, D))
    b1_d = din("b1", (1, DFF))
    b2_d = din("b2", (1, D))
    onef_d = din("ones_f", (1, 128))
    idnf_d = din("idnf", (128, 128))
    nsel4_d = din("nsel4", (128, 16), BF16)
    bsel4_d = din("bsel4", (4, 512), BF16)

    out_d = nc.dram_tensor("outT", [D, T], F32, kind="ExternalOutput").ap()
    dbg_d = {}
    if debug:
        for nm, shp in [("d_q", (128, 512)), ("d_qp", (128, T)), ("d_kp", (128, T)),
                        ("d_v", (128, 1024)), ("d_A", (128, 128)), ("d_mx", (128, 128)),
                        ("d_x1", (128, 512)), ("d_ln2", (128, 512)), ("d_h", (128, 512)),
                        ("d_st", (4, T)), ("d_w1q", (128, 1024)), ("d_o", (128, 512)),
                        ("d_tb", (128, T)), ("d_m2", (128, 512)), ("d_iv", (4, 512)), ("d_y", (128, 512)),
                        ("d_qs", (1, 8))]:
            dbg_d[nm] = nc.dram_tensor(nm, list(shp), F32, kind="ExternalOutput").ap()

    w1q_d = nc.dram_tensor("w1q_s", [NFC, 128, NDC * 128], BF16, kind="Internal").ap()
    w2q_d = nc.dram_tensor("w2q_s", [NFC, 128, D], BF16, kind="Internal").ap()
    x1_d = nc.dram_tensor("x1_s", [D, T], F32, kind="Internal").ap()
    ln2_d = nc.dram_tensor("ln2_s", [D, T], BF16, kind="Internal").ap()
    mix_d = nc.dram_tensor("mix_s", [128, 16, SEQ], BF16, kind="Internal").ap()
    spl_d = nc.dram_tensor("spl_s", [D, 512], F32, kind="Internal").ap()
    st2_d = nc.dram_tensor("st2_s", [2, T], BF16, kind="Internal").ap()
    sq_d = nc.dram_tensor("sq_s", [2, 16, 1024], F32, kind="Internal").ap()
    tb_d = nc.dram_tensor("tb_s", [2, 1, 16], F32, kind="Internal").ap()
    trow_d = nc.dram_tensor("trow_s", [2, 1, 16], F32, kind="Internal").ap()
    corr_d = nc.dram_tensor("corr_s", [2, 256], BF16, kind="Internal").ap()
    corrv_d = nc.dram_tensor("corrv_s", [2, 1024], BF16, kind="Internal").ap()

    es = ExitStack()
    with es:
        Sb = lambda nm, shp, dt=F32: es.enter_context(nc.sbuf_tensor(nm, list(shp), dt))
        U1 = Sb("U1", [128, 32768], BF16)
        U2 = Sb("U2", [128, 16384], BF16)
        U3 = Sb("U3", [128, 16384], BF16)
        U4 = Sb("U4", [128, 10752], BF16)
        U5 = Sb("U5", [128, 4096], BF16)
        U6 = Sb("U6", [128, 4096], F32)
        U7 = Sb("U7", [128, 3072], F32)
        XRE = lambda kx: U6[:, kx * 512:(kx + 1) * 512]
        OT = lambda ko: U6[:, 1024 + ko * 512: 1024 + (ko + 1) * 512]
        X1T = U6[:, 2048:2560]
        INV4 = U6[0:4, 1536:2048]
        BCM = U6[:, 2560:3072]
        BCS = U6[:, 3072:3584]
        WRK = U6[:, 3584:4096]
        
        c_idnf = Sb("c_idnf", [128, 128])
        c_nsel4 = Sb("c_nsel4", [128, 16], BF16)
        c_bsel4 = Sb("c_bsel4", [4, 512], BF16)
        c_onesb = Sb("c_onesb", [128, 512], BF16)
        c_onesf = Sb("c_onesf", [1, 128])
        c_rT = Sb("c_rT", [128, E], BF16)
        c_r2T = Sb("c_r2T", [128, E], BF16)
        c_iidx = Sb("c_iidx", [1, E])
        c_invf = Sb("c_invf", [1, 128])
        kPI = Sb("kPI", [128, 6])
        kCp = Sb("kCp", [128, 11])
        kCn = Sb("kCn", [128, 11])
        tinyb = Sb("tinyb", [1, 8])
        st2_bf = Sb("st2_bf", [2, T], BF16)
        s_cols = Sb("s_cols", [128, NTT])
        row_a = Sb("row_a", [1, 512])
        row_b = Sb("row_b", [1, 512])
        row_c = Sb("row_c", [1, 512])
        mo_bf = Sb("mo_bf", [1, 512], BF16)
        so_bf = Sb("so_bf", [1, 512], BF16)
        corr = Sb("corr", [2, 256], BF16)
        corrv = Sb("corrv", [2, 1024], BF16)
        inv4b = Sb("inv4b", [4, 512], BF16)
        q_acc = Sb("q_acc", [128, 2])
        q_tc = Sb("q_tc", [128, 6])
        qsc = Sb("qsc", [1, 8])
        qth = Sb("qth", [1, 12])
        a_hq = Sb("a_hq", [128, 128])
        acc_a = Sb("acc_a", [128, 128])
        acc_b = Sb("acc_b", [128, 128])
        mixT = Sb("mixT", [128, 128], BF16)

        pb = [es.enter_context(nc.psum_tensor(f"pb{i}", [128, 512], F32)) for i in range(8)]

        sems = {k: es.enter_context(nc.semaphore(k)) for k in ALL_SEMS}
        P = Prog(sems)
        V, SC, TE, G = "vector", "scalar", "tensor", "gpsimd"

        R = Res
        rU1, rU2, rU3, rU4, rU5, rU6, rU7 = R(), R(), R(), R(), R(), R(), R()
        r_pb = [R() for _ in range(8)]
        r_const, r_kpi, r_tiny, r_rows, r_st2, r_scols = R(), R(), R(), R(), R(), R()
        r_corr = [R(), R()]
        r_corrv, r_inv = R(), R()
        r_qacc, r_qtc = R(), R()
        r_xst = [R(), R()]
        r_xre = [R(), R()]
        r_ot = [R(), R()]
        r_x1t = R()
        r_bcm, r_bcs, r_wrk = R(), R(), R()
        r_ahq, r_acca, r_accb, r_mixT = R(), R(), R(), R()
        r_mrow = R()
        r_dw1, r_dw2, r_dx1, r_dln2, r_dmix, r_dspl = R(), R(), R(), R(), R(), R()

        def TSc(eng, out, in0, s1, s2=None, op0=AL.mult, op1=None, **kw):
            if op1 is None:
                return P.op(eng, lambda E_: E_.tensor_scalar(
                    out=out, in0=in0, scalar1=s1, scalar2=None, op0=op0), **kw)
            return P.op(eng, lambda E_: E_.tensor_scalar(
                out=out, in0=in0, scalar1=s1, scalar2=s2, op0=op0, op1=op1), **kw)

        def TTn(eng, out, in0, in1, aop, **kw):
            return P.op(eng, lambda E_: E_.tensor_tensor(out=out, in0=in0, in1=in1, op=aop), **kw)

        def STT(out, in0, sc, in1, op0, op1, **kw):
            return P.op(V, lambda E_: E_.scalar_tensor_tensor(
                out=out, in0=in0, scalar=sc, in1=in1, op0=op0, op1=op1), **kw)

        def ACT(out, in_, func, bias=None, scale=None, accum=None, **kw):
            def f(E_):
                kwargs = {}
                if bias is not None:
                    kwargs["bias"] = bias
                if scale is not None:
                    kwargs["scale"] = scale
                if accum is not None:
                    kwargs["accum_out"] = accum
                return E_.activation(out=out, in_=in_, func=func, **kwargs)
            return P.op(SC, f, **kw)

        def CP(eng, out, in_, **kw):
            return P.op(eng, lambda E_: E_.tensor_copy(out=out, in_=in_), **kw)

        def MM(out, lhsT, rhs, start, stop, **kw):
            return P.op(TE, lambda E_: E_.matmul(out, lhsT, rhs, start=start, stop=stop), **kw)

        def TRp(out, in_, ident, **kw):
            return P.op(TE, lambda E_: E_.transpose(out, in_, ident), **kw)

        def MEMS(eng, ap_, val, **kw):
            return P.op(eng, lambda E_: E_.memset(ap_, val), **kw)

        def RECIP(out, in_, **kw):
            return P.op(V, lambda E_: E_.reciprocal(out=out, in_=in_), **kw)

        P.dma("ld", c_idnf[:], idnf_d, writes=[r_const])
        P.dma("ld", c_nsel4[:], nsel4_d, writes=[r_const])
        P.dma("ld", c_bsel4[:], bsel4_d, writes=[r_const])
        P.dma("ld", c_onesf[:], onef_d, writes=[r_const])
        P.dma("ld", c_iidx[:], iidx_d, writes=[r_const])
        P.dma("ld", U7[:, 0:E], rT_d, writes=[rU7])
        P.dma("ld", U7[:, E:2 * E], r2T_d, writes=[rU7])
        MEMS(V, c_onesb[:], 1.0, writes=[r_const])
        CP(V, c_rT[:], U7[:, 0:E], reads=[rU7], writes=[r_const])
        CP(V, c_r2T[:], U7[:, E:2 * E], reads=[rU7], writes=[r_const])
        for (ap_in, off, n) in [(bq_d, 0, 512), (bk_d, 2048, HK * HD), (bv_d, 3072, HK * HD)]:
            for c in range(max(1, n // 1024)):
                w_ = min(1024, n - c * 1024)
                P.dma("ld", U7[0:1, 2048:2048 + w_], ap_in[0:1, c * 1024: c * 1024 + w_],
                      writes=[rU7])
                ACT(U5[0:1, off + c * 1024: off + c * 1024 + w_], U7[0:1, 2048:2048 + w_],
                    AF.Copy, reads=[rU7], writes=[rU5])

        MEMS(V, tinyb[0:1, 0:1], 1.0, writes=[r_tiny])
        ACT(tinyb[0:1, 0:1], tinyb[0:1, 0:1], AF.Arctan, reads=[r_tiny], writes=[r_tiny])
        MM(pb[0][:, 0:1], c_onesf[0:1, :], tinyb[0:1, 0:1], True, True,
           reads=[r_tiny, r_const], writes=[r_pb[0]])
        TSc(V, kPI[:, 0:1], pb[0][:, 0:1], 4.0, reads=[r_pb[0]], writes=[r_kpi])
        TSc(V, kPI[:, 1:2], kPI[:, 0:1], 2.0, reads=[r_kpi], writes=[r_kpi])
        TSc(V, kPI[:, 2:3], kPI[:, 1:2], -1.0, reads=[r_kpi], writes=[r_kpi])
        TSc(V, kPI[:, 3:4], kPI[:, 0:1], 0.5, reads=[r_kpi], writes=[r_kpi])
        TSc(V, kPI[:, 4:5], kPI[:, 0:1], -1.0, reads=[r_kpi], writes=[r_kpi])
        for i in range(11):
            TSc(V, kCp[:, i:i + 1], kPI[:, 1:2], float(2.0 ** (10 - i)),
                reads=[r_kpi], writes=[r_kpi])
            TSc(V, kCn[:, i:i + 1], kCp[:, i:i + 1], -1.0, reads=[r_kpi], writes=[r_kpi])

        MEMS(V, tinyb[0:1, 1:2], 10000.0, writes=[r_tiny])
        ACT(tinyb[0:1, 1:2], tinyb[0:1, 1:2], AF.Ln, reads=[r_tiny], writes=[r_tiny])
        TSc(V, row_a[0:1, 0:E], c_iidx[0:1, :], tinyb[0:1, 1:2], -1.0 / 64.0,
            op0=AL.mult, op1=AL.mult, reads=[r_const, r_tiny], writes=[r_rows])
        ACT(row_a[0:1, 0:E], row_a[0:1, 0:E], AF.Exp, reads=[r_rows], writes=[r_rows])
        CP(V, c_invf[0:1, 0:E], row_a[0:1, 0:E], reads=[r_rows], writes=[r_const])
        CP(V, c_invf[0:1, E:128], row_a[0:1, 0:E], reads=[r_rows], writes=[r_const])
        def build_tables(g):
          for t5 in range(NT5):
            sl5 = slice(t5 * 512, (t5 + 1) * 512)
            gsl = slice(g * T + t5 * 512, g * T + (t5 + 1) * 512)
            P.dma("ld", row_a[0:1, :], pos_d[0:1, gsl], writes=[r_rows])
            MM(pb[1][:, :], c_invf[0:1, :], row_a[0:1, :], True, True,
               reads=[r_rows, r_const], writes=[r_pb[1]])
            CP(V, U6[:, 512:1024], pb[1][:, :], reads=[r_pb[1]], writes=[rU6])
            for i in range(11):
                TSc(V, U6[:, 1024:1536], U6[:, 512:1024], kCp[:, i:i + 1], None, op0=AL.is_ge,
                    reads=[rU6, r_kpi], writes=[rU6])
                STT(U6[:, 512:1024], U6[:, 1024:1536], kCn[:, i:i + 1], U6[:, 512:1024],
                    AL.mult, AL.add, reads=[rU6, r_kpi], writes=[rU6])
            TSc(V, U6[:, 1024:1536], U6[:, 512:1024], kPI[:, 0:1], None, op0=AL.is_ge,
                reads=[rU6, r_kpi], writes=[rU6])
            STT(U6[:, 512:1024], U6[:, 1024:1536], kPI[:, 2:3], U6[:, 512:1024],
                AL.mult, AL.add, reads=[rU6, r_kpi], writes=[rU6])
            ACT(U4[:, 2048 + t5 * 512: 2048 + (t5 + 1) * 512], U6[:, 512:1024], AF.Sin,
                reads=[rU6], writes=[rU4])
            TSc(V, U6[:, 1024:1536], U6[:, 512:1024], kPI[:, 3:4], None, op0=AL.add,
                reads=[rU6, r_kpi], writes=[rU6])
            TSc(V, U6[:, 0:512], U6[:, 1024:1536], kPI[:, 0:1], None, op0=AL.is_ge,
                reads=[rU6, r_kpi], writes=[rU6])
            STT(U6[:, 1024:1536], U6[:, 0:512], kPI[:, 2:3], U6[:, 1024:1536],
                AL.mult, AL.add, reads=[rU6, r_kpi], writes=[rU6])
            ACT(U4[:, t5 * 512:(t5 + 1) * 512], U6[:, 1024:1536], AF.Sin,
                reads=[rU6], writes=[rU4])
          TSc(V, U4[64:128, 2048:4096], U4[64:128, 2048:4096], -1.0, reads=[rU4], writes=[rU4])

        def tbc(t5):
            return U4[:, t5 * 512:(t5 + 1) * 512]

        def tbs(t5):
            return U4[:, 2048 + t5 * 512: 2048 + (t5 + 1) * 512]

        def phase_x(g):
          for t5 in range(NT5):
            sl5 = slice(t5 * 512, (t5 + 1) * 512)
            gsl = slice(g * T + t5 * 512, g * T + (t5 + 1) * 512)
            for dc in range(NDC):
                k = dc % 2
                xsl = slice(dc * T + t5 * 512, dc * T + (t5 + 1) * 512)
                sqsl = slice(9216, 9728)
                P.dma("ld", U6[:, k * 512:(k + 1) * 512], xbT_d[dc * 128:(dc + 1) * 128, gsl],
                      writes=[r_xst[k]])
                CP(V, U1[:, xsl], U6[:, k * 512:(k + 1) * 512], reads=[r_xst[k]], writes=[rU1])
                ACT(U4[:, sqsl], U6[:, k * 512:(k + 1) * 512], AF.Square,
                    reads=[r_xst[k]], writes=[rU4])
                MM(pb[0][0:1, :], c_onesb[:, 0:1], U1[:, xsl], dc == 0, dc == NDC - 1,
                   reads=[rU1, r_const], writes=[r_pb[0]])
                MM(pb[1][0:1, :], c_onesb[:, 0:1], U4[:, sqsl], dc == 0, dc == NDC - 1,
                   reads=[rU4, r_const], writes=[r_pb[1]])
            TSc(V, row_a[0:1, :], pb[0][0:1, :], 1.0 / 2048.0, reads=[r_pb[0]], writes=[r_rows])
            TSc(V, row_b[0:1, :], pb[1][0:1, :], 1.0 / 2048.0, reads=[r_pb[1]], writes=[r_rows])
            ACT(row_c[0:1, :], row_a[0:1, :], AF.Square, reads=[r_rows], writes=[r_rows])
            TTn(V, row_b[0:1, :], row_b[0:1, :], row_c[0:1, :], AL.subtract,
                reads=[r_rows], writes=[r_rows])
            TSc(V, row_b[0:1, :], row_b[0:1, :], EPS_LN, None, op0=AL.add,
                reads=[r_rows], writes=[r_rows])
            RECIP(row_c[0:1, :], row_b[0:1, :], reads=[r_rows], writes=[r_rows])
            ACT(row_c[0:1, :], row_c[0:1, :], AF.Sqrt, reads=[r_rows], writes=[r_rows])
            for ttl in range(4):
                tt = t5 * 4 + ttl
                TRp(pb[7][:, tt:tt + 1], row_c[0:1, ttl * 128:(ttl + 1) * 128],
                    c_onesf[0:1, 0:1], reads=[r_rows, r_const], writes=[r_pb[7]])
            ACT(row_b[0:1, :], row_b[0:1, :], AF.Sqrt, reads=[r_rows], writes=[r_rows])
            TSc(V, so_bf[0:1, :], row_a[0:1, :], -1.0, reads=[r_rows], writes=[r_mrow])
            CP(V, mo_bf[0:1, :], row_b[0:1, :], reads=[r_rows], writes=[r_mrow])
            P.dma("sc", st2_d[0:1, sl5], so_bf[0:1, :], reads=[r_mrow], writes=[r_st2])
            P.dma("sc", st2_d[1:2, sl5], mo_bf[0:1, :], reads=[r_mrow], writes=[r_st2])
            P.dma("ld", st2_bf[:, sl5], st2_d[:, sl5], reads=[r_st2], writes=[r_st2])
            if debug and g == 0:
                P.dma("st", dbg_d["d_st"][0:1, sl5], row_a[0:1, :], reads=[r_rows])
                P.dma("st", dbg_d["d_st"][1:2, sl5], row_c[0:1, :], reads=[r_rows])
          CP(V, s_cols[:], pb[7][:, 0:NTT], reads=[r_pb[7]], writes=[r_scols])

        def phase_v(g):
          for dc in range(NDC):
              k = dc % 2
              P.dma("ld", U7[:, k * 1024: k * 1024 + 1024], wvT_d[dc], writes=[rU7])
              CP(V, U2[:, dc * 1024:(dc + 1) * 1024], U7[:, k * 1024: k * 1024 + 1024],
                 reads=[rU7], writes=[rU2])
          for fc2 in range(2):
              for dc in range(NDC):
                  MM(pb[2][0:1, :], c_onesb[:, 0:1],
                     U2[:, dc * 1024 + fc2 * 512: dc * 1024 + (fc2 + 1) * 512],
                     dc == 0, dc == NDC - 1, reads=[rU2, r_const], writes=[r_pb[2]])
              CP(V, so_bf[0:1, :], pb[2][0:1, :], reads=[r_pb[2]], writes=[r_mrow])
              P.dma("sc", corrv_d[0:1, fc2 * 512:(fc2 + 1) * 512], so_bf[0:1, :],
                    reads=[r_mrow], writes=[r_corrv])
          P.dma("sc", corrv_d[1:2, :], U5[0:1, 3072:4096], reads=[rU5], writes=[r_corrv])
          P.dma("ld", corrv[:], corrv_d[:], reads=[r_corrv], writes=[r_corrv])
          for tt in range(NTT):
              for fc2 in range(2):
                  bank = (tt * 2 + fc2) % 2
                  for dc in range(NDC):
                      MM(pb[bank][:, :], U1[:, dc * T + tt * 128: dc * T + (tt + 1) * 128],
                         U2[:, dc * 1024 + fc2 * 512: dc * 1024 + (fc2 + 1) * 512],
                         dc == 0, False, reads=[rU1, rU2], writes=[r_pb[bank]])
                  MM(pb[bank][:, :], st2_bf[:, tt * 128:(tt + 1) * 128],
                     corrv[:, fc2 * 512:(fc2 + 1) * 512], False, True,
                     reads=[r_st2, r_corrv], writes=[r_pb[bank]])
                  ACT(U3[:, tt * 1024 + fc2 * 512: tt * 1024 + (fc2 + 1) * 512],
                      pb[bank][:, :], AF.Copy, scale=s_cols[:, tt:tt + 1],
                      reads=[r_pb[bank], r_scols], writes=[rU3])
          if debug:
              CP(V, U6[:, 0:1024], U3[:, 0:1024], reads=[rU3],
                 writes=[rU6, r_xre[0], r_xre[1]])
              P.dma("st", dbg_d["d_v"], U6[:, 0:1024], reads=[rU6, r_xre[0], r_xre[1]])
        P.set_fence()
        def head_pipeline(is_q, h, g):
            wsrc = wqT_d[h] if is_q else wkT_d[h]
            k = h % 2
            wslot = slice(4096, 6144)
            P.dma("ld", U7[:, 0:2048], wsrc, writes=[rU7])
            CP(V, U4[:, wslot], U7[:, 0:2048], reads=[rU7], writes=[rU4])
            for dc in range(NDC):
                MM(pb[7][0:1, 0:128], c_onesb[:, 0:1],
                   U4[:, wslot.start + dc * 128: wslot.start + (dc + 1) * 128],
                   dc == 0, dc == NDC - 1, reads=[rU4, r_const], writes=[r_pb[7]])
            CP(V, so_bf[0:1, 0:128], pb[7][0:1, 0:128], reads=[r_pb[7]], writes=[r_mrow])
            boff = h * 128 if is_q else 2048 + h * 128
            P.dma("sc", corr_d[0:1, k * 128:(k + 1) * 128], so_bf[0:1, 0:128],
                  reads=[r_mrow], writes=[r_corr[k]])
            P.dma("sc", corr_d[1:2, k * 128:(k + 1) * 128], U5[0:1, boff:boff + 128],
                  reads=[rU5], writes=[r_corr[k]])
            P.dma("ld", corr[:, k * 128:(k + 1) * 128], corr_d[:, k * 128:(k + 1) * 128],
                  reads=[r_corr[k]], writes=[r_corr[k]])
            for t5 in range(NT5):
                sl5 = slice(t5 * 512, (t5 + 1) * 512)
                bank = t5 % 2
                for dc in range(NDC):
                    MM(pb[bank][:, :],
                       U4[:, wslot.start + dc * 128: wslot.start + (dc + 1) * 128],
                       U1[:, dc * T + t5 * 512: dc * T + (t5 + 1) * 512],
                       dc == 0, False, reads=[rU4, rU1], writes=[r_pb[bank]])
                MM(pb[bank][:, :], corr[:, k * 128:(k + 1) * 128], st2_bf[:, sl5],
                   False, True, reads=[r_corr[k], r_st2], writes=[r_pb[bank]])
                if debug and is_q and h == 0 and t5 == 0 and g == 0:
                    CP(V, U6[:, 0:512], pb[bank][:, :], reads=[r_pb[bank]],
                       writes=[rU6, r_xre[0]])
                    P.dma("st", dbg_d["d_q"], U6[:, 0:512], reads=[rU6, r_xre[0]])
                TTn(V, U4[:, 8192:8704], pb[bank][:, :], tbc(t5), AL.mult,
                    reads=[r_pb[bank], rU4], writes=[rU4])
                TTn(V, U4[:, 8704:9216], pb[bank][:, :], tbs(t5), AL.mult,
                    reads=[r_pb[bank], rU4], writes=[rU4])
                sqsl = slice(9216, 9728)
                ACT(U4[:, sqsl], pb[bank][:, :], AF.Square, reads=[r_pb[bank]], writes=[rU4])
                MM(pb[4][0:4, :], c_nsel4[:, t5 * 4:(t5 + 1) * 4], U4[:, sqsl],
                   t5 == 0, t5 == NT5 - 1, reads=[rU4, r_const], writes=[r_pb[4]])
                ppb = 2 + t5 // 2
                half = t5 % 2
                hsl = slice(half * 64, half * 64 + 64)
                MM(pb[ppb][hsl, :], c_rT[:], U4[:, 8192:8704], True, False,
                   reads=[rU4, r_const], writes=[r_pb[ppb]])
                MM(pb[ppb][hsl, :], c_r2T[:], U4[:, 8704:9216], False, True,
                   reads=[rU4, r_const], writes=[r_pb[ppb]])
            TSc(V, INV4, pb[4][0:4, :], 1e-12, None, op0=AL.max,
                reads=[r_pb[4]], writes=[r_inv])
            RECIP(INV4, INV4, reads=[r_inv], writes=[r_inv])
            ACT(INV4, INV4, AF.Sqrt, reads=[r_inv], writes=[r_inv])
            CP(V, inv4b[:, :], INV4, reads=[r_inv], writes=[r_inv])
            if debug and is_q and h == 0 and g == 0:
                P.dma("st", dbg_d["d_iv"], INV4, reads=[r_inv])
            for t5 in range(NT5):
                ppb = 2 + t5 // 2
                half = t5 % 2
                hsl = slice(half * 64, half * 64 + 64)
                MM(pb[5][:, :], c_bsel4[:, t5 * 128:(t5 + 1) * 128], inv4b[:, :], True, True,
                   reads=[r_inv, r_const], writes=[r_pb[5]])
                CP(V, U6[0:64, 0:512], pb[ppb][hsl, :], reads=[r_pb[ppb]], writes=[rU6])
                CP(V, U6[64:128, 0:512], pb[ppb][hsl, :], reads=[r_pb[ppb]], writes=[rU6])
                TTn(V, U6[:, 512:1024], U6[:, 0:512], pb[5][:, :], AL.mult,
                    reads=[rU6, r_pb[5]], writes=[rU6])
                TSc(V, U6[64:128, 512:1024], U6[64:128, 512:1024], kPI[64:128, 3:4], None,
                    op0=AL.add, reads=[r_kpi], writes=[rU6])
                TSc(V, U6[:, 1024:1536], U6[:, 512:1024], kPI[:, 0:1], None, op0=AL.is_gt,
                    reads=[rU6, r_kpi], writes=[rU6])
                STT(U6[:, 0:512], U6[:, 1024:1536], kPI[:, 2:3], U6[:, 512:1024],
                    AL.mult, AL.add, reads=[rU6, r_kpi], writes=[rU6])
                TSc(V, U6[:, 1024:1536], U6[:, 512:1024], kPI[:, 4:5], None, op0=AL.is_lt,
                    reads=[rU6, r_kpi], writes=[rU6])
                STT(U6[:, 512:1024], U6[:, 1024:1536], kPI[:, 1:2], U6[:, 0:512],
                    AL.mult, AL.add, reads=[rU6, r_kpi], writes=[rU6])
                if is_q:
                    ACT(U4[:, 6144 + t5 * 512: 6144 + (t5 + 1) * 512], U6[:, 512:1024],
                        AF.Sin, reads=[rU6], writes=[rU4])
                else:
                    ACT(U2[:, h * T + t5 * 512: h * T + (t5 + 1) * 512], U6[:, 512:1024],
                        AF.Sin, reads=[rU6], writes=[rU2])

        def phase_heads(g):
          for hk in range(HK):
              head_pipeline(False, hk, g)
          if debug and g == 0:
              CP(V, U6[:, 0:2048], U2[:, 0:T], reads=[rU2],
                 writes=[rU6, r_inv, r_xre[0], r_xre[1]])
              P.dma("st", dbg_d["d_kp"], U6[:, 0:2048], reads=[rU6, r_inv])
          for hq in range(4):
              head_pipeline(True, hq, g)
              if debug and hq == 0 and g == 0:
                  CP(V, U6[:, 0:2048], U4[:, 6144:8192], reads=[rU4],
                     writes=[rU6, r_inv, r_xre[0], r_xre[1]])
                  P.dma("st", dbg_d["d_qp"], U6[:, 0:2048], reads=[rU6, r_inv])
              for hk in range(HK):
                  for t5 in range(NT5):
                      TTn(V, U4[:, 9728:10240], U4[:, 6144 + t5 * 512: 6144 + (t5 + 1) * 512],
                          U2[:, hk * T + t5 * 512: hk * T + (t5 + 1) * 512],
                          AL.mult, reads=[rU4, rU2], writes=[rU4])
                      for ttl in range(4):
                          tt = t5 * 4 + ttl
                          MM(pb[6][:, tt * 8 + hk: tt * 8 + hk + 1],
                             U4[:, 9728 + ttl * 128: 9728 + (ttl + 1) * 128],
                             c_onesb[:, 0:1], True, True,
                             reads=[rU4, r_const], writes=[r_pb[6]])
              CP(V, a_hq[:], pb[6][:, 0:128], reads=[r_pb[6]], writes=[r_ahq])
              if debug and hq == 0 and g == 0:
                  P.dma("st", dbg_d["d_A"], a_hq[:], reads=[r_ahq])
              for tt in range(NTT):
                  TSc(V, acc_a[:], U3[:, tt * 1024: tt * 1024 + 128],
                      a_hq[:, tt * 8: tt * 8 + 1], None, op0=AL.mult,
                      reads=[rU3, r_ahq], writes=[r_acca])
                  src_, dst_, rs, rd = acc_a, acc_b, r_acca, r_accb
                  for hk in range(1, HK):
                      STT(dst_[:], U3[:, tt * 1024 + hk * 128: tt * 1024 + (hk + 1) * 128],
                          a_hq[:, tt * 8 + hk: tt * 8 + hk + 1], src_[:], AL.mult, AL.add,
                          reads=[rU3, r_ahq, rs], writes=[rd])
                      src_, dst_, rs, rd = dst_, src_, rd, rs
                  TRp(pb[7][:, 0:128], src_[:], c_idnf[:], reads=[rs, r_const], writes=[r_pb[7]])
                  CP(V, mixT[:], pb[7][:, 0:128].rearrange("p (b i) -> p i b", i=16),
                     reads=[r_pb[7]], writes=[r_mixT])
                  r0 = 512 * hq + (g * 16 + tt) * 8
                  P.dma("st", mix_d[:, :, r0:r0 + 8], mixT[:],
                        reads=[r_mixT], writes=[r_dmix])
                  if debug and hq == 0 and tt == 0 and g == 0:
                      CP(V, U6[:, 0:128], src_[:], reads=[rs], writes=[rU6, r_xre[0]])
                      P.dma("st", dbg_d["d_mx"], U6[:, 0:128], reads=[rU6, r_xre[0]])

        for g in range(4):
            P.set_fence()
            build_tables(g)
            P.set_fence()
            phase_x(g)
            P.set_fence()
            phase_v(g)
            P.set_fence()
            phase_heads(g)
        P.set_fence()
        r_stg = [R(), R()]
        r_abt = [R(), R()]
        r_acc1 = [R(), R()]
        r_sq = [R(), R()]
        r_arr2 = [R(), R()]
        r_tb = [R(), R()]
        r_trow = [R(), R()]
        r_qsc = R()
        ACC1 = lambda m: U7[:, m * 128:(m + 1) * 128]
        STG1 = lambda m: U6[:, m * 1024:(m + 1) * 1024]
        ABT1 = lambda m: U6[:, 2048 + m * 1024: 2048 + (m + 1) * 1024]
        ARR2 = lambda m: U7[0:16, 1024 + m * 1024: 2048 + m * 1024]
        AC2 = lambda m: U7[0:16, 512 + m: 513 + m]
        wsrcs = [wsrc1_d, wsrc2_d]
        MEMS(V, qsc[0:1, 0:8], 0.0, writes=[r_qsc])
        MEMS(V, qth[0:1, 6:7], 1.0, writes=[r_qsc])
        MEMS(V, qth[0:1, 7:8], float(2.0 ** -30), writes=[r_qsc])
        TTn(V, qth[0:1, 8:9], qth[0:1, 6:7], qth[0:1, 7:8], AL.add,
            reads=[r_qsc], writes=[r_qsc])
        TTn(V, qth[0:1, 8:9], qth[0:1, 8:9], qth[0:1, 6:7], AL.subtract,
            reads=[r_qsc], writes=[r_qsc])
        TSc(V, qth[0:1, 9:10], qth[0:1, 8:9], float(-(2.0 ** 30)), 1.0,
            op0=AL.mult, op1=AL.add, reads=[r_qsc], writes=[r_qsc])
        for m in range(2):
            MEMS(V, ACC1(m), 0.0, writes=[r_acc1[m]])
        for b in range(128):
            for m in range(2):
                P.dma("ld" if m == 0 else "sc", STG1(m),
                      wsrcs[m][8 * b: 8 * (b + 1)].rearrange("s p f -> p s f"),
                      writes=[r_stg[m]])
                ACT(ABT1(m), STG1(m), AF.Abs, reads=[r_stg[m]], writes=[r_abt[m]])
                for j in range(8):
                    TTn(V, ACC1(m), ACC1(m), ABT1(m)[:, j * 128:(j + 1) * 128], AL.add,
                        reads=[r_abt[m]], writes=[r_acc1[m]])
        for m in range(2):
            P.dma("st", sq_d[m], ACC1(m), reads=[r_acc1[m]], writes=[r_sq[m]])
            P.dma("ld", ARR2(m), sq_d[m], reads=[r_sq[m]], writes=[r_arr2[m]])
            MEMS(V, AC2(m), 0.0, writes=[r_acc1[m]])
        for s in range(1024):
            for m in range(2):
                TTn(V, AC2(m), AC2(m), ARR2(m)[:, s:s + 1], AL.add,
                    reads=[r_arr2[m]], writes=[r_acc1[m]])
        for m in range(2):
            P.dma("st", tb_d[m], AC2(m), reads=[r_acc1[m]], writes=[r_tb[m]])
        P.dma("ld", U7[0:8, 520:522], tb_d[0], reads=[r_tb[0]], writes=[r_tb[0]])
        TTn(V, U7[0:8, 524:525], U7[0:8, 520:521], U7[0:8, 521:522], AL.add,
            reads=[r_tb[0]], writes=[r_qsc])
        P.dma("st", trow_d[0][0:1, 0:8], U7[0:8, 524:525], reads=[r_qsc], writes=[r_trow[0]])
        P.dma("ld", U7[0:1, 528:536], trow_d[0][0:1, 0:8], reads=[r_trow[0]], writes=[r_trow[0]])
        TTn(V, U7[0:1, 536:540], U7[0:1, 528:532], U7[0:1, 532:536], AL.add,
            reads=[r_trow[0]], writes=[r_qsc])
        TTn(V, U7[0:1, 540:542], U7[0:1, 536:538], U7[0:1, 538:540], AL.add,
            reads=[r_qsc], writes=[r_qsc])
        TTn(V, qsc[0:1, 0:1], U7[0:1, 540:541], U7[0:1, 541:542], AL.add,
            reads=[r_qsc], writes=[r_qsc])
        P.dma("ld", U7[0:2, 544:552], tb_d[1], reads=[r_tb[1]], writes=[r_tb[1]])
        MEMS(V, U7[0:2, 552:553], 0.0, writes=[r_qsc])
        for i in range(8):
            TTn(V, U7[0:2, 552:553], U7[0:2, 552:553], U7[0:2, 544 + i:545 + i], AL.add,
                reads=[r_tb[1]], writes=[r_qsc])
        P.dma("st", trow_d[1][0:1, 0:2], U7[0:2, 552:553], reads=[r_qsc], writes=[r_trow[1]])
        P.dma("ld", U7[0:1, 556:558], trow_d[1][0:1, 0:2], reads=[r_trow[1]], writes=[r_trow[1]])
        TTn(V, qsc[0:1, 4:5], U7[0:1, 556:557], U7[0:1, 557:558], AL.add,
            reads=[r_trow[1]], writes=[r_qsc])
        for m in range(2):
            TSc(V, qsc[0:1, 4 * m + 1:4 * m + 2], qsc[0:1, 4 * m:4 * m + 1],
                float(2.0 ** -24), 1e-8, op0=AL.mult, op1=AL.max,
                reads=[r_qsc], writes=[r_qsc])
            TSc(V, qth[0:1, 3 * m:3 * m + 1], qsc[0:1, 4 * m + 1:4 * m + 2], 0.5,
                reads=[r_qsc], writes=[r_qsc])
            TSc(V, qsc[0:1, 4 * m + 3:4 * m + 4], qsc[0:1, 4 * m + 1:4 * m + 2],
                float(2.0 ** -25), reads=[r_qsc], writes=[r_qsc])
            if m == 1:
                TSc(V, qth[0:1, 10:11], qsc[0:1, 4 * m + 1:4 * m + 2], 6e-7,
                    reads=[r_qsc], writes=[r_qsc])
                TTn(V, qth[0:1, 10:11], qth[0:1, 10:11], qth[0:1, 9:10], AL.mult,
                    reads=[r_qsc], writes=[r_qsc])
                TTn(V, qth[0:1, 4:5], qsc[0:1, 4 * m + 3:4 * m + 4], qth[0:1, 10:11],
                    AL.subtract, reads=[r_qsc], writes=[r_qsc])
                TTn(V, qth[0:1, 5:6], qsc[0:1, 4 * m + 3:4 * m + 4], qth[0:1, 10:11],
                    AL.add, reads=[r_qsc], writes=[r_qsc])
            else:
                CP(V, qth[0:1, 1:2], qsc[0:1, 4 * m + 3:4 * m + 4],
                   reads=[r_qsc], writes=[r_qsc])
                CP(V, qth[0:1, 2:3], qsc[0:1, 4 * m + 3:4 * m + 4],
                   reads=[r_qsc], writes=[r_qsc])
        MM(pb[0][:, 0:6], c_onesf[0:1, :], qth[0:1, 0:6], True, True,
           reads=[r_qsc, r_const], writes=[r_pb[0]])
        CP(V, q_tc[:, 0:6], pb[0][:, 0:6], reads=[r_pb[0]], writes=[r_qtc])
        if debug:
            P.dma("st", dbg_d["d_qs"], qsc[0:1, 0:8], reads=[r_qsc])

        P.set_fence()
        r_aw = [R(), R()]
        r_prd = [R(), R()]
        r_sgn = [R(), R()]
        r_q2 = [R(), R()]

        def w1_store(c, base):
            dc, f8 = c // 8, c % 8
            for j in range(8):
                P.dma("st", w1q_d[f8 * 8 + j, :, dc * 128:(dc + 1) * 128],
                      U3[:, base + j * 128: base + (j + 1) * 128],
                      reads=[r_q2[(c % 2)]], writes=[r_dw1])

        def w2_store(c, base):
            P.dma("st", w2q_d[c // 2, :, (c % 2) * 1024:(c % 2 + 1) * 1024],
                  U3[:, base:base + 1024], reads=[r_q2[(c % 2)]], writes=[r_dw2])

        def quant(nchunks, src_fn, store_fn, m, dbg_tap):
            for c in range(nchunks):
                k = c % 2
                wslot = U6[:, k * 1024:(k + 1) * 1024]
                awslot = U6[:, 2048 + k * 1024: 2048 + (k + 1) * 1024]
                prdslot = U3[:, 4096 + k * 1024: 4096 + (k + 1) * 1024]
                sgnslot = U3[:, 6144 + k * 1024: 6144 + (k + 1) * 1024]
                qmslot = U3[:, 8192 + k * 1024: 8192 + (k + 1) * 1024]
                base = 2048 + k * 1024
                P.dma("ld", wslot, src_fn(c), writes=[r_stg[k]])
                ACT(awslot, wslot, AF.Abs, reads=[r_stg[k]], writes=[r_aw[k]])
                TSc(V, awslot, awslot, q_tc[:, 3 * m:3 * m + 1], None,
                    op0=AL.subtract, reads=[r_aw[k], r_qtc], writes=[r_aw[k]])
                TSc(V, prdslot, awslot, q_tc[:, 3 * m + 1:3 * m + 2], None,
                    op0=AL.is_gt, reads=[r_aw[k], r_qtc], writes=[r_prd[k]])
                P.op(V, lambda E_, aws=awslot, prs=prdslot, qms=qmslot, mm_=m:
                     E_.scalar_tensor_tensor(out=qms, in0=aws,
                                             scalar=q_tc[:, 3 * mm_ + 2:3 * mm_ + 3],
                                             in1=prs, op0=AL.is_gt, op1=AL.add),
                     reads=[r_aw[k], r_prd[k], r_qtc], writes=[r_prd[k]])
                ACT(sgnslot, wslot, AF.Sign, reads=[r_stg[k]], writes=[r_sgn[k]])
                TTn(V, U3[:, base:base + 1024], qmslot, sgnslot, AL.mult,
                    reads=[r_prd[k], r_sgn[k]], writes=[r_q2[k]])
                store_fn(c, base)
                if debug and dbg_tap and c == 0:
                    CP(V, U6[:, 2048:3072], U3[:, base:base + 1024], reads=[r_q2[k]],
                       writes=[r_aw[0]])
                    P.dma("st", dbg_d["d_w1q"], U6[:, 2048:3072], reads=[r_aw[0]])

        quant(128, lambda c: w1T_d[(c // 8) * 128:(c // 8 + 1) * 128,
                                   (c % 8) * 1024:(c % 8 + 1) * 1024], w1_store, 0, True)
        quant(128, lambda c: w2T_d[(c // 2) * 128:(c // 2 + 1) * 128,
                                   (c % 2) * 1024:(c % 2 + 1) * 1024], w2_store, 1, False)

        P.set_fence()
        if debug:
            P.dma("st", U6[:, 0:512], mix_d[0:128, 0:1, 0:512], reads=[r_dmix],
                  writes=[rU6, r_xre[0]])
            P.dma("st", dbg_d["d_m2"], U6[:, 0:512], reads=[rU6, r_xre[0]])
        for dco in range(NDC):
            P.dma("ld", U7[:, 0:2048], woT_d[dco], writes=[rU7])
            ACT(U1[:, dco * 2048:(dco + 1) * 2048], U7[:, 0:2048],
                AF.Copy, scale=1.0 / 64.0, reads=[rU7], writes=[rU1])
        for c in range(2):
            P.dma("ld", U7[0:1, 2048:3072], bo_d[0:1, c * 1024:(c + 1) * 1024],
                  writes=[rU7])
            ACT(U4[0:1, c * 1024:(c + 1) * 1024], U7[0:1, 2048:3072], AF.Copy,
                reads=[rU7], writes=[rU4])

        def ln_rows(sscale):
            TSc(V, row_a[0:1, :], pb[2][0:1, :], 1.0 / 2048.0, reads=[r_pb[2]], writes=[r_rows])
            TSc(V, row_b[0:1, :], pb[3][0:1, :], 1.0 / 2048.0, reads=[r_pb[3]], writes=[r_rows])
            ACT(row_c[0:1, :], row_a[0:1, :], AF.Square, reads=[r_rows], writes=[r_rows])
            TTn(V, row_b[0:1, :], row_b[0:1, :], row_c[0:1, :], AL.subtract,
                reads=[r_rows], writes=[r_rows])
            TSc(V, row_b[0:1, :], row_b[0:1, :], EPS_LN, None, op0=AL.add,
                reads=[r_rows], writes=[r_rows])
            RECIP(row_b[0:1, :], row_b[0:1, :], reads=[r_rows], writes=[r_rows])
            ACT(row_b[0:1, :], row_b[0:1, :], AF.Sqrt, reads=[r_rows], writes=[r_rows])
            CP(V, mo_bf[0:1, :], row_a[0:1, :], reads=[r_rows], writes=[r_mrow])
            TSc(V, so_bf[0:1, :], row_b[0:1, :], sscale, reads=[r_rows], writes=[r_mrow])

        def bcast_ms():
            MM(pb[4][:, :], c_onesb[0:1, 0:128], mo_bf[0:1, :], True, True,
               reads=[r_mrow, r_const], writes=[r_pb[4]])
            MM(pb[5][:, :], c_onesb[0:1, 0:128], so_bf[0:1, :], True, True,
               reads=[r_mrow, r_const], writes=[r_pb[5]])
            CP(V, BCM, pb[4][:, :], reads=[r_pb[4]], writes=[r_bcm])
            CP(V, BCS, pb[5][:, :], reads=[r_pb[5]], writes=[r_bcs])

        for t5 in range(NT5):
            sl5 = slice(t5 * 512, (t5 + 1) * 512)
            km = t5 % 2
            for dci in range(NDC):
                P.dma("ld", U2[:, km * 8192 + dci * 512: km * 8192 + (dci + 1) * 512],
                      mix_d[0:128, dci:dci + 1, sl5], reads=[r_dmix], writes=[rU2])
            for dco in range(NDC):
                bank = dco % 2
                for dci in range(NDC):
                    MM(pb[bank][:, :],
                       U1[:, dco * 2048 + dci * 128: dco * 2048 + (dci + 1) * 128],
                       U2[:, km * 8192 + dci * 512: km * 8192 + (dci + 1) * 512],
                       dci == 0, False, reads=[rU1, rU2], writes=[r_pb[bank]])
                MM(pb[bank][:, :], U4[0:1, dco * 128:(dco + 1) * 128], c_onesb[0:1, 0:512],
                   False, True, reads=[rU4, r_const], writes=[r_pb[bank]])
                ko2 = dco % 2
                CP(V, OT(ko2), pb[bank][:, :], reads=[r_pb[bank]], writes=[r_ot[ko2]])
                P.dma("st", spl_d[dco * 128:(dco + 1) * 128, :], OT(ko2),
                      reads=[r_ot[ko2]], writes=[r_dspl])
                CP(V, U4[:, 10240:10752], OT(ko2), reads=[r_ot[ko2]], writes=[rU4])
                ACT(U4[:, 9216:9728], OT(ko2), AF.Square, reads=[r_ot[ko2]], writes=[rU4])
                MM(pb[2][0:1, :], c_onesb[:, 0:1], U4[:, 10240:10752], dco == 0, dco == NDC - 1,
                   reads=[rU4, r_const], writes=[r_pb[2]])
                MM(pb[3][0:1, :], c_onesb[:, 0:1], U4[:, 9216:9728], dco == 0, dco == NDC - 1,
                   reads=[rU4, r_const], writes=[r_pb[3]])
            ln_rows(1.0)
            bcast_ms()
            for dc in range(NDC):
                kx = dc % 2
                P.dma("ld", XRE(kx), xrT_d[dc * 128:(dc + 1) * 128, sl5], writes=[r_xre[kx]])
                P.dma("ld", OT(kx), spl_d[dc * 128:(dc + 1) * 128, :],
                      reads=[r_dspl], writes=[r_ot[kx]])
                TTn(G, WRK, OT(kx), BCM, AL.subtract,
                    reads=[r_ot[kx], r_bcm], writes=[r_wrk])
                TTn(G, WRK, WRK, BCS, AL.mult, reads=[r_bcs], writes=[r_wrk])
                TTn(G, X1T, WRK, XRE(kx), AL.add,
                    reads=[r_wrk, r_xre[kx]], writes=[r_x1t])
                P.dma("st", x1_d[dc * 128:(dc + 1) * 128, sl5], X1T,
                      reads=[r_x1t], writes=[r_dx1])
                CP(V, U4[:, 10240:10752], X1T, reads=[r_x1t], writes=[rU4])
                ACT(U4[:, 9216:9728], X1T, AF.Square, reads=[r_x1t], writes=[rU4])
                MM(pb[2][0:1, :], c_onesb[:, 0:1], U4[:, 10240:10752], dc == 0, dc == NDC - 1,
                   reads=[rU4, r_const], writes=[r_pb[2]])
                MM(pb[3][0:1, :], c_onesb[:, 0:1], U4[:, 9216:9728], dc == 0, dc == NDC - 1,
                   reads=[rU4, r_const], writes=[r_pb[3]])
                if debug and t5 == 0 and dc == 0:
                    P.dma("st", dbg_d["d_x1"], X1T, reads=[r_x1t])
            ln_rows(0.5)
            bcast_ms()
            for dc in range(NDC):
                kx = dc % 2
                P.dma("ld", OT(kx), x1_d[dc * 128:(dc + 1) * 128, sl5],
                      reads=[r_dx1], writes=[r_ot[kx]])
                TTn(G, WRK, OT(kx), BCM, AL.subtract,
                    reads=[r_ot[kx], r_bcm], writes=[r_wrk])
                TTn(G, U4[:, 10240:10752], WRK, BCS, AL.mult,
                    reads=[r_wrk, r_bcs], writes=[rU4])
                P.dma("st", ln2_d[dc * 128:(dc + 1) * 128, sl5], U4[:, 10240:10752],
                      reads=[rU4], writes=[r_dln2])
                if debug and t5 == 0 and dc == 0:
                    CP(V, U6[:, 0:512], U4[:, 10240:10752], reads=[rU4],
                       writes=[rU6, r_xre[0]])
                    P.dma("st", dbg_d["d_ln2"], U6[:, 0:512], reads=[rU6, r_xre[0]])
            if debug and t5 == 0:
                P.dma("ld", U6[:, 0:512], spl_d[0:128, :], reads=[r_dspl],
                      writes=[rU6, r_xre[0]])
                P.dma("st", dbg_d["d_o"], U6[:, 0:512], reads=[rU6, r_xre[0]])

        P.set_fence()
        for c in range(8):
            P.dma("ld", U7[0:1, 2048:3072], b1_d[0:1, c * 1024:(c + 1) * 1024], writes=[rU7])
            ACT(U2[0:1, 8192 + c * 1024: 8192 + (c + 1) * 1024], U7[0:1, 2048:3072],
                AF.Copy, reads=[rU7], writes=[rU2])
        for c in range(2):
            P.dma("ld", U7[0:1, 2048:3072], b2_d[0:1, c * 1024:(c + 1) * 1024], writes=[rU7])
            ACT(U3[0:1, 6144 + c * 1024: 6144 + (c + 1) * 1024], U7[0:1, 2048:3072],
                AF.Copy, scale=2.0, reads=[rU7], writes=[rU3])

        for tq in range(4):
            sl5 = slice(tq * 512, (tq + 1) * 512)
            for dc in range(NDC):
                P.dma("ld", U2[:, dc * 512:(dc + 1) * 512],
                      ln2_d[dc * 128:(dc + 1) * 128, sl5], reads=[r_dln2], writes=[rU2])
            for fc in range(NFC):
                k = fc % 2
                P.dma("ld", U3[:, k * 2048:(k + 1) * 2048], w1q_d[fc], reads=[r_dw1], writes=[rU3])
                bank = fc % 2
                for dc in range(NDC):
                    MM(pb[bank][:, :], U3[:, k * 2048 + dc * 128: k * 2048 + (dc + 1) * 128],
                       U2[:, dc * 512:(dc + 1) * 512], dc == 0, False,
                       reads=[rU3, rU2], writes=[r_pb[bank]])
                MM(pb[bank][:, :], U2[0:1, 8192 + fc * 128: 8192 + (fc + 1) * 128],
                   c_onesb[0:1, 0:512], False, True, reads=[rU2, r_const], writes=[r_pb[bank]])
                ACT(U1[:, fc * 512:(fc + 1) * 512], pb[bank][:, :], AF.Gelu,
                    reads=[r_pb[bank]], writes=[rU1])
            if debug and tq == 0:
                CP(V, U6[:, 0:512], U1[:, 0:512], reads=[rU1], writes=[rU6, r_xre[0]])
                P.dma("st", dbg_d["d_h"], U6[:, 0:512], reads=[rU6, r_xre[0]])
            for dg in range(2):
                for fc in range(NFC):
                    k = fc % 2
                    P.dma("ld", U3[:, 4096 + k * 1024: 4096 + (k + 1) * 1024],
                          w2q_d[fc, :, dg * 1024:(dg + 1) * 1024], reads=[r_dw2], writes=[rU3])
                    for j in range(8):
                        MM(pb[j][:, :],
                           U3[:, 4096 + k * 1024 + j * 128: 4096 + k * 1024 + (j + 1) * 128],
                           U1[:, fc * 512:(fc + 1) * 512], fc == 0, False,
                           reads=[rU3, rU1], writes=[r_pb[j]])
                for j in range(8):
                    dco = dg * 8 + j
                    MM(pb[j][:, :], U3[0:1, 6144 + dco * 128: 6144 + (dco + 1) * 128],
                       c_onesb[0:1, 0:512], False, True, reads=[rU3, r_const], writes=[r_pb[j]])
                    ko2 = dco % 2
                    CP(V, OT(ko2), pb[j][:, :], reads=[r_pb[j]], writes=[r_ot[ko2]])
                    P.dma("st", spl_d[dco * 128:(dco + 1) * 128, :], OT(ko2),
                          reads=[r_ot[ko2]], writes=[r_dspl])
                    CP(V, U4[:, dco * 512:(dco + 1) * 512], OT(ko2),
                       reads=[r_ot[ko2]], writes=[rU4])
            for dco in range(NDC):
                ACT(U4[:, 8192:8704], U4[:, dco * 512:(dco + 1) * 512], AF.Square,
                    reads=[rU4], writes=[rU4])
                MM(pb[2][0:1, :], c_onesb[:, 0:1], U4[:, dco * 512:(dco + 1) * 512],
                   dco == 0, dco == NDC - 1, reads=[rU4, r_const], writes=[r_pb[2]])
                MM(pb[3][0:1, :], c_onesb[:, 0:1], U4[:, 8192:8704],
                   dco == 0, dco == NDC - 1, reads=[rU4, r_const], writes=[r_pb[3]])
            ln_rows(1.0)
            bcast_ms()
            for dco in range(NDC):
                kx = dco % 2
                P.dma("ld", XRE(kx), x1_d[dco * 128:(dco + 1) * 128, sl5],
                      reads=[r_dx1], writes=[r_xre[kx]])
                P.dma("ld", OT(kx), spl_d[dco * 128:(dco + 1) * 128, :],
                      reads=[r_dspl], writes=[r_ot[kx]])
                TTn(G, WRK, OT(kx), BCM, AL.subtract,
                    reads=[r_ot[kx], r_bcm], writes=[r_wrk])
                TTn(G, WRK, WRK, BCS, AL.mult, reads=[r_bcs], writes=[r_wrk])
                TTn(G, XRE(kx), WRK, XRE(kx), AL.add,
                    reads=[r_wrk, r_xre[kx]], writes=[r_xre[kx]])
                P.dma("st", out_d[dco * 128:(dco + 1) * 128, sl5], XRE(kx),
                      reads=[r_xre[kx]], writes=[r_dspl])
                if debug and tq == 0 and dco == 0:
                    P.dma("st", dbg_d["d_y"], OT(kx), reads=[r_ot[kx]])

        with nc.Block() as block:
            @block.sync
            def _(sync):
                P.replay("sync", sync)
                for q in ("st", "sc", "ld"):
                    for s_ in QSEMS[q]:
                        if P.cnt[s_] > 0:
                            sync.wait_ge(sems[s_], P.cnt[s_])

            @block.tensor
            def _(tensor):
                P.replay("tensor", tensor)

            @block.vector
            def _(vector):
                P.replay("vector", vector)

            @block.scalar
            def _(scalar):
                P.replay("scalar", scalar)

            @block.gpsimd
            def _(gpsimd):
                P.replay("gpsimd", gpsimd)

    return nc


def host_inputs(inputs, core):
    x = np.asarray(inputs["L0_attn_hidden_states"], np.float32)
    beta, hg = core // 4, core % 4
    xb = x[beta]
    xbT = np.ascontiguousarray(xb.T)
    xr = xb[hg * T:(hg + 1) * T]
    xrT = np.ascontiguousarray(xr.T)
    Wq = np.asarray(inputs["L0_attn_Wq"], np.float32)
    wq4 = Wq[hg * 512:(hg + 1) * 512]
    wqT = np.ascontiguousarray(
        wq4.reshape(4, 128, NDC, 128).transpose(0, 3, 2, 1).reshape(4, 128, NDC * 128))
    bq4 = np.asarray(inputs["L0_attn_bq"], np.float32)[hg * 512:(hg + 1) * 512].reshape(1, 512)
    return xbT, xrT, wqT, bq4


_SHARED = {}


def shared_inputs(inputs):
    if _SHARED.get("done"):
        return _SHARED
    f32 = np.float32
    bf = ml_dtypes.bfloat16
    Wk = np.asarray(inputs["L0_attn_Wk"], f32)
    Wv = np.asarray(inputs["L0_attn_Wv"], f32)
    Wo = np.asarray(inputs["L0_attn_Wo"], f32)
    R = np.asarray(inputs["L0_attn_rand_matrix"], f32)
    w1 = np.asarray(inputs["L0_ffn_w1"], f32)
    w2 = np.asarray(inputs["L0_ffn_w2"], f32)

    sh = {}
    sh["wkT"] = np.ascontiguousarray(
        Wk.reshape(HK, 128, NDC, 128).transpose(0, 3, 2, 1).reshape(HK, 128, NDC * 128))
    sh["wvT"] = np.ascontiguousarray(Wv.T.reshape(NDC, 128, 1024))
    sh["woT"] = np.ascontiguousarray(
        Wo.reshape(NDC, 128, NDC, 128).transpose(0, 3, 2, 1).reshape(NDC, 128, NDC * 128))
    RT = np.ascontiguousarray(R.T)
    sh["rT"] = RT
    sh["r2T"] = np.ascontiguousarray(np.concatenate([RT[64:], RT[:64]], axis=0))
    sh["w1T"] = np.ascontiguousarray(w1.T)
    sh["w2T"] = np.ascontiguousarray(w2.T)

    def make_wsrc(w):
        R_, C_ = w.shape
        WR, WC = R_ // 32, C_ // 32
        QC = WC // 32
        arr = w.reshape(WR, 32, WC, 32).transpose(1, 3, 0, 2).reshape(32, 32, WR * WC)
        wi = np.arange(WR)[:, None]
        wj = np.arange(WC)[None, :]
        Lp = (((wi // 32) * QC + wj // 32) * 32 + wi % 32) * 32 + wj % 32
        perm = np.empty(WR * WC, np.int64)
        perm[Lp.reshape(-1)] = np.arange(WR * WC)
        return np.ascontiguousarray(arr[:, :, perm].reshape(1024, 128, 128))

    sh["wsrc1"] = make_wsrc(w1)
    sh["wsrc2"] = make_wsrc(w2)
    sh["bk"] = np.asarray(inputs["L0_attn_bk"], f32).reshape(1, HK * HD)
    sh["bv"] = np.asarray(inputs["L0_attn_bv"], f32).reshape(1, HK * HD)
    sh["bo"] = np.asarray(inputs["L0_attn_bo"], f32).reshape(1, D)
    sh["b1"] = np.asarray(inputs["L0_ffn_b1"], f32).reshape(1, DFF)
    sh["b2"] = np.asarray(inputs["L0_ffn_b2"], f32).reshape(1, D)
    sh["pos"] = np.arange(SEQ, dtype=f32).reshape(1, SEQ)
    sh["ones_f"] = np.ones((1, 128), f32)
    sh["idnf"] = np.eye(128, dtype=f32)
    sh["iidx"] = np.arange(E, dtype=f32).reshape(1, E)
    nsel4 = np.zeros((128, 16), bf)
    for t5 in range(4):
        nsel4[:, t5 * 4 + t5] = 1
    sh["nsel4"] = nsel4
    bsel4 = np.zeros((4, 512), bf)
    for t5 in range(4):
        bsel4[t5, t5 * 128:(t5 + 1) * 128] = 1
    sh["bsel4"] = bsel4
    _SHARED.update(sh)
    _SHARED["done"] = True
    return _SHARED


def kernel(**inputs):
    nc = build_program(debug=0)
    sh = shared_inputs(inputs)
    in_maps = []
    for core in range(NCORES):
        xbT, xrT, wqT, bq4 = host_inputs(inputs, core)
        m = {k: v for k, v in sh.items() if k != "done"}
        m["xbT"] = xbT
        m["xrT"] = xrT
        m["wqT"] = wqT
        m["bq"] = bq4
        in_maps.append(m)
    res = run_bass_kernel_spmd(nc, in_maps, core_ids=list(range(NCORES)))
    outs = [res.results[c]["outT"] for c in range(NCORES)]
    full = np.concatenate([o.T for o in outs], axis=0)
    return np.ascontiguousarray(full.reshape(2, SEQ, D).astype(np.float32, copy=False))


# version 06
# speedup vs baseline: 1.0127x; 1.0127x over previous
import numpy as np
import ml_dtypes
from contextlib import ExitStack

import concourse.bass as bass
import concourse.mybir as mybir
from concourse.bass_utils import run_bass_kernel_spmd

F32 = mybir.dt.float32
BF16 = mybir.dt.bfloat16
AL = mybir.AluOpType
AF = mybir.ActivationFunctionType

NCORES = 8
T = 2048
D = 2048
HQ, HK, HD, E = 16, 8, 128, 64
DFF = 8192
NT5 = 4
NTT = 16
NDC = 16
NFC = 64
SEQ = 8192
EPS_LN = 1e-5

ESEM = {"tensor": "sT", "vector": "sV", "scalar": "sS", "gpsimd": "sG", "sync": "sY"}
NQ = {"ld": 8, "st": 4, "sc": 2}
QENG = {"ld": "sync", "st": "gpsimd", "sc": "scalar"}
QSEMS = {q: [f"s{q}{i}" for i in range(n)] for q, n in NQ.items()}
ALL_SEMS = list(ESEM.values()) + [s for ss in QSEMS.values() for s in ss]


class Res:
    __slots__ = ("w", "r")

    def __init__(self):
        self.w = None
        self.r = []


class Prog:
    def __init__(self, sems):
        self.sems = sems
        self.cnt = {k: 0 for k in sems}
        self.items = []
        self.fence = []
        self.qidx = {q: 0 for q in NQ}

    def set_fence(self):
        self.fence = [(s, v) for s, v in self.cnt.items() if v > 0]

    def _waits(self, reads, writes, extra):
        waits = {}

        def add(tok):
            if tok is None:
                return
            s, v = tok
            if waits.get(s, -1) < v:
                waits[s] = v

        for r in reads:
            add(r.w)
        for w in writes:
            add(w.w)
            for t in w.r:
                add(t)
        for t in extra:
            add(t)
        for t in self.fence:
            add(t)
        return waits

    def op(self, eng, fn, reads=(), writes=(), extra=()):
        sem = ESEM[eng]
        waits = self._waits(reads, writes, extra)
        self.cnt[sem] += 1
        tok = (sem, self.cnt[sem])
        self.items.append((eng, fn, waits, sem, 1))
        for r in reads:
            r.r.append(tok)
        for w in writes:
            w.w = tok
            w.r = []
        return tok

    def dma(self, queue, dst, src, reads=(), writes=(), extra=()):
        eng = QENG[queue]
        idx = self.qidx[queue]
        self.qidx[queue] = (idx + 1) % NQ[queue]
        sem = QSEMS[queue][idx]
        waits = self._waits(reads, writes, extra)
        if self.cnt[sem] > 0 and waits.get(sem, -1) < self.cnt[sem]:
            waits[sem] = self.cnt[sem]
        self.cnt[sem] += 16
        tok = (sem, self.cnt[sem])
        self.items.append(
            (eng, (lambda E, d=dst, s_=src: E.dma_start(out=d, in_=s_)), waits, sem, 16))
        for r in reads:
            r.r.append(tok)
        for w in writes:
            w.w = tok
            w.r = []
        return tok

    def replay(self, engname, eng):
        waited = {}
        for e, fn, waits, sem, inc in self.items:
            if e != engname:
                continue
            for ws in sorted(waits):
                wv = waits[ws]
                if waited.get(ws, -1) < wv:
                    eng.wait_ge(self.sems[ws], wv)
                    waited[ws] = wv
            instr = fn(eng)
            if instr is not None:
                instr.then_inc(self.sems[sem], inc)


_TAB_B64 = "UEsDBC0AAAAIAAAAIQCnAR6J//////////8IABQAaW52Zi5ucHkBABAAgAEAAAAAAABNAQAAAAAAAJvsF+obEMnIUMZQrZ6SWpxcpG6loG5TaqKuo6Cell9UUpSYF59flJIKEndLzClOBYoXZyQWpAL5GmYmOpo6CrUK5AMuBoYG++vrY+0Ff9rbP7BVs5/1ld/+fPBPu9kvrttJ/9gFpBfasZj22ClfKbJ76x5k1y5hZuewQdauvJfD7lr9G9uzZ87YVkRutOWdMtPW91+L7aHQNFv7Mndb0V06th6TRW0ftzHaXgp5YOPvfMim23KFjeXaiTbGvjU2n6qjbASNbW24rivb6D7ktdmw86u16Kkr1nNcdlifmTjfulyvy5ovKN/aws7fmr/c2Jo3U9o6fwqbtYboS6uY6yet+k6us+KLmW6VL9Rs9Ygx2apniouVi7WWlViWsNUa//+WquJ3La+X7LdMOL3UsmhSv6XoigrLJ8HhlsHlVpZilYqWRVe4LR/pfLIAAFBLAwQtAAAACAAAACEAd6p5kf//////////BgAUAHBjLm5weQEAEABApgEAAAAAABFpAAAAAAAAnNzlF1dVGihgurs7pVMaCWlpUFqQEmkFBGt0ZlBAuqVDQjqlkZAWkJISFCUUSUUUdZwZZ+698z7+A5cvzzrnd/beb+0vsBbTW7Zr0bpTwgSvJnizWK8Xhj7/crEnChSr2a9asdIFivUe9PKwl3u81G3Qy71e+H/vG/UYOPSF//t+aN8eg1/4v8/FH69YqXqV0iVKF/hrgf/PP6lSJUrwvz/jmCdFOJ+F0oRLmD5dOJmp0lvPwhnCpUyRORzDklnC1cyWNSzBmVzFQrmczyJ5wmVMnj8czSSPhfk5ku/z8aLhh3y8hGemKCVeVioTbmHZcuF6pi8fTmb1iuFOlq3se+auEs5j9qrhLCavKR9WrRNuZ4V64SZWr+88ZmsUzmSjp8IjzN3E+czSLHyPNVuHu5nh6TAHa3IKZ3M3K7QLa3MT9zJdp3ASq3QJtzFn13AOE3QPKnE4t7DWC+EeFuwXLma6/s5j+gHhZDYcGB5mk5fCYyw0KFzChkN8z5wvi5fVh4Y7+eQr4ces93p4gPX+4pnJ3gzr8l3uZ7W/hzuYYERQmMO5lCVGh6uYYUw4hVkmhO+x7MRwPVNODsey/JRwI9NMDSewwYzwEEvODJtwNY8xyyzns+LscDNzzwvzsTbncSH3MvuCMC8f4ywu4AdMvjCsxdHcwwqLwvrcxIPMtiRsxJk88uf7dZ6Zf334PgtuDBez3KawITfwMItsDpcx67ZwBvNtDxey8Z7wKDPtDacxwf5gOGsfDPeyxZHwJFt84plljoZNuY7HWfZYuJ6JjoeF2ZTvcCmPM+sp+bHEmXAVM50Nq3Aat7HJufAYy10IN/Cpi+LgJzzOKp/bj1kvi4d5roYlOZ+rmf1a2IqzeJppboQT2OQb8TLpt2EpjuIaVr0dbmfqe2EGjucUZv4+nM4cP4WzWfqXcC2r/TPcwSZ/iI/l/xNuZPb/ho04i0dYL0nC/9mSB3iK6ZKGk1gvme95gKeYL3lYhAu5jMVThCtZIU24iUXS+p7F0ocrWCNDuIvJM4bpWZCjOZmLmSRzmJOPcSTn8ANWzxLuZMVs4WYWyS5eNswVHmaZ3GEjruMRZs1nHWfwMFMXCMez7GPhejYrGn7KsiX0h+t56s/3JT2zR6nwOrOXDmexcplwK4uX1T+mL6/OLP542IAreYiNK4dHWayK/rLNE+E5tqgdnmTjOtYzXQP7sDQncQXXsnXD8AwzNwmns3UzvzNF8zAbx3Ama7YIdzP1M2EmtuZ4TuMZNmijPkzc1nqO4DSW72ieuJFHWLVz2I7beeHP9109s1X38DQb9bIfC/QOH+cifsiEfcwBn+HbnMyzfLJv+DFL9BMHV/E0kw5QVz7DURzPsyw+yNwx8VB1ZKth9merVz2z1mvhHjZ9PTzOVn/xPZO9Gb7Lp98KP2Or4eLmaZ5lkpHhSLYa7TtmHRPOYLax5pwzuZZJJtiP1SaFO5h4inqwzgzxcR/Pstgs94hV55grPj1Xvsz2gbiYanmYm+M4j6VXip/NVtmXn/IzFlwdLmamde4Hn97oe+b9MFzAIlvDZWy2zTmsvD3cyrY7w/PM+5H3XMDzTLjb/LPtHr+zzj51ZtsDfmftg+Fe5jxkHedwH6t9EjbnDp5gpaPhFuY+pv5MfjwczfSf2o+TeYJZToTvMeG5MC3f5kQmuhC+w2Kfh9W5gjtZ+lL4BNfyIz55OfyY1a9Yz2RfuY8sdy3cwDY3wnOsf9N7HuQ51r/tPZPdtT8T3zNXrM8RXMaDLHA/XMRUP7gXTPzAerb5UTxs84tn1vg13MUav4XtuIsXmP/f4fvs9J/wMtP9N8zDgqzKSZzPxdzONAniL64mMFWicByTJwlHM0fSsB5n8wCLJg+Xs0SKcBXbpwwvMmnacBQ7Zwm/ZPtsvmeDXOEhtssdXmCpPGENruEuds5rf2bMH1bgVG5i9gJhQc7iYtYoFDbkLh5mksLhSFYtGm5nzWJhfe7mQZYuHpbjWm5gzRLWsVBJ69ieS7iWF5mjdNiRs3mJqcqYB3Yup37MXSEsy47szHlcz0v8kqkqhm9wHH9n4crhUmau4jxO53q2rhaeYekn5M3WNf3OFLXCDHyMYziFH7BE7bAVW3MVT/MMC9QJFzFPg3A+qzU019zBQyzQ2Hp2bKqOzNBCnEze2lxzNKcy6dPuGxM/E3bgCH7Oeu3ExwM89Of79p5ZrGO4gvU6WcesXcIZbPCc9czbNVzAjt3kywbdfc/8PcL32ahXeIRJelvHkTzErH3Ew4J93W8+0y88y0T9wwp8h5tYeUC4lRUG+Z31B7vffO7l8CsmGRqm4UhOYO1Xwr2sOFz/uJmfM+s7YUPO4GGmHRlOZJlR4To+N0Z8TO8fOjJzMqczxXjrOIZf8anJ6srn+AnP8is+NiV8mh34AT/j58w0NZzGZ96zL5+eET7Lz/gFM8xynzhkbviQHeY5j40WmDd2Whhe5nOL5MNnlvqOZ3mZVZeH25l1jb4x4Vpzwrc5kk+vkxfTr9cndt0Qfs3aG80TG2xyP1hiS9iUq3icqbbKi+N4lk9sky8/4mU+8ZH3TLwrHMFu+8Kr7HBA/dn7SPgdi39iHVfyKtseEw/P8zLLHDf/THJKfVnpdLiFT37mfvFjHmbes2ENLuAuJjlnf6Y8H2bnWM5ih4vh8/yc37DO5/LkPp5njUtha+7iGWb/wnecxV2s87V9meB6WIQlOZzLuJq1v9Ef7uVVPvat+81nb7qfTPZd+C673bGebe/Jlzm/D7twDq+w0g/6yZQPzBHHciUr/RjWZSdu4X5eZoWHvuMmXma3n8XPJo/MNY/xO7b5NTzHtr/Jl2n/5T0n8jzr/iFOpkuQ+H9OYtpE4UQmThaOYIGU4SKmSBWOYeM04VFWTxu24U6eY+N0YVse5Xl2Sh9eZq0M4R52yRheYZbM4XvsktXvzJEtnM022cXD7nnCayyWN1zBBPnD4exSwP4sWihsxOU8woKPhYuZvWhYgtU4i6u4g11KOI/VSnrPHbzCqqXD7exRLrzOpo+Hx1moYriERSqHy5i5qr5wOi8z2RPhu6xSI9zGHrWcz8xP2odV6vie7euGF9mjfjiE1/mQKRqElTiUY7iFPzNvo3ABiz2l36zYItzMui31l/t5hIVbhc9wKc+y6zPy4Ne8zvxtw/eZr33YjAv5KfN18J51Oob7WKeTZ1Z9NmzCNtzOYzzHV7qGv7BZD+ezSE/zwfovhO14kBeYsnc4llX66Ddr9A13sUZ/8bIrd/EYv2alAfrMdoOdz7TDwols+or5Z87Xw86cwy/Z9U3nsfNbfmehv3rPJfySif8WjmC6UeEk5h4tX87jMeYaG85l5wn2Z7qJ9mPOyfJgu/fUg4lnh0k5gqP4+KLwQ7b/wH1kqmXuMZtyHKfzODMttw+n8SIfWxF+wCQrw+c5kt8w8Spxs/SacC2rrw138rkN4Vestyk8wBpbzRef3+k8Vvso3MHnd/mdDfeEh9lxb3iJyfeFBTiai5jsUPgunzgcfsQmx80DG3waHmL9E+4d059WV7blZH7A83z+jHyY8Kz3fJvfMOE571n1Qvgkt/Nj5rxoHpn+c/tzMr9hg8vyY64v7M+5/JjFvgxX8LFr5oAfcCfLfRNuYJXb4TYmuBMOZ4674Wymvh/mYUeO53xeYpXv7c8OD80lP+dXf77/yTPL/ByuY5lH8uM67mT3X8JrzPyre8l8v5tbLuRhlvh3uIrP/6E/fO6/4mOmBEn+ZzdO41XmShyW51xu5HNJw69YLHm4gmlThs9zIr9hrQzhHj6XyX7Mkjl8j2mz2od5s4dFuYDLWTKH+LmaG1k3V7if5fP4nR3yht35Oa+xZD77smP+8BLLFQw3MNtjYVnO5HomLSZPluEoTuQ6piseTmK50mEDduMGHuJVdi8jH+YpG85nufJhRXbkBm7mJRavHK5kgqrhcGarFr7AmbzJCtXDTSxQK1zE3HXCecxYNyzIqVzMRPXDd5imgbpyAtexUsNwC5M9ZU75LvewZdPwWZ7iF+zXPLzD9i18z4v8grVa2Z+Z2rqXrNc+PMBunfSTDbuEh9mwa9iMh/kpk3cTD0fzC7bvIU52f17/eI03mb1vOIuNBoSteISn2f4l+7PUoHANcw0O5zL3EP1nnqHmldWHhTuZ45VwNsu+6r6x22vqx26vhz15lTfY6Y3wMlu9KR/2fMv37PQ33zPVcOs4jqdZYkS4ik+MDD9iunf1i5P4BXOMCcuzG2dzI6+y01jx8anx8uAnvMG8k8MFzD1VPOzJefyCN5hnmrg4n1fZ4r3wJJ+daR+mnWX+OJE32WK29XxqjjyYf25Ym8/yfe7lF8wxzz6czU9Yfb55Y/b3zT+zfRBW40zuYOJl4QiWWmH++cIq+bHlavXjKV5l2jXqwoxrw6ksvi5cyZ4b9YdJPwzrcxQP8skt4cfsudV6ZtgWTuEb28PfmXGHOeBUzmeZPc7hOn7MHnvD66ywL9zE7AfUnz0PiY/FD8ufLY+oI3t+4ntWPBo252aeYO9j4XdMczysxwk8wKKn5MHlvM7Hz4Yfsuy5cD1rnxcP9/LEn+8veGaGi+4pp3Ae610RH9N9FU5iy6vqw4LXwsUs+41zuZ4nmO6m/djtO/PKZLfCkmzOPnyXq3mCt9jxTniJ6e45jznvh3OY4Hv1Z0/24XBe5w3eYpYfwveY50fzy2Q/iZsdfxUf0/8WTmaVf4bb2Otf4bfM+od9OIOXmOM/YU3O5m5WSpj0f25hqkThOPZMEt5gtaThDrZIFvbgSV5n0RThctZPaV8e5A2mSRP24QTeYtK0fuco3mKX9OEVPpUt/IT1szuXmXKE09glp/XMljucycZ5w6NMmi8swPocxUU8yMz5w86czi9ZpUC4jR0Lh5eYq2jYi3P5LWsUC3exRwn9YOJS4eMcwQ/Zs6x+sFSlcA3zVBYn53Mbk1aVPzPVUF8+UTesyx78iPt5nU3qhcfYub56Me9T4QK2ahyeZpcm+skmTe3HGs3Ui+VbiIsbeZ15W/uOf+cCbuR/2fkZ8TJ9m3AyU7RzHsfwOhO0D4ezSyf5MN+z4UK26iJ/PvGc+jJjN33lVK5h9e7i4U5eZ91e+sTSfXzPtbzOyn3DrXy+X/gNGw6QFw/zClsNlA+Tvmi+WGZwuI59Xg5vMd9Q9WGRYWElLuMWPvmquPgxDzPha+4BG/I1vs35PMzfWOl1eXELT7PUG/rBgm+Fi5npb+4T8/89fJ+t3w7PsNeI8FsWHxlWYGt25kpu4hl+ycqj9JPl3rUfN3ATM442Z0w+PuzK0fyaDSaEh5h8ou/Ya7J8mGpKOI4FpoeLWGhWuITZZ4ez2GCuueQhfsPW88I2PMNzfH6B71lnYbiPPd53H9h6kX1YZHFYkhW4jKu5iXWW2J8Zl6ov8yw3f0y+Up4czUPssUp8rLg63Mzya8Pe3MjvmGBjmIvDOZcFN8mbi3mGhbeGS9lvW3iHmbeH09lrh/6z9kfecy+/Zd094X722Ctf9vrY9yx2IFzBXgf9zi7H9J1X+A2THXcO+/NdXuddVv803MlWp8PTrHom3M6sn5lzzuAiPn9W3PyG37LmuXA3050PJzHbxXAmc3wePsvZ/OLP95ecxy/4LVtdlg8LfBG+wUX8nU2vhMeZ+6twHhveCA+z3y3zwmq3wx187m74FZvetz/L/xA+yY38mFkfhEVZhpVYizO4nOu4hXvY5sewK8/xaxb8yT1hkl/DIhzJZWz8m/14lF8zyT+sY5vfzQnP8Rv2+5d6svK/zSm3cicr/+E9S/8nXMvy/1VXNkmQLOLgMZ5jg0ThIaZIEmbnGM5iv6ThHeZMHs5hjRThLqZIFRZicY7hEq5k4tThCDZIIz7WSBd25S5+zcoZw63MlCmcxoqZw82smMUz0+QIJzB5znA0n8sVfsV0ucNJTJonHMVmecXLT/k1M+YLp7JC/vBp9uMmfsY7rFXA9+zNPfyM37FqIeu4nXuYvXBYjrO4gaWKhGvYrIT8+Cm/ZrqS6sN0pdWRk/gVC5QJG3ERj7B3WfmweDnzw5LlxcPV/JrlHpcPy1XwzDKVw3UsUTVcxb9WC/9gs+ryZpIa4UimqhmOY9Va6s58teXBhVzJEnWdzwT1w+F8rFH4ATM1Ne9M1Dx8h8VahCuYqLU4+A4XMsXT6s4x/I6F24ZV2JtLuY3fsVj78Bmu4Nk/33fwzBc6hjf53LPmhYM7hz/yuS5+Z9Hn7MvlPMv+PcO7fOZ5v7PkC+aIz/TxOxP1Uzc27R8eZ5oB4QRWHxzuZOoh4XhWfjncyt7D1I85XwnnMO0bYTV250Tu4DUWf9N8sdLwcAtzvBPOZr2R4QG2HxVeZPfR9measfJl1vFhfhbkDL7Pxew+0X5MOjUcxd7T1IPl3nN/OWBWeI/p5oST2HVu+DW7z3ce0y5QP6Za6P5ywPv2Z5el4RUmXxaOZq7l4Vz2XRneZqo19me2teFM1tkY7mPxLfrHOlv9zsLbwmZcyk/Zbnt4gQN2yIfNPwpPMMOu8GlO4WdMszvMzAmcziwfqxPf4xXW2h/uYe8D+sv0h8LJTH5EfhzNT1nmeLiOdT8N9/PJE+HHrHky3M26p3zPdGfMDwueNa8sfk4/2Py8PvMEb3PAJfXmE5fDj5j2S3GyLifyY+5nrq/MF8tcDdvwBa7jOd5k3Wv2Y5Lr4UhmvBFW5Avsy6nczJu8zQHfypc1boe72Oiee8gjvMbk9/WZA763H3P8GOZjEdbhbC7kMu5jpYdhVTZiO27hdh7hBZb6KVzDFx6pA9v9FnbjBV7lgH/Ih2n/qX/sy4k8x9tM9C/14zu8xor/Vg82Yjdu5j4e4VXm/k84jyn+G45hjgTJ/2cLzuZJFkoY1uIS7mH3ROE1tkgcduNJXmWLJN6zTdLwHLMnD2uzO2dxL6/9+Xsqz6yZOtzNDGnDAZzCe0yRTlwcw5NskCE8xIIZxc3FPMdcmcK5zJYtnMnS2cO1LJYzXMEBucTHzHnCypzOrUyXN8zESZzGjIXCIpzKZX++L+yZ3UuqJ9uUkh/Tlg4nslsZ/WW38p6Z7XH5s2BFdWPuSs7hPJ5jnirhfLatGp5nrhrqzYY1w8N8oVZ4k+VqhxvY7Unf8SpvMn2dcDJT1g3T8AWO5QTe/PO7er7jBN5k2/ryYeIGYQ6O4Gx2b6w/bNNEvZizaTiHZZuFfbmet5mgeTichVuFS9mttbow09PmixWeCTexahtxcjuvsU/b8BazdTUfzN7NfWefHr5n357iZ97nwwWs3Mv9YKkXwjVs0Nt9ZpK+YWOO5NE/3w8QB4/yFhsP9J6VXgq3MOcg/WCFIeEz3MSzTD00HM9Sw8TPBK+4NxzOeWz8unjY4A35suJfws3M+ab42Pyt8AQb/NW+PMSjLPq38E0u57+Y7+/hQmZ8O5zKvu/oJxOODBPxbb7DPmPUnwnGhQU4nItYbrz7zmwT7cOZvMXsk8wfM0+WP/tyOo/yNrNN8R1n8jYTTBUn001TZ07iIfadbj3rzwgPMuOsMCvzsQabcypncCF38QSLzA6XseG8sBUP8zRzzg97cg5vsOvC8GumWmRfjuNhJlgcvsDhvMmeS+3PPiv0j31We2bXNc5nn7V+Z9/16st6G8IDTLhRXfk2FzLzJvPAEtvD/lzFuyy+K1zJF3eH3zPDnnAKS+4NVzPpx+Eo9t0nftY5EO5jm0PhOb54xHlMdyxsyUk8xawn5MUZXMUKJ8NNbHEqPMkip8MmbMEXuYzHeJLfM+0Z6ziRJ/niWd+z5Tnxs+Xn5oCn+DWzXrI/Z/Ak81wJ57P21XAvK14LNzPN9XAC89ywnsm/CYtxNFew/rfuN2velCd383v2/M59YMtb8mTC2+Li25zAmvftyzTf24cTeIoZfjCfrPnAej72UB78gAdZ+adwK9P97F5yEm8x26OwKmdyOxv/Gh5lmt/EzXL/tD838BYT/CvMweGc/ef7f7uHbMfZXM0LzPaHOFn0v+Fy9kwQ/zH2DXZIGH7OHknC62yTLDzHPsnDF3mL37NKynAbs6YKC7EmZ3AJdzN1mjAHK7MHe3I8Z3Mrr/MGM6YLp7JA+rAOF3Efe2SwD3tmtB9bZVI3nubnrJzZ/uzJrdzHG0yXLZzESrnDLeyTX51Zu0DYhHt5jP0LhnfZ8zHnsV2R8AJLFA1XMV0x6ziJN5i2eFiXrTiR+3mahUuES5mmZDiBqco6lwU5jpO4mO3KiZ89y4uPJSqGFbiKm9iokn3Ynkd4gRdZoUpYlY24idt5hH2q6RezVw9nMd8T4UL+tVb4B8vWDtezzpPmhn3q2J956oYVOZ+b2b6eOvEib7BsfefxxQbuM19s6JmFm+gn+zc1b2zXTB2ZpXn4HhO0DIez3TO+Z6E24RK2bydu9mwvfibtEI5ikmfDkWzf2XoW76KOXMlb7Pmc/Vm0Z7ic7Z+3H5P00meO5Cy27+17lu8TbmTqvuF49ugnDl7nDb70YvgDKw0KtzDLUPVmpdfCftzCO+z5uv057M3wEZP+NczAUZzCcu+EG9hjhLngdd798/1Iz8w5WhycwxusOtY9Y5lx4Tq2H6++bDspPM9ck8O5bD/V96w0XV1YYma4ilVnmRdu5y3mmaMOnM8NbLQwPMISS+zPBkvDQ8z2QTiT/ZbpF1OvCNMyB8dzImez0srwVW7hr0ywRn4czlvMvC4sxelcw9Ibw7Vsuik8zp6b9ZMlt4Sr2X6rfrDpNutZ+KNwKZPvUQf252iO51023BceZsH94WImPhimYh/24wiO4y3eYfVDYXfu5DUmOKyuLHIkXMayR9WJ63mDOU7Yn7N5jRVOh5tY7LNwBUuf0x/2v6g+THjJ+ezDt3mDt9j9svOZ8auwI6fyEvt9rV7McdU+nM1rLHHd/WD9G+FBpvg2HMOUN8NWHMvTLPRduITNboefssgd/WC/u+Jlu3vhBXb/Xrys90N4gP1+tJ6NHoateYRnOOjn8AH7PbKeSX8Nn+QofsxWv8mXg/5hPxb+3f1hyn96z3ocy6U88Ofv//LM7H+oA2fxGiv9J9zCev+1nv0SpYz82CpJeJqDkoYPmChZ+A6zpw5nMVma8F12Sh9eZr4MYQcu5OfMkymczzqZw31MkiWsz5E8yIHZwvscmN0z0+UIJ7F+Tt/xIO+zTm7ns1Me+bBM3rAh1/EwC+UL+3MJ77JX/vBbDiqg3qxd0P7sxL08zMt8opB9+BEfMFXhMCfHcQ4HFpE/cxRVJ87mQeYppl9sWSI8xfolfc/KpX3HrTzFUmXDNWxZwe+sW9l77ucpdqhinvhk1fBjpqwWjmWF6mE/buIdJq8RpmYRjuZ4LmOemvZlPTZlP87nJh7gcd5htlphA87kIT7/ZPgNO9U1D8zbIFzAhA3DamzKTnybO3icl1n7KfPFvXzAAk3CRaza1P7czstM0zzMyAmcyrQtwol8unX4GYs9Ha5gk2fCY0zbxno2aed3NmnvmfU66Af7ddQPPt9JvVnvWfeYB/gtOzxnHpmwmzozZXfzyOd7qS+/4QNm7xPOYq9+zmOe/uaL1QbqJ59/0b6sOSjczUFDnMenh6o3qw2zHzu8Jj/m/Es4h0//1Xqm+bt7zAncz3zDw4VM8Y45ZlGO4QIuZ/JR6sDRnM9eY9SLSceGo/j8OPVhx/HhJRafEK5k82nhCWZ+LxzI6bzPNDPkzVSzwnFMPTvMz/F8n9nmhOU5kxvZYl54kh0XiJ+ZFoa1OI17mGeRerHJYveDpZaHa9hilfPYb7X7wtxrwnlMvsE+HM01bLHJfky21ZzyXe7mwG3qywzbwyksusN8sN8u8bHi7nAzq+wNt7HXfvPCFgfEx46H1JdlDofrWPBYuJilT4Rr2fGk9cx6OpzBfGfcBz7xWfgRO561nrXO6ycfvxB+yLIXw/VMcckccgzHMf9l88fkX+gbE34VtufbvMiOX6s7L/EOa17TT/a74XcW+Casxo5cxB28xOQ3xcPRvMgMt80fp3ANM9zxnsXv2Y8reZGP3w9b80OeYe8fwgH8jveY6YF7yDQ/hRPY/hfnsfev9mHr37znGX7H1P+QB9tzPFfyIvP+073iAt5n4n+FI1jlD/eF7f8rX17kPWZLmOp/NudAzuQJ3ueAROE9Fk4c1mZzLuVenmCmZGF25mMvTuMsLuS3zJEynM18aX3HhOnCt1k6fbiWj2UQJ3vyA57gDSbMaj8+lT38hB1zhpfYO1f4HSvkDjexX97wDtPkCwtxApewQn7r2buA/VmsYLiCTxQKP2LRx8LlTFkkLMixXMzCRfWPbYqFL/Ecf2DP4urF4iXDZ7mSX/DJsuHHzFs+XMAmlcwZj/Eee1b2O2/wHgdWNa9MWE3/mLS6OrA8R3EsN7LkE2E5ruYGJqgRpudwTmbPmuJk4lryZE+O4DHeYJsn1Z2N6oRH2LOu75mzXjiHBRqELbiIJ1m2UbieTze2Lz/jDQ5oqt58uqXv2KuV+8rGz/iOR/kZS7QNG3AVD7Fue79zPw8xV4dwLh/vLH5+yBtM30V/2Lur+8Ty3fWf6XuEmVmTAzmZ07mb99miZ9iBnXmSn/NLpuodjmOHvuEQfs6HLNIvXMZ+L4V32HmQ/TnkZes5YKh+MtsrvuNMPmS918KXeIA/MNFfwqrszXe4nd8x6ZvuH3O/pc6cx93s+Vf9ZNq/y5MT+SWrDpcnt/Me670tD1YdIX9u50O2HhmeYYJR8uRwfseX3lUfthgtHp7kPbYe43ye4UMWGhsuYeYJ5o+dJoaXmW5KOIlVpsmb23iAT80MP2GZWerCddzOl2bLj/nnuIccwvf5IR+y9zz14uPzfce677v37LxIf1lrcbiH6Ze4x6zFFziAk7mRe3iT9/jW0vDfbP5BeIKdV4iHGVaGU5h+dViMk7mCA9Y4j8XW+p3p1ukfB2zwPStvDLey/odhAx7kIT6+WX2ZbKs8+C6/ZO7tYR2+wHncx5tssFOdeIgnWP8j5/Egv2T1veFOPvaxOeMQfsAf+JBJDoRpWIAjOYGLWP2gunIn77H6Ie9Z/7C42emo+8Zmx8JPWeN4uIsDTtifqU6G49jptP341GfuI186K2/mOa8unM+HfOFi+CJv8ntm+Tx8j+kvOZedOZmf8EsWuewcLuNN1vhS/mzzdXiOya6bO+a6ETblXB5n7m/NHzPcdN+Y/lbYm5P5HZvfVi+e4EM2vBMeZpu74mWbh56Z8qdwLFP+7JnJH4Wj+cQv4Ues+Kt4uJkPWeW3cBuT/8N+zPKH/rHBf9w7NvivZ/ZOnDrqwcJJwqUsmjRczqrJwgEcwu28x4d8MXn4Er/nDyyeMmzAlTzEZGnCd9ksbfgpW6QLe/Mkv2POTOEcFssWrmCW3OF7rJIn3MZ8+cICXMhFzJ0/nMfeBZzPAQXVhS8VDofyB/7M3kWsZ8Ki4dvsVCy8zFYlwtN8oXR4k7nKhXPZ+3H7M3FF+bEVB3AEF/E077FoJfPBEpXDVSxQ1Xo+Wz38gjmf0B82q6G/LFtT3HyZ6/kdf2LBWuFilqgTluIqruHLda1nrnrqxVYN5M3CjcTDpfyOzz4lP5Zr7D038AuWaCouvtjMnPB7/sAUzfWZY3iTiVr4ne/wJvO3sj/f5w8s+7T6snqbcCfTtA0nMGF788lnO8iPBTvqB1s+G55i787qx8RdzB1fek58rNk13M1s3cOZLNIjXMZCPcMlbNnL+azdJ9zLIv2sZ8uBvmf+wfLk+/yC+YaEFbmQm1lmaLiOFV/3O194Q968yR9Y+i/h4+zItfyQlzjgLfeVLf/mPU/xHnP83XeczVN/vh/uXJ7iJbZ823v2HqW/LDzavWHmMeae0/k9M4/3noMnhD+y40T94CV+wbyTwgV8ZnJ4luWmuZfMOT2cw2EzwkdMPNN8ssus8ApTzA6b8hmO4XGeZZ054T42mauOPMZT7DhPnkw9PxzPhAvdY77N9X++X+KZbZeF51lrebiHRVeEy5l5tX5xOr/n4DX6wyZr9YfH+AXbbQgvsO4m+3A/f2STzfZh2h3hRObb6d4x00fhNNbfbf55kPf40h73jXU/DptxPz9ll/36z8GHwqH8kT8z3RH7chL3s9cn4bcsdTRcwy7Hxc8rvMdsJ8PGnMmjzHtK/7iA33PoafGy3Rn9YaPPxMkj/JbtzvqejS85n+kuy5vlv5QXN/IKy11zP1nnuvvCUjfUh51uhpeZ8LuwLwfzbd7mj8x9S504jwtY4Xa4iXnueM9OnM9NvMxed9WLje+rDzN+HzblVB5nyh/kwbH8kW0fuM8s8GO4iOl+CgtzEpcy/6PwfRb9xf1ns1/NP9v+5jwm+pd17MRefIfLeZnfMssf4XvM9t+wCHtxJpfxWxZOkCbyYZuk4Tm2TR724nl+y2opreMOnuNTqcM+/IS3+GT6cCg/5s+snDHcyucyhX35FW/zpczhD0yVNSzGcVzBBtnFz0M8x9q5wr0slDt8kYO5hN/zR5bJow5cx09YukBYlWu5nU8VUj9+wlus/lhYjzt5gE8VsZ6f8BaTFA1H8vFi4YesXjwcwJ28x8El5MtipdWXWR8Pm3AGj7FyBf3l0Ir6z8qVnMOt/JEvVlZ3tq0SduaLPM8v+T0HV7Mf+1RXHw6u4Xc+V8v88SveZq0nwz0sXMf9Yp+61vMWbzNzffXiS5zOY/yB+RqoGxfyZ5ZqGK7hM0+FZ/lkY/eOfZvaj7f5M8s2C+tzPQ+yb3Pr2LJleIpPtA4/Ysqnw7Fs3FbfeJTfs2JH53Azb7NqJ/eHRZ8Ll7Not7Adl/MCX+zhPLZ73u/s00vfWPsF9eBQ7uVB/sxsvcMcnMnZbNE3PMns/cMabMhZ3MXDbDdAvHxxoHxY+0XxMflLYSOO5hEOHmT+2fBl5/BFHuYFfs/cw8J57PuKfjHHq/ZlO87mYV5g39es54uvO4+N3hA/B/9F/Oz8pj7yS95iqbfcY67hHg79m36y1d/D0xw60u98cZT4+Nho6/gBT//5foz4eJq3WHxsuJK5x6kv+45XH9acIC7u5s9sNdE5TDkpLM+x3Mihk63nc1PCr1hiariKRadbx+X8mYnfC/NxBBey6IywOZfzBNvN0Wde4I+sPi/cyTLzw3VMvyCszMncynQLw0lsvkQ+PMGf2W6peHiBPzL1B/bhYI7nCf7IcsvCDWy4OvwbD/M/7LPWvPDx9eGHHLzB/iyyMVzGzh+6F0yxyTqO4Y+svtnccSdvM9O2sAqncRszbA+nsM6OcB9zfRTO5dDd6s2qH4fb2W6fujPLQfeQ7/F7Fv40rMul3M8MJ8THTCflweSnwtEcfFp9WO1suIPJzpk7duG7nMwrLHAhLMRFXMKKn4eb+fTl8DMW/kJefPFL+XPwdfEy8TfmgtU5gsu4k+W+M59scTs8yb73zQPLfO/+cfAPzmfiB85jmh/DCez70H4s8ZO8uYqf/fn+Z88s9Ej9WOUX88gq//DMQf8MHzDhH+Hb7JIg7f+8wi6JPDNV4nAckyYNR7FP8vAWc6cM5/HJ1OHHTJ0mHM8X04Xfc3D68EdWyhoO4hY+YLps4SQWzhEu5aBcvufgPPZnqrxhNo7jTGbPH85i2wLheVYsFLbmZp5hl8LqyXyPec+FvMJ0RcNibM1nOYkreIZfcHAx+TFZyTA9s/NdTuYs9i0V3uaTZfSPg8qqJ3OUD3OxC2dzLq+wxOPhKqataF9O5AP2rSQv3uaPzFglzMtybM2pXMANPMOC1cLFLPlkuJrNG4Qn2LRheJyFGsmbS3iFyZ7yHZvyXS7hcaZrKn5O4hn2baYOTNQirMV3uIfVWtqPO3iG1Vr5na9wB8/wFw5urf6s/XS4l32fER8zttEHlm8b9udG3mXt9uHL3MufmKlDOI09O4Y3mL2Te8pZPM98z8qDC/kjCz2nLyzUNezKFzmIS/g1v+cDPtvN/WTrHvbhGX7P5M+HeTia81m1V7idvV8Iv2PzPvrAE7zN1n2dy8H95M8u/c0vE7wkLg7naDYbFH7K1oPFxTP8js2HiIsZX9YvTuV5th9qLnmRv7DgMPeX2V+VB1/hLF7hL0z4unrzbc7noDf0k4/9JfyAzd6UP7u+ZR5Y62/uIxP93T1l8uHmhKP5BRu8HR5i1xH2Z7Z3w5nMMDqcwhZjwpMsNTZcw3zj3APmGx8W5kIuZekJ9uUgruVJPmDpie471/IuO0zSJ37OX1h1svO4nSc5eIr5ZZZZ4XtsMdf3fOV9+7PA4nAR8y4Ly3IB17PQcvtyCU8y1YpwHN9aHf6b+deKg+/zF5ZaH1ZmU67hVh5noQ3iYP+N6suSH4armXezvNhuS3iBTbfbn4N26Sdf2SNept4bZuV4zmD+j8OiHMz3uZw/svr+cCebHhQHj/MBex8Jv+PLR8OfWOiYPLmEF5jtuPvDdif8zsdPhh+y8il9YI3T8uIu/siEZ8K3mf5COJkNL4aHWeZzfeQ63mXnS+GXbPeleDn4ivP51FfhJ0zzddieE3iRT13VZ37CX/58f80zB13XD3a7GV5ljlthXc7mfma6HZZgb07jKn7HsnfcR2a4G05h+3vi5EX+wrrfO59JHoQj2e/H8A6r/hxuZ/tf7c+L/IU1/mEOWOX3cBv7/dP+rPHvsA6f5i7u42d8+Q/zzaQJ0/3PUhzFNWyfOLzIbknCq0yRLCzIMVzMIsnDZayTOhzEfXzA5GnC0cyZNpzDbhmcz7aZwvMclNl+LJojXM4MOb3nFC5nltzheyycJxzMpfyRtfOG/bmXd1kpf1iVW7id/QqEL/IOv2fbgvJjkcfUkymKyptj+ICJi8mLI/ge0xbXP07kYj5ZQpz8mHfYsGR4mP1K+Z01S4e72aGM7/g577BU2fBlruFPrFsu3M+G5dWJh3medSuErbmfZ1ipovpxCx+wUiVzxy28yryVwwXMW8UzS1WVBytWDzczdY1wPNPUCicw+5Nhcc7iSlavE+5k67ryY4564Ww+1TD8hE0b6QeP8w4bNw6PsmBT88HKzcIG7MetPMQ7zNo8nMEOLcJX+Dl/YaWW6s5ireXB/lzBT3iXOZ6WP/O0CeezSDv3iY3by5fdOpgPXuUDJukUjmSRzvZjiq7uJTN2tx+n8gFL9wzXMtvz4Uzm6mOuOZf72b+v/JmhfziFmQeE0/nkQPeagwaJh2WGhM24jp+yyMvW8WUu4wP+xEavhkdY/nXruZEP+PQb4WfM9Rd5s+Sb4Wp2eMscMc/fwsJ8nM9xPpfyQ37Ftn8PzzPl8HAsU78tL47nEWZ5J6zK97idtUeGe9lylHvEU7zD3GPDeSw0Tv24hA9Ydny4nqkmum8cx1/YcrJzWX9KeJD9pomHb80I/82qs+XHVHPsy3E8xefmqjf7zzO/fGJ++BGLLXDfmW2h79ifr3AmP+Jd/sLy75s7Dlqsfiy0RF35+DJzwlTL5cX+K5zDQSvtx5yrwjnMulod+DJn8Cv+xDJr3EPWXSdf7ucvrLbevtzBn/jWh/rH5zY7l6W3hmtZebu8uJUPWH2X87iTPzHbbn1hib3hKvbfp34sud/9ZokDvmeeo+4xnzguD/bjR/yKd1jtU/XjDv7CzCflw+n8iUVPhcuZ8bQ8OJV3+coZ+zPNWXPNNpzAFTzHROfDd5jkonw5kl+x8ufi51b+xCGXwofMcDlsxSk8zdRfiJvjOYFZr8uDM3iOib8LR/DlW+Jh/tvh+0x1x/1i9rvmjLO4k0nvhaOY5IG6sOLDcDNf/Sn8la8+8syuv4iTX/MnlvzNfLLs7+F61vu3evAAz/35/g/PLPsf6zkoQfo4nw/4E19NGP7KUknCNayZMuzP3bzLV1NZz/xpwq58n1+zS0a/8wq/5pDM4UOmzxJOZtccvme6nOEkps0bTmS+fGFxLuRK1i8QDuJBPmDVwurH7fyJSYqEeTiS81mraLiHdYqF+5iqeDiO2UqEDTiTh5irVDiX7UuHF1m6TLiWz5TVP57lXVYpH25jg8edx9wVwnksXkn92L+K/VioeliUS7icyZ8IR7NrDf1kuZrhBiaubQ45grtZ7MlwBWvW9Ttfqxf+xjQNwgnM3kgcnMWvWfepcD+HNDGf7N9U/izQLFzE/s39zvItwo1M0jqszJHcyi5PuyfM005f2YXzeZZXWLaj77ieV/hMJ+vY/NnwBBt0Ng9M1zUszDrswWGcxKXcx+t8xAY9w5bsx0M8xTt8rZd+8uUX3Eem6RvW4DPswgncxbO8woT9xMW3eYjF+ps35hgQzmbCgWFKvs2xHPKi+WGqwe4/M78cTmeRoerMZdzHl4fJn6VeDdfwsdfDD5jpDflxGg+x9l/CvXz5r/Zn+7+Za17kXb78d99zyNvyZaF3wu5cwmtsMcI5HMTXeJIX+YC/sfYo/WZX7uUVfs3M74qb0/kTO472Oy/xJ7YYIy72Hyte3uVvbD9e/Mw6wTmcwUtMPDEcwWpTwx2sNs0zX5vhPLaf4zzmnRuW4wJuYPd58uAgDuE13uUDPmSK+eEY1lkoDu7jRab5QN84gXv52hr5sNXa8DQTbAyHs9yHYSNu4BFW2hRuYcIt7ikbbgsPs/t2efO1HeLhax95Zqo97jOL7RM3V/A0Mx9UX07nQxY+4ncu5UPmPOo+cw4PMfGn5ocNToQDeYj3WeKU77iKh5jiTFiaY7iWzT4LP2W2s+4PZ/InPnY+/IANLjiPz30efsUyl8IOXMfPmeHLMDtf4xTO4m9seSU8xQpfh5vY4Kr68xAfssB19eAirmWBG/blIm5i4m/DKuzLEdzG20x8M6zJhhzB3TzMNN+pCydwCvvfcp9Z6ba68DVu4Sn+9ud3d7xnm3vhOba8r488xa9Y9sdwPUs+DFdz2M/hI9Z6FO5hpn+EudmB0ziPnzPh7+47M/wzzMIKnML3uIlp/h3m4WucwPn8jW3+ox7MlyDD/1zIEgnDVSyUKGzCJTzGIUnCh+ybNLzNAsnDl7mIP7FJCvuxZspwN5OmCkexZOpwIFfzPtOkCScwdcZwPDtmDi+xS/bwCovnDF/jSv7GTLnDMpzGdWycNzzKgQXEx44Fnc8Uhb3naxzDS/ztz+8e856Ni1nHo/yN9YqHB9itVHiVOcuGzTiHn7JS+XALSz6u7mxfIbzIjpXEx5KVfc/GVcTJ4lXDWlzJPaxdTR5sxL08wCNMXV2/mbaGfdiGE7mX55i6pvXMUDvswSm8zpRPWseyHMvxXM8sDcL3WKBhWIOLuIvdGukX6zWWNwc2N19M3CIcwfYt9YevPWM+WKxduIL1O4QH+VRH+fMTXv/z/bPi4HXeZ5Uu4TbWfS7cz7pdPfOJbuFHzNlDnJzDFXzqeXGxRy9xsHafcBj38hHz9AvnM3H/sA6f4Qju41lWHhhuZfnB4UamGioOjuNeph/mPHbia5zMfbzM31j8VfeGSV4PR7LwX8KlrPOm/Zj3rfAx1uBrXMAPuIu/MdXf5cWU75h/5hihzpzN+Uw0MhzCd/iQjUa538w2Wp05kxuZZYz7xTxj7c/5fMgq48wnS04MV7P5FHPAEzzLAVPDe6w6LdzOLO+FT7MT3+NnvMwiM8JlTD/THDDz7HA6H5ujPxwyV37MPT+cx0SL1ZlJlpoXNvkgPMaBy9xnDlltPviQv3HgGt/xPh8y0bqwBd/hyT/fr/fMbhvCqyyzMVzHXB/qA4dwLu/xIVtssj+TbglHsdNW/WCnbZ45cKf8eZ+/sfWu8Awz7gmnsvFe+/Ao77PYfudyBS+z+sFwJ1Mect844LC8We1IuIMVPgk3ccAx3zPvcfeeOT913ziHG1nohLy4hEf5zEn3haVOh2uY+IxzOYILOOQz/WOl8+FQbuHPbHghPMwMX6o3p/A+k37ld47ifQ655nwWu64vTHPTvecEbmfb78LuPM9rTH4rzMPRnM8Gt90DHuJVVrzjO77GzTzE3/78/a5nNr0fHme+78MqXMhtfOyHsBY/4J4/3//omfl+CotzIVcy1yP3ldV+tQ+bcgf38Dir/qberPUP33HI7/rHIf/0zNL/Ctcyw7/NCRv+1zyxTKKM/3MdcyQOZzND0rA6G3IKd/IwUycLxzNX8nAuk6QJu3Ekr3Jg2vA+h6QPH3JgBr9zQMbwHhtmEg/zZw6fYEO24/v8iId5gXmzhK9xAX9j+mzhZHbLLh+mzyFuTuZ9dssfDuNVPvrzfSHPrPlYuJtVSqoDt/Ees5VyDmfyKoeVsT+Tlg3TMDe7cxQncB6vcUgV/WL6qvJmt2rO57DqzuewJzyzZk35slbdcA8zNbCOr3MaH/EfLNJIP7iM99mgcXiIrzeznsOa25fdW8qfiVqH77DoM2ElLucWPt02/IxD2qkfB7TXTxbsGC7mwE7iZ5pn9YfZO4ezmKxL+C6zdg1nMGl3feawnvJliRfCVUzeOxzNoX3Dn5nnpXA+cw4O5/D1IerLFC+HY5h2aDiRlYeFW5nl1fA9DnxNPVj99XAnu78hL17jIzb8S3iYCd8Mn+Lb/ITV37I/k/01zMGifJezuZwN/6bvPMyHTP1333E8D7P7cHmwxEj9YdF3w7ZczvPMMFo9OIWPmH1MWJxtOYsreZ71J4QHmWqSvnMc53DoFPPCBlPdP+afFpbn+9zIatPDHUwxw/ww26ywK4dyJr/mz6wwJ9zE2nPDvawyP9zGLAvC1nyPZ1j7feuZa0k4l22X6iPP8xoLfCBuLuLPHLLcvLDqCvlyO79m9ZXO405eY9VV1rHY+nAFh25wPlN+GI5l0k3hKFbdbD+m2SZvTuB5lt+hv8y203xyJh9x6EfiYepdYUmW53iu5kaW2h2uYeM94VEO+9h5TLpPfkx9wP6sdzA8wOKH5cuVPM/iR8KOHMiXuJKXeJ8/sNHR8AiTHw9Hc+gJ9eEbJ8Pfmf90WIhPcCDf5xJ+xPssftb8syOHcSXP8BIfceh58bHyhXArB150HlNccr+Z+XI4nTW+CHex4xXnsufV8AYr3Qi3sPU37g/P8GvmuBXOZvnb5onJ74SZ2J+jOY132equ83iaX7PjffHz1R/CX1n6oTpxGNfyPh9xyM/hQ/b/RTxM/muYk6M5h61+U0ee5g1W+od6cujv+svu/wqvscS/w/pcxYPM/Ye55Tz+zlb/FQcLJsj0PxezWOJwBdMmCycyW/KwFmdyD4emDIfxZz5iiVRhT67iDXZPG15jynRhE7ZkT47lMZ7iDSZKH9bjOzzA3BnCvOzMeVzAL/lq5vBXVsgaPs1N/Iyds1vP/DnCIXyfD5ksZ/gun8oVfsKCecKaXMzd7J83vMuK+eTPzTzAMvnDdUxRMEzHMZzEzkXlw0ylwml8o3T4O9uVCS+wZFl142pu4pBy6sF25a1n48d9x6N8yBwVwtIcxtlcy0fMVlE8bMk3OJObeIq/M3WVMAPHcwpT1AgH8lWO4X3+ygI13RMu4iqmruV3jucqFqytr1zMdazyZLiNGevIm1N5ipUbqiO38hFTNrKeY3mKTZ5yP/lqE/myXFPncAO3slHz8AiTtDCf7MahHMlJvMqf2b+l+8FurXzHTq3Dyyz0jO+4hFfZoI3feYhX+VLb8Ac2aace7N9ePBzWQX2ZvGPYlKN5nOU7hRtZtrM+cz238aUu4mHt5/SPe3mKT3UNP2G5bvrEQj3Uha/21F9me9594kz+ziYvqDuP8TJz9BYHO7EbZ/MTXuZVJupjHd/hZWbqa/74LKdxK79ggv7hcOYf4D7yWb7PVfyC+V4SPxfyKpsNCj9l88H25Ql+wSxDwveY/2Xn8y9Dw3+y+TD7sMEr3vMQT7D/a+aSd/mImd8Mp/ONt/SXWf4qPpb8u3y4ml8ww/BwCoe+7Xy+wZ/5iL8z/eiwHSfzAp8fE37DlGPDsXx2vHiYbbK5Zaop+sRx/JSZp6kHu84Iv2bXWZ7ZabZ55LC58uKzi73nF3zEokvC5Wz8gbrxKH9m1mVhcc7gSrZaHp5m09XhcWZeKz/mWhfO5dAPxcuf+Yivbgt/5ZPbncOPeZwNdppLlvso3MByu8IXOJQbeJM/M+Vu/WbOveEcFv44XMpW+9SDrx4QP984aP6Y7Ej4Lp//xLxx2FH14Eufes8f+IglT/yfmuk7/seyfxy3vWWv7F32HmVFtmxSqGS0S/b82JtskRVCWWUmI0V2yUpEStG4G2Svbt/f734e3u8/jsd1vc7rPJ/rdE9Y8Cvxsd2h8CRrHw53MscR+THh0TA/O3MYF/FnNjqmH+x5XD354An9Z9Vvw0/Y+aT9WPSUvrMXl3MHr7Ldafkx6/dhRc7iRlY+G37MB34My/ItrmXnc+Jj/5/1k8nPh2P40AX3hf1/t56P/ifcyoZ/hPvZ8E/P7Py385n9YjibiS6Fw9n6n/A4O1/2PfNeFxcX8tb99zc8s/ZN+3Inj7PhLfEy4b/WsxOH8TjPMf1/wynMfS+czwQJM/2PnhzKKyyVKGzNNTzOzInDzpzBn1k1SfgJcyYN+3Aur/P1ZOHfbJo8bM5DPMxKKcJNLJFSvFzF40yVWr4czyv336fxzNfShn8x2QNhYb7C1zmaS/kH/77/XbqwPZfyNDundy5/5l8snTH8kNUyh9vYKmt4jFmzqTdn8TofyxV+zpfyhL/z9XziZ4kC6srUhcIJbPJw2J9f8hYTFQt7czivMWEJ9WFnDuNp/nz/91Kemai0fVmrXPgZ05UPJ7NwBX1g+0rOYbYq9uHb/IzNHg078WueY+dq4mOf6urP1DXUgxN4jclrhmPYppZ9+A2v33//mGc2qhseYN4mYTEu5Ap2aiZ+tmluHvkN/2LFFuFGZmsZluTbXM0+rc0Hr/MWC7Yx51zCD5nsSX3iaJ5mn/b2Ze0O4U4+0TH8iu2f8T37dxIPkz0fVuBobmCjzurL17rpJ//iNVZ8Rb3Y/lXns+Vr4VHWfz3cy4rdfc/ePe3POn3CXUzdV958nhP4FX9irgHhPOYdaE6YfVA4m68PDv9mlSHhZvYcFl7hQ8PD91lkZFiPy7iH/UfpB+uN8Tt7T7SO13iLNd8Kd7DslHAte04TH8tND9fx4RnhB0w/M5zCqrPExU+4h0XelhczzxYXO3EGd/Dc/d/nemb7eeaFfRaYd7ZfqM/sw9P8idfZ6V37s9Miz3x+ie9Yeqn7yPbL9IGneY2JV4Qj+Noq94KtVusfj/EWq60Lt7HKenFxM39iuo1hUVbhZC7nZubYFD7COdzCHJu9Z6tPxMlOW9WLmbaHpTida1jlU+cz146wMOdxKXt+Zj7ZZ5f+seCecAnr7w338smvw2+Z7EiYm6M5nxWOhRuY/7g42J+LeIW3WPOEOWa6b9WfhU5ax/e4gwlPhcNY8Wy4kZV/CD9mxXN+Z/nz4Xr2/FWcLPqbOWCp/+gPU/wRjmX/P+XH5/8yb0xxUVzsw7H8mNeZ5FI4kvX+Cfcw5+VwLvNeETcXcjlTXQ1LsBof43iu4jZ+zievmw8Wux2u4Gv33FemSZD5f2bjRL7NWgnDJ9mHn/FbXmfZROFalk0cvs61/JutkoTH+HpSvzNpsrA0y3IUP+RaFksermCa1OFr7M2J/IvXWCut/Jg6o3M5gWtZLLPzmDNrOJcZs4XTWDx7uJK5c4TzmfFB6/lYzvBzPpbLM3vnEz+fzh8+x+78jj/yEh8vEH7B1wvqM//mdWYsJD72Kex3ZioSZmF+TudMLmLmomFPzuAVFn1InFzOL9i7hPz5dEn5Mn8p57FgafPAJfyLScuE2TmKs/l8ufAnPl4+rM/X+AX38i/2qaBefK2S31micriKCauED3AIh/Et3mODquLkPv7EnNXMJ+tUD3cxYw39ZfKa+sMxvMKXa4X/4fOPOY8JHg+HskRd+bFePXXhHl6//76+Z5ZqEK7hIw3twy3cw7qNwt2s1Thsy894gkmb6Dufe8K9YYGm3nMxf2SRZuLmMl5n2xbOY4XW4QamaBOOZdl24Vqmbh9W5ARu5PMd3GP+xL/Zu6P7wsLPhI+yIZdyK/fz9efsxz7Py49lXgo/YsGX3TOWfyVcz6qvhc35CQ+zTnfzyeY9nMvDvM6SPfWHq/kji/S2D5fxMMv0tT8/4nXm6RcuYNv++skEA8LsHMrZTDkorMJx3MzX/0992XuIfvG5ofJh7xF+Z58x4mXLCWF39uZRXuI1Fp6k/+wzxX5sOUvePMoTTDo7TMZRHM2s74SzmH2u+rDG/PBpvs5P+R3/5uMLwoH8gnfYYKG8uY/XmPpdeXMCl/L1xc5jmvfChziR77P7UnVll2XhqzzPP9n7ffGwywrrmGKVenAsv2ODNfJi3o/ChXxurXlh6XXhh2yxPhzCI7zH1zfIn302mgcW2hS+x96b5cNUn4SV+SrH82P+yVxbxcV5PMLy28TB9bx+//1288DrvMOUO9ST4/jd/fefhc/zO/7ESp+HPbmJV1h3pzngbl5int1hcdbhAq7kLj69RxxMtzeczHT7PLPOQd+z5pfqwh28zvSHrePTnMId/I4JjoZDWfiY+8KUx9WVhb8NG7M7l/IgL/Hp75zH1KfDVpzAY6z+fbidmX8MG3IG97PjhfB7lv/NHDH/7+4hF/E8y//Hejb/I+zHw7zJGn+6F/yU11jkr7ARB3IZD/AOs/wdVuRMbmTBi+Gj7MMl3MrrTHEpHMtkl8PR7HPdeva+KX7WvOU9d/AaU9/WN6a+65nN/1U3tkqY5X8eY6fE4Tm+njT8m+2Th6dZKlW4hglSh0OZMU04jQUeCHtzMa+xQfpwHx/MEFbhO9zMBhmtZ5XMfmerLPJjjqxhXs7hQnbKFr7Kc/yTjbOrAw/yNHvnkA/7PBheZ9lc4UCu5R2my+13TuYd5s0rXqbOF/bjBN5k0oLhKD5ZOPyWHYqEZ5j9IXlyNg+yU3F1YbnS4TrmLxM24SJ+ycLlw6UsXyFcz04VwyE8x3vsU1k9WbBKuISJq4qfI3iQmR8VB2dwKR+s4Ry+w+vsVFNcfKxO+Dmb1tMPHuJNlm5gH3blhzzHC3y1oXngn7zDpo2cw4GN/c6mTcTDQ7zJh5qHxfg+V7BWi/AzPtEy/IrtW8uDp3mBlZ8MP2aydmF3juYlFn5KP5j5aX1ilvb6zplcwtLPqSdzdQvnseSL4Wq++rK6ss8r+s18r4ZF+S6Xc2B39WbJ3vZnnz72Y/1+4V7W7x/25kDu5TXeub9uoPd8cpD7zESDw5RsxtYcznH8mseZ5v/CiUwwRH85lBdYfFi4kh1Gh515hj8z5VjrOI5n2GJceIRPjpcfX52gX0w80T1nhUnhBiZ8Sx/ZgsP4Lo+w7ORwLR+boo78nF+z2tTwDW7jReaeFs5niunhWHadoZ5sMkuf+SWvscXb4uXA2eaD1eaIg/XmhntYZF64jCUXiJ+reZFdF4qP3Re5r6z9njnnTl5ni6XiZZbl7in7cCY/5HXWXBnuYOVV4cd8eHX4AfN/GDbiIh5gs4/0j+XXhev5xgb5M/nGcAx7f6wfrLI57MfNvMnqW8PtfHxb+AXrbA93MePn4TSm2RkW40SuYIsv1JeP7g23sv6+sDH38iDf2C9fJj7gXrMuR3Ald7P7V+aBrx0K/2Lvr9WLRY6aPw48Zn7Z+NuwLQfyIE/wDsueNIdcy51se8p3zPRdWInTuYnFTqsve58RP3N/7z4z81n3hDN4gQXPqTMHcgkv8g6T/ByOZP3z+sbHL6gTv+BBtv1Fvmz7q2d2/U18fPD3sADL8h0u5lom/iPMwZIcwTlczQf+FC+78i0e5AXm/8u+HMhFXM07zHcxfJclLukvV3ET217WD57gRT50JXyfua66V+zHedzKm3z8ur4wy+1wJgvdCd9jmrvmjBN5jZn+ay45nSuYP1HWqA+7JgkvcGDS8A4zpQins16acA/bpg9PMFmGcDQTZQzTsw6Hcwp3MWUm69ib47iL19g9S3iJT2QL3+BXvMim2cM2PMRv2DZn2I4neJIZcoXl2ZRTuZ6HmCN3+AjncAu75lF/5s4Xzmf3/PJj04L2Z7lC4Tp2LhL+zDeKyp/pHgons/fD6skniqkb65cM97JrKXPCC7zDvKXD2uzOhdzJSyxQxjou5iVmLRfOYo4K8uQc/swWFcMjrF1Z3biTh1i3qu+4m0c48BH5sXM1dePPvMZc1cN5fPKx8FvmrBPOZZPHwy/Zua59WbyefbiS37JEg3AVEzYynxzG9SzY2HdcwlWs0sTccDMvMtETYUcO5/dM2dz3HMeL7NjGehZ/MmzGflzJr3mT3duZD/Z9KrzBSk+Hm9i6Q3icj3cMv+CDz4bV+Abf4TZeZOtOzuVx3mDbzuEJNu8aHmbZbuFaNnwx3M9qL6kHt/Frpn7F+ezHCbzBm8z4qrpxGi8xwWvuGbtzIIdyFi/xDqu/EW5nih6+51jeYbOe8mGhvmEFvscNrNxPX/gxL7JCf3PF3tzA73mN1QeIl3UHufdsNlh8LP5/5pFJh4Sj+MCwsAH78S3u400+MTz8iq1HiJvHeY2tR3vPzOP0iTN4iYXHu99cylXsPcH+LDcxXMfKk9SZ7d4KT7LAtHAxC87Qby7hHZafFTbleh5ix7f1iYlmy5PV2ITDOYPb+CWLzgmXM8s74UwmnetcjuL3TD8vnMLc863nfH7P8gvkxbQLxctJ/JKZF5ln9uUMXuQNFlrsXjD1EnXhBG5j52Xhz2y3XP9Y7X3r+ejKcCv7rnY+u6w1NzzPa2y3zjqe5HlW3+C+sftG88qCH5sbpt0hD07iz/fff+aZpXeGH7LfLveNXb4QDx/aE77PR74Ma3ELP7v//rBnPnU8PMX03ziXU3iTGU7oO6dyBht+G+5nm1P6wG94g22+855pvndvOZEfM+1Zv3MSP2aXH9SHuc6F89jmJ+exzc++4zc8z6znrWM/zuI3vMmav4Q7+NSv6ssEv4WVOJSb+MYf7g3T/WlfvsHJ3MGLzPp3+BjfYG/O4ue8yGv311/0nt0v+Z6XeI2d//E7+/JnXuMNDrwc3mGKG+FYprmp70x8OxzB3omyxb4skTiswVX89P77pJ6ZLEU4mg+kDt9inbThLlZ4INzA3umcz6fSh6f4VIawH0/xJrNnDGezdyb7sU5m5/PBrOE7TPKguDmSbzFxznAEk+YOR7F4nrAvV/IGi+YLl7NNQfnwG95kgkJheg7lFKYoHOZkP47lXN5kyiLhOBZ4OFzMsiXCtaxaMvyEb5QOL7JL2fA8C1YIlzBdpXAyk1cO87Ifx3Ahb7Lho+Hz3M+fmLpm2IETeIZ1ausHB3EXb/Au69UJ97BhPecyZX3v+TTHcT+/Y4GG6svkjcK0HMNJrN4k3M7STcMPWaeZ+PlgC+/Zhe9wF8/z+ZbqxyStwkdYhyO5hbuYvLW4maFtWIhT+R7TPykuTuF5pn3aveEkrmS79uaEJ3mTXTrYj+06Wscuz/id/Tr5nv2e98ycXd0bdunme/Z4IbzMpC+GpTiKa+6/f0nduYY/Mc3LYRlO5Efs+5o5Zf3u4V4m7RFm5SjO4uM9fccvuJepe4XlOIHrmKavPnMip/KBfmFtvsWdLNTfPDDzgHAGew8Kr7HvUPkyyTBxshhHcgJX3P99eNiBK3iGKUaEmTiW01l7lPhZbKx92GW8eWC/ieaHz0+2jj/xPMtNCQdzHf9l72nqwYozw43sO1t92HpOeJyZ3gkLcDoXM8NccXIqf2KG+WFLTuVRllsY1mUjPs913M0D/ImV3w3r8GPuYsvF8mIXHuVxnme+JWFhvsulzPme89iSc7mLR+//vtwz230QnmTlFeJm4lVhdbZkJ47gdh7lufvfrQ5rcjt3MOGacBgzfxjm5gzOZ66PwnmstDbcxPTrwiqsy9bsxynczN08zpvMvj6czSIbzAOX8Sd23xheYs5Nzmc/zuVx3mS7zfrB5J94zzE8yX5bfM/M28MO7MYZPMNfmO/TsCAbsR/f5RIe4E12+cy88vnP1YN5d4YL2Wi3fVhsb7iCZfaJnx/xDBMfME8cwR1sfTDszeO8xuKH1IUr+Qu7f61/TH5EPTiGN9n6qHNY85h4+Pg34Rd8/oT6sN23+sm836kXq5wxp8zzfViLC/gZH/ghLMe3uI5VfrQfa54TLzv8pN4s+7M42ZdreZI3mO+XsBXf5TH2+1X92O039efjv6sXC/7H/PGZP8LBPMt/Wf+vcC+T/h2OYoqLYVE24Fgu5z6W/CdczdaX9Zulrugr1/Antromfw64Ed5m0pthflZhXfbjKC7iZu7mTba65Tymuh2OZ7s74uBJ3maOe+EclkyY/X9WZAeu5kaeYc7E4fOcy5/YJEn4JfMnDWtwET9lxWTO5UaeYbsU4UlmTxXOZo7UYT3O4R7mSBMWYD3O4WLu4RNpw6/YPWN4iekzec8pvMRUmcM8fIivczwX8H3+zR7Zwst8IFf4Fh/I45mF8oXvsXf+8Br/r2D4XxYoFLbkYh5lhiJhF07leRYsGi7hI8XDLXy2RPgDG5cMDzJBqXAoHywdvsM2ZcJvmLOCOeOjVcOt7Pho+D0HVw//5eAantmkpn7wS/7NzLXCGUxZOxzHRHXDohzO5UxXP5zMJg2cw5YN1ZtZm4QPcxY/YPGmvuNKHmW6ZvrGyZzKxM3DEUzWwhyzETtxNKfwAM8xa0tzzVl8n4NbqTdTtRU/x/Mokz4ZjmKLp63nER5lpg7hdLbsaO55lNc4+BnxsPNz4c/s2ym8wUee955beINVO4efMGmXsDw7chTX83um7RYO4CTeZtMXwkMc/Ir42fE1+zHtG+FTnMRTrNY93MbCb4bl+CiXch23smMP5/GJXuLiV/yXTXuLnz36hJeZta+5YedB+sGHh6onP+B65hnmOy7gz6w4PNzI1CPC1pzA40wxUl4cy6/YeYx+8WfeZuex3vOJcb5n5/F+5xMT/M4sk8KZLDY5XMHmU/SXh3mK/aaGN9lmevgNn5phnnmKN1h2ZriWDWaF+5hzTvgM5/Ism80Nv2aV+eFmJl3gPrDMQnXhR/yZgxeZJxZZrC58hsu4gmfZd4n8+Pry8G8mfT+swFHcwKc+UCeWXSEuruW/fGFN+CsrfBjW5gbuZLOP1IsvrPM9K28IP2bCjeEwptrkO/bmeP7KaxzwiXllmi3hRA7eJh++/ql6Mdln4Whm+jycztd3Wc+Mu8NpTLPHeUy7zz3nJH7AHvvDy0x2IHyTo/kPExwMi3MoVzLvl+rAhbzNF75yLn/lZSY9JE+O4t/McTjsygGcwwu8zQxHw6l8/Rv7seAp885nuIQreJblvgvXscBp77mY65jvTPguB5/Vfzb4wf1nsR/DplzBQ6zxU/gpB5yXL4teCFtwOY/wmV/lwzq/h7uY4j9hSb7AsVzNX/nMH/Zjnr/CBcz4t/lkpovy4HR+ylaX1InH+C+TXA4LcySXMvlVv7MV+3IMl/IYb3DwNeexzPXwI5a/IW+u51m+edN9YYpb6sfX75g/tvpXHMz2X/3jm3ybR/gP892zD9/lMT6UKMf/fITvcws7JA7PMEWSsAnH8ks2SB7uY9qU9uEknuHg1OG/rJs23M2a6cIdLJU+7Mo1vMCKmcONrJnN9+yRIxzAy7zNFx4Mf2WTXGFffskbfCBP+BaT57WeY3iDSfOFrTiKx9g1f/gG3+QFXuQ/bFXQ93yjiHUs+VC4mq0eto7HeJF9i4uXA0qqD1OUVUf25mCO5WVe479sWC58gvv5FXuUFwcv8waTVQhHM1+l8F2WrxyuZ5oq4cOcyA/Y4xHn8NlHwx/Yt5rzWaJ6uIqpaoTjmatmWI6PsBZf4Dyu4xZ+xl9Z6DFzwPf4D9vVCU8yVd2wABtzPBfzIPvWlx8zNwhnMHVDcXMCf2WzRuHXrNRYvNzEf1iqiXvLwU+YD2ZtGlblLH7CZM19x9H8l/Vayot7eIOZW6sbX+IAzuBJ/s7b7NvGfnysrTni57zMN5+UP3N3NB+cz3V881nrmeS5cCSf7WQ+mayzvnM0x7NMF/eCH3E9674Q7maCl8KSHMrVTPVK2ITj+SWTvRa25QAO5mie4G3+y0JvyJPv8Qc2e1Od+DX/YeGe4VIW7BV25RJe4MO99ZUf8AZf6GPeWamvuWa2fuHbbNA/3MfBg+THFEPUl2O5mn2HOp91h+kPnx2hHhw80v58dpT68Af+w9yjzR2rjwm3s8dYcfAyb/CFcfrKX3n7/vvxnpl7ov04nzdYbFJYgSu4gX2n2Jc3eJv5ptqHffkuV/AG35yuHqwww3xwAy8w5cxwHJ+dpQ78gb+y2Wx15Ne8zK5z5MMLvM0e86xnw/lhc+7nYdZfGO7lm++qA//hDfZfFN5ikSXO5TJe5pvv2YclloarmH1ZOJsJlouXQ3mYT70fnmLFD8KNbLVSnXiMF9h0tX14iKf40JrwfSb5MBzJ/GvDRcyyIZzJwRvdJz67SZ+ZdLN6cRQvs+0n4Qnm3mJeOJ+/sv9W/WDFberBTJ/ah9M5n2V2hB+x6Wfi4yFeZtPPvecjO8MtbLErbMkjPMpUu8Px7LHP/sy4X384jafY94B55LNfqi+LfxWuZPlD5oHreYEJDvuO9TiUK7mHbx51//gPb7PtMXHxBH9g0uPWcRRPcMBJ+zHhqXAYW3ynrmx6Wj/Y94z68Nnv5c8feIHpzsqLk/kPy/+gbnz2R/uwxTnn8ghvMNNP5o4NfhYP9/EC050XB7v+4ndW/Y+54Se8zFx/hPNY+E915VLe5pt/y5eFL1rHtv/oD2teDXewxzXns/b1cCdz3grnMu/tcCG73g27sS8v8BfeYMIED/7PZhzGr1klcbiZ6ZOFU9gjdXiZ6R4IJzNT+nA6G2UO+/IAbzBPlnABc2YN57JUjnAw1/BfZnowrMjmnM6NPMwEOcOhTJkrzMjs7MpxnMbZvMDmecKneZjfsXS+sA4/5C42z+87disQ/sJkhcLRzFwknMFMRcNnOZ0/MMVD+saxvMyni4mX5YvrF9fzBvuXCG+xb0m/87FSYWN+zoPsWl792LhC+CYP8h9mqBxOZeMq1jHdI+EATuZtJqxhzlmrZvgZEz0WlmEP9udwfsTLvMUKtcMN7FHXOnarp58sUl8e7MFlPMjLrNog/ITpG4W5OIXzmLdZ+CQX8ltmb+lczuZllmrtnrF9m/A08z8lD9bmIi7jThZ7OlzBAe31h8U6+J09nlF/XuYtVn823M70ncLcnML5rN1FPBz8Qvgvs74YzmLXl80nE7wSDmWW18KZzPOGe84F/JBP9rQvv+UFZuulruzFt7mIV9m1t+9ZpU+4mY8Nci4/57dMP9jvnMJv2W2oeWXmEe45Z/AGy4zSN37Ey3xzrPvMCuPcF5adEK5lwknuK3O9pY+cx52sNdl9ZpGp4uUy3mDlGeHH7D/TfDHj7DAHy3MAp3EO1/M2i88P63Ild7PMAr/zI+7mgIX2Y7FF1nMFd7PUUnPNNfyX3ZbrJ9t/4D6z/wr5s/9Kz8y4Wt6suCbcyAEfipcDPvLMFOv0hSXZn2O5jKt56/669Z5ZfoO6s+JG8bDHJvPHPB+7l8yz2X1jIS7gIr7H1J+EE9hgm/niPt7go9vDrRzwuXrwzZ3uA3vsEi977PPM/PvFxYYHwv3MejCcxR5f+Z4DDjmfj3wdbmHSI/LhKN7ggKO+Z91j5o85T/iec3mDSU6FI5nqdDieA763Pwec9cxuP5tfNv4tPMj0v4dT+PAf4Qes+XfYhTt4niUuhavY4LK+sssV69n2tvc8wfNMck++7J8w5/+8xb6Jwht8Omn4HWukDD9luVThOpZJE37EbGnDt1n2gXAtB6QPbzNvhnAhi2UMV7BhpnA/E2cLu3AEzzNZ9nA0yz8Y1uN67uHTueTLbrnDX9glj/c8z1+YJG/4AKvxafbiSL7FbfyOV1kxn3O5keeZM786ci5v8/ECvucXPM86hcJdTFxEHfl0UXGxy8PhmzzPf1iyWLiaaYuHk5irRDiPFUqGG/hMqfAsB5SWD7OUCWfywXJhFb7DzcxVye+cx81sWNk8sXrVcDuzPhrOYtsa4QlWrOk9e3AQN/IEL/Mu69cK97J+bc/M87g+cAG/Y7e65o+J6plDDudIDqivnkzVwDmsz7YczwXcyxPM2jCsy1nczWyN3H/WbhzuZJcm5ojVnnAf+Ewz88CqzcNP2K2F/Fm2VbiWlVuHHzNxG+tYmU3YnyO4lh/zS95i27bqwbZPemb2p8PZzNE+nMOcHdxbVu8Y1uB2fspSz4iTa3iLfZ8Pb7BJZ3EzezfxsNaL+sDPeJ4ZXg5Lcyo/ZJJXzBcHdDdfrPCmPnIDz7JXz/AqS/QKVzFFn3Ass/U1T6zdTz24k7fYoH+4j3kGmGf2H2Y9Ew4Ph7HcyHAdy43yzAFj5MtHJoiLW7iTuSeG81l7kt+ZbGo4mrmnhflZgfO5iBuYdnqYiQM4idN5m11m6jdTvh2O48Ozww/YZY59eZ63meEdc8LCc8MXuZS/8Zn55oGPLgy3suHicD9fXBIO4W+8xwrvqQcbLvM90y8Pp7DUB+4R86w0H3xgVZiX3fkWF/ISH1ntXHbjFu7nL0yyXj4cyXvsvsH+TLbJfDDDZvVm/63mmfU/DQdxL++y1A75M+Vn3nMc17DOznAXH9itHiy9NyzLD7mWg/Y5n1UPhJ8wyZfqwDyHwjJ8kt05hAv4Eb/lJd5j8sO+4xjeY8oj8uYTR807v+JtJjoWDufD37gvTHVCXziet1ji27AaV3Ebk58KH2R1juE73M5e3zmPV3mLtU+HO1n7TNiJO3mOPb4PLzPtWfXgJN7mE+fUiUN+Ul8+cV4fOYBf8VveZtUL5oOJfwnLsTKrsjtHcB0/5ie8xCK/h8tY9I+wNZfzOKv9KQ9u41dM9Je+cTjHsNdFfWGKy+FYDrmiXux+Vbx85lp4lsmvh+U5huuZ+VY4gzX+G37K1glzRX6skjjczP5Jw1uskszvLJEyXMUSqcJSXMU1LJo6XM5CacJyfI/r+Gra8E/mTxcuYsn04SAO4Wre5T2WzOB3pswYjmPTTOEhpswclmcn9uA4ruc5XmaCLOFQFs0mfxbKLm+WzROuZYW8YXdu4CXmKBgW4xyuYKtC4TF2L+x7Plok3Mp0RcMBnMzbbFYsfJZf8wc2KR6255c8zWYlfcecpbznXH7N6hXsx2e5naf5AxtVDA8wRSVzwbG8xwGV5cdEVcLhzPJoOJM1qoWfsmuNsC8v8Abb1gxPcFAtc8nmj4WHmbF22IvTeJXN69iXh3mCzR/3nhnqWs+pPMF09fSZlRuE1fgxt7FZQ/1h1sbhLD7bRD+Y/IlwDFs3DY8zZYuwJutyHHdwN3M8aW5Yii3YnnM4l2t4hKeZqp0+cjwvMGV7cXEcd7NXB/3nVd7gsx3Vg6WfCT9ksWfdV2Z+LizPumzEGVzP3TzA5zqFP7LZ8/rF4p3DllzJoxzSxf1g9q7O5wDO5gHeZq9u6sCWL9if3V8ML/Gll+zL33mbLV/xPVu+ql88ygvs/pr9meENc8+M3cMybM+u7M4hnMaPeJoXeIn32OJNc8YUPcKxLN1L/5m1r/vDxv3NDXvxIH/gVVYaIF5u4j0mHhyO4OPDwy/47Aj7sv1IebLlaHXmkDH2Z69x4mHj8eJl/knqyiFcxEu8x2Iz3Ad2n2UdB8wxH+z+jt/Zam54jOnmhUX4EidzGX9npoXWczqXcdCS8C6ffU/9OGip31lyWbiayd8PxzDxB/rDNKvdY77EifyRv7PpmvAQK3wYbmD6deEUNlrv3jLBhrAAh3IxC20M3+PDm8IP2GqrevOlbeJjlu3hTL70qd+Zdkf4AIdwEt/iPbb53JzyG55mpp1hXU7nblbZFW5m6t3miBN4ien3qB8f2CsuttknDlbbry7cxmPMfCDMx0Zsyhl8lwd4iM8dNAcs9WW4hk0PiYuH+A2bfu09mx+2Lw/zR7Y6Yq55jHdZ4Jh5Ya7jYR7O4wL2/CbsxSG8wqu8x9wnwuKszflcyZ0cdFJ8bP6dvJj3TLiQg763ngnOhs9wKM+y1Q/qwDo/hruY/1y4iPV+Dvew7PlwLYtfsA8f50ru4hds84t+Ms3vYWFO5FLW+ds+zH4xLMdnOJvreJYNLjmX+/gNB1wObzPFjXAsk94Os3AUZ7LonXA589wNi7A8F3AZ17P+v+Fepr0XTmKNBLn/56d8M2H4D1MkDseybpKwPtuwJ3dzL7/hFdZKHn7G+inDZ7iXZ1k9VTiE23mPldKEm1j6gfBD9kznfCbOFBbjS+zJEVzB33mF1TPLg9v5GbNlCbPzbc5mqazhGjZ7MPyaWXKGM9kwd7ifCfOEw5g0XziKWfL7jv05k/t5i8ULhCv5SMFwC/MUDutxAfcwU5FwOis8FG5g+ofDKexULDzHGsXNH3uWUG8OKanfzFYqrMq3+QmfLhN+x15lw6vMWd45nMsrTFQpfIHD+Ss7VRY/K1YRJzfyHnNWtY5zuZFVHtUnbuYtvlDd+UxRKyzIOhzLJdzFpI+FgziKd1m1tvqxcZ3wIGs+Hu5gkrrhSBaoFy7mI/XFxV7cwl28ykENxMNHG/qOW3mVFRqFLbiBR5inSdiaC3ic2Z9wH1mzWdiAO7iPrZubF/bncV7hLT7cKizLD7j2/vs26sO13MWWT4ZHWbaddWzRQb7M0FFcnMpbLPRMWIpl+B7X8CP2f8737PW897zKW8zQRd05lQvYtVt4gUlfkAdH8Qhfejn8neVeCZ/iOp7iS69az9av6w/7vyFednrT/WWqnmF5duIQjud6nuM9JusVjmap3uJlL67hKV5l/T7hXvbvJ16+MMD9Z6JB4XAOGuz+sP8Q37P1CPVgoZHmjC/wPR7lr6w0KtzEnqPNPSuN1Qdu4u9MPyF8iFP4PttPUnee5jlWekue3MS7999P9sz+U+TPJFPDkUwyLSzIlhzJJTzKLDPCmWw8MzzIErPCVcw9O3yYnTifH/Ac28wJv2G2d8JcfJvz2HCePnA/rzDn/HAuOy10Hnu+az2fWRSeZZvF4uFLS9SVv/MWiy0z51zBq8y4XJ85jafZ6QPxsf9K+7PNKvFw0Gr9Zp418uICXmH9de4XM663L6fxGzbfIC4e5jk23+g9E21yH9npY+uZeos54gTO5JAd4T0O+kx+bP25+8vkO8Mx7L9bvZhpTzidRfc6h8t5jxW/lBc38jBLHtJfruZVVv863M48h/WBLx1xPn/nPT58LGzFD3iMjxz3O7fwGPN8Yx0HcQG38C6znxA3W3E2t/MYH/o2fIHv81eWOOX+s9x3YUeu4/fMdVY9OY9XmfoH+bABJ3AL9zHlT+rAcVzAVuflw0G/qAeT/epecDSnMc9v9mOy363jg3+E77Djn/Jli7/DI8x50X1lf87lFd5i6sthek7gFFa7Fm5jgRvhYj56M9zKarfCdtzGk0x8JxzBQv8NB/I93mGve/rHPP/f////V4oLuIapEoYdOJ5n2ClReI55E4el2Y4DuZAf8iTvMGnSsBCbcBTf45dMnyzMxp6cwrd5hZ2Si5eJU4UjWDVNOISf8B4LpRcHX8wQ/sZKGcNNTJZJfBzNt9kpi3j4QNZwEN/iXdbIFn7KDtnVl2d4hxlyhFNZN0+4m+kK6BsncyFrFwx3slfhsD+v8hZ7FVE/XuU9pnvIOez1sP14lbeYsnhYnuO4nnVLyIclSoarmLl0OIMDy6gPW5YNjzJVeXPOpyqGp/jCI+GvTFEtHMtcNcJ57FnL/HFQbf1kzzp+5wuN7M/yTeTLB54wF3yiWfgVE7YIC7Moh3Epl7N/S/Vm4VbWsULrcAMTt3VfmOXJcCb7t7Mfaz9lflixfbiRPTvIn6U6+o5ruJNJngnzsTBH8l0u5WPPhp+zfid94F7eZbLn3VO+2NX9Zplu4Ues8ZL7yLIvh2tZ6hV5sNdr5py5Xw8LcD4XM9Ub5pHJu4djmPdN95UZe5ojTuOvzNM7fJgL+AEz9XF/OZ33mK6f+8tB/dWPgwZ7ZtYh4SzmG6ZPTDPcPWNPTuQpXmH+UeEiZhgTTmWdseLgLt69/36cZ9YZbx1fnKhP/I1XWXOSeNmRO3iK3zP1W87hBO7iq5PDP1lgiv6z11Tnc9A08XPIdP3hkBme2XOWe8krvMV8b+sHs78TFmNPzuYKXuGjc8XJrbzKfPPsz2zzzT3f5nz2WuB7DlkoH3Z8V31ZfJE4uJJb+fji8AtmW+Icvs2rfHGpevE33mLJ5fblan7BzCv0iTN4lzlWhuXZhC9wDtfzS/7KCqvUgRv4PVOsDscy9Yfmjh0/sp7V13nP7fyegz4WP0tukTeLbA0rcRk38ZHt4RZW+FTczLfDfDD75/blbC7jizv1hVV2h5tZZY9nPrLXfeUWnmLafeEk5j1gPjiIC3mVd9nkS31izsNhKc7lGg466numPB6OY/9vwoG8xTtMd8K8czK3suK3YWVu5MdsclK8TH3afDDjGXXkNG5m3u/lwYW8y45nzQ+z/hA24iweYNkfw7XMcy5cwGI/hT25glc46Gfnc9Avnln9V3PN9P8JpzD7RXPGhP+EBTiMiznosv056LpnvnjDfHLQv35noQR5/2c1vsdtTJww7M8RvMXMicMZHJgkvMNGycMDzJwifIKDOINf8S5TpAqr8wmO5XZ+xZ6pwyt8Lm34I9Onky+ncBtLpBc3e3IVD/AKB2UQL4tmDJezQKZwMfNmDheyUVb78ADvMnt2eXA2f+SLOcLfWC5XuI7P5baeafKGD/FFDuREvs/feIcN84X7WT9/uJf1C3hmz0LqxVIPO5dr+BtLFfOeA4s7n4lKhNnYn8P5Nm+xfKlwPbuVDn9hrTLhZ6xV1jP7l7cfX6wgXj5XUX2Zr1L4LlNXDiuwAV/kBG7gPv7Gpo+Eh5js0bAyR/NjZq0ezmKHGuEZlq+pHhxUy7yx4mPhRr74uHhYqa59uYln2LieurAXB/Igf+RV3mHV+uEnTNEgzMMaHMsF/JQNGuoz9/EX5m3k/rFxY3GyDw/yKq8zQZNwKF9spj6s3Fw/mKtlWJQN2I7zuJz7eJIlWpkfvshV3MDfWK+1/LmHv7BDW/1ig3bO5XMd9If1nrUP8z1nvpn2eedxEn9h2i7es1jXcAUHvqD/fOzF8HNmeykszre5kqleDsdz0CvmmZVeDRvyOW7ifv7IB18L32GS18XNkVzBqt3NKxO/KS6O4Him6xFOZo6e4Rzm62VfvstP2LC3uJm6j7nlBF5nqr7OZ6J+3rMbh3M8f7n/+wDPLDIwXMZ2g9SdJ3mXtQaHn7HW/3nmQ0PC99l2aHiCrwwL/2DS4WEy5mZJVuIojuZ8ruYmPjhC/HyHd5lqZFiD4/kps48KZzPZGHPG0fzx/vux7jF/5G8sPM46LuWPbDtefVhjQtiUn/IQs0wMZ7L8JPtzPX9kwcnqyRe5hKv5G7NPDevwOc7mLv7IptPEx8dmuO9sO1N+TDHLHHIsf2HWt8NZfG6285h3TliCC7mKaeaHE9loQXiAaRb6nb3eDa+y3Xvmhyd5l48uC7cyx/KwOJuyF+dwJQ/x6v31HzmXh3iS3daqD3Ov9x3n8xAHbgjvsNsm3/PJzeG3rLNFX9loq3qx6Tb7M8WOsDRf5iCO5Yf8D+8yy2fmlwM/Fy/T7NQf/t+e8L8stTdswTU8wsf3h18wx4GwAedwHwd+5Xw2+NrvzHck7M93eYsFjoWL+fhx57P/N9Yz04lwOjN8a674JLtxKrfyW/7CxN+FI9jhTHiGg75Xf7Y7a77Y4Af58rnzYR/+yOtscMF6Zv4tnMGEv4fD+PJ/zAEb/xEeZIG/1I+JLobDmfxyOIbpr9iXgziF/+Fd1r4a7mSum2FhzuNSprhtjtns3/Brlv2vfbmWO1n0XliFy7mZ/w9QSwMELQAAAAgAAAAhAPfurI///////////wYAFABwcy5ucHkBABAAoLABAAAAAADhagAAAAAAAJzd97/P5f8AfpvMyFZmNskekZk9Q5IipCXJdnA6KLKljKLMZJcIUTLLSFRIaYm0JEk6Rud9vt/v53H3D3z9cr89n6/ndV2Pdf2Il9t3bdexe9o0w9OMLvtIvyF9B5etV7xs/SfqlK1UvOyjAwcPHdz76V4DBz/S7/9737z3U0P6/b/vhzze+5l+/+9zuarV61SrU6l8peKJxf9//slaPkOa//uzirdkCSvzZa5jpWxhG67lJ6ydPazLLdx6430OzyyQM6zJedzEprnDPbwrT/g+C+cPq3E+N/CeAuFHzFfEvpzDPWx4a7iDJW4LlzB7sXAasxUPpzJNyWAs77k9bMWPeIAZSvudz/MAM5QJ2/B5fsLsZcXDe8o5h/kqyJOtKtqfbSrZj8WqhouYUCO8ygo1w85czaPMV8d5rFw3XMcid1nPBTzKJg3C3ex0d/g5mzUM97Jy47AhO7AX13EHj/CHG9838Z6lmtqXTbiM67ibFZqFjbmau1iyebiUD7UIv2OhluGr7NYqfIwn+AtvbxO+wW7tfcfbO/id7TqGn7JdJ88s2DV8hQ/dJz5m7x5WZ1dO40Ye580PhHfwRb7F23uIjy0fFBf381O2fCjswf38hvl7hnPZu1f4I7s+LD527h0eZdo+zuE47mfhvuF8FnkkXMBS/ezLZTzKFo+G+5jv8XAOOzwZHmG3p8ITbPt02J+HeI65BoYzWXFQuIZdB4fHWWFIuJoVhzqXa3iC/YaFZ9ltZNidJ/g1SyeEy5l/lHWcyxOskCge9nvWebx9vDzYnW/wOL9m9efDjez8QtiLR/kD208KD/O2yWFRvs6FfGh6+B0rzBA/28+0H2u/FLbhFn7C/i+H59h2dniI6eaE49l3XniGbV6xHxvOD3ew64LwOEu8FlbmEq5jxyVhD37Gb1jpjXAtOy4PB/EzXmT5N8NV7L9Svuy7Wj4styasx3u5ktv4BR9aF37HdG+F49njbfGz4Lve8xV+w17vhz/woe32Z60Pw81MvzMswYf4HJfwO3bcHX7GtHvCcbxrb1if7/MDdvkoPMa++8IzrH4g3Mh+B8Oz7PhJ+AQ/42/scigcwmO8xC6HwxE8xmRm+SzswUn8hrd8Hr7MvifEz9xfh7PY/2R4juW+DSuzHldyHbfxsVPhLyxxOnyCS/gb+50JzzL3T+Jh77PhAP7I8+z3s/XM+Vs4g2V+D9/kgHPWs/r5sBM38nP2+9P+vOlCOJk5/3Ie8150DmfzPG+5FA7ky7zAvpfDM8x9JZzFGtfDd1kqJWzIZdzBLqnhfTzGL9k5Xdr/8yizpg8LcArnMXvGsDen8UfWzxx+wP5ZwnNMkzOswr4cy7d5hk1yh724mz+weL5wMYvn98xeBX3PzIXCF1jj1vBdNr4t3MXexeTHu4uHH7J+iXAIP+Allrs9XMkMpdWXz3Me7y8TPsav+AtrlRUHN/ND1ipvHw7kZn7FC7y7gvVsUtF67uZX7Fc5PMv77wif4Ff8jQOrOI9Z7zRHHMYpnMbLzFQ1nMgC1dWH8/gLe9XQb95WK6zE17mWOWuHM9i3nvlj1rvkzQGcwrM8z4YNwh3s1VAd+AMvsGKjcA37NbYvn2iinuzX1O9s2jzcw0Ytwp3s21I+vL+VPvGW1uHLfKBteJKPtTPf/IWXeH97cfExfsWz/IUlO4ibS7mHWTqG3TiJJ9ios/xYvIv7zn5dnctc91nHmdzJtN2s4zieZe8H/M4feZa5e4SzWPDh8BWW7Bv251KeY+5HwlKcxWV87FH1Y93Hwq18cIA+81ue4WNPW8/cA8PGnMVdLPmM+rMfl/IEz7LmoHATnxhsXllqiPi4jGdYZGi4gBmGiY/PcxcfGykf9k7QF/Yfpb58cLS6sP8Yv/OJRPGyf5LfWXRcuJCjx4fXWWliuJY9Xwi/Z9lJ8ucTTOAKnuFvvMrsk8MBnMbzrDglXMMyU8M3+cQ0+/Lm6eLki/yeZWeIi0+8aD37zhIvu88Ov2bOOeEM9p3ne+Z4RR85jNP5Cy8z84LwFvbjC3yZZzliYZjMnovkxxyLw7s4ne/z9iXhG3xsqbhYdVnYlO9wD2u+4T7wpuXhZOZ8U33YdIX1rLJafOzPt/k+z7HQ2vBVFngrnMe+G8wPz/A8G2wMt3PEJvVjsc3hIvZ/z3c8x2T23+o9C2wTD3tu1w8+8aH5Yotd4T5W2xNuYM+91rP2R+EW1v7YM3PvC2exx35x8hueY6UD7jNLHwyX88FPxM1v+Rv7f2o/VjzsfrLFEeu5j78xy+fhJPb4QpwsdlQ/WPqYuNiTA7ic3/J7nmf2E2FhTuN8Dvja96x10r3gZv7C/t/Inw+fCk8x+xnnsN9Z95dlfwmHcQUv8+FfreMpnmXZ39SXK/gbh52zH/P/EZbhIM7lm7zI/n+GA3mOF/jYBfXggL/Vj2kuec+xPM9b/wlfY43L4bu8LTl8nbmvyYOlOYCzOJfLeZ43/ReW4xBO5kpeYv2U8APWT/XMImnT/Z8L2Cpd2JoHeJAZ04cTWCpDuIytMoUDOJDDeIDneYGXmTtz+Bhn8RcOzGod78kW9uFHPM0G2e3H7fyFA3KIhw1y+Y7ZbvaeAzmV23mBj+W2P1vnCR/mEB7kKV5ivlvCOcyUL3yCE/kb++UPzzJ7wbA5B3EaP+ZF9imiXsxza1iML3ERS92m33yimHiYs3g4gw+XkCcfLq3uPMXzrFM2fI8ZyoXPs3B56zmf5/lYZfVnmirqwrGcxgE1rOeAmp7Zrpa+81NeYJm64ZtsWC/cwUH11Zt5GoQV+RLXsOHd8uAO/sIWjcwR9/ES2zQJP+GIe8Jk5mwRVuAMrmbZVuEK9m8bnmObdurBT3iBd7XXZw7h+zzFSxzQQb3Zp5N5ZNt7vechnmZi5zCFbbqKm4n8hOeYwvL3OYeDuIqHeJH57w/nstED6sydTGatHuFmNnxQX1mrp99ZqVdYlb24lu/wBzZ82H58orf7yJJ9nMM+XMrNPM38fcMkzmUqez3ifPbvp67s86g68jQv8pHHwp+Y9nHvOY4/ceCT5ouN+juXO3mOg55yHssMcD9Z5mnPLDvQvHPAIPPIsoPlzSe4gj/wN2YfEhbnNC5miRHhEtYbHW7jwDHyZY1EcXEg3+V5XmCrZ8MDrJUUNuYQbuYuXmKtsX7nTePDyWz1nPvCXjzAQ/yBA54XN8/zAhtPEAcbT9RfDuEunuMlDnjBvrxrUlifT/B9fsDfOGCy9Sw6JWzHgVzIT3mBCVPDq0yY5pm3TA9fZv8Z8uDAmfbjgBfFw1YvqS8HzvY9+8+xH5vODfew/jzv+QH3MNsr9uFUnmP1V/3OAdzIczzPyvPDdcywwPkcxOe5hxdv/P5aOJoXeZ3FXg8X8Y6F9mc79uFb3MNPeZrDFoWXWX6ZOnMVL3DQG+LiwDf9ztYrw94cyCE8yB95gZeYZVU4iRVXO5dreJEZ14YTOHCd+vECr/Pmt8K8bMRBfJGzuZMXmevtsA5n8j1mXC8e9n/HnDDbhjAfp3IO223UJ960KZzMxu+Fu1h4a1ic87mYT7zvvjPTh/rEiTzI3jvUgz/y4o33Oz2zz64wkaeZwiG79Zfp94bPceDH1vMCUzhkn/XMcMD9YblD4Uo+cjj8iXU/C9tzKw9z0BfyYe6j6sBZ/JFZjouXk5jC/l+6VzzHy+x9wn4c8FV4nnlP6itncz5LfhMuZa7vwtqcyS2s/334Acv/4F6z0il14FpeZJYfzS0HcRKn8iJrnQ5HcDOT2eyncC8fOatffOjn8Dve9Kv55y2/mRu+zAssfS5czqZ/iIejuYfJvM6c58MZzH9BPziX89npr/Bz5r8Y3s7anMs3uIXDLpkPDrwsHxb5N3yUo7mAP/M6O10VD5tet55DuIc/8xLLpoYrWCVN+v+zHgfxbW7jRVZIG67mkPThJXbKEH7OTJnCiayaJXyHJbKGD3Mgl/AUL3BE9jCZj+YKf+ajN3tm0dzhQt6ZN+zLR7meZ/gzE/OFKSxVICzHZVzJ0oXCDlzOIyxaxO9cyCPMeFvYkX05gZ/xDLMUDZtxECdxLy+yVDFx86biYWcO4GQe5XkOKaHfrFky3MRHSoU/sf/t+sZzTGapMuLnMu5lznJhW/blDB7iGaatIA+O43nmrmwOOIvrWeOO8F0+XiX8lQOrmQteYAqr1wyHcSMvc0At8TChTniVWes6lwM4hb/yPPvWlz8z3h3ezQn8kB0bip+fMYVtG6krszZ2Pgc0MQc8z0ts29R65mgWTmfR5uae+VvIg3N5ho+3VHf+ygus0DpczYFt/M787cK6HMC53MrzrNLBPnybFzigo3vGITzPi7zEep3Ce7mNX7BiZ/twDS+xQlf58db7wtf4+P3qwqQeYSoTH9RfZuoZTmSjh8OdLNMnfJOP9g1/ZtpHw3Ycx0+Z4zH15nRuZbsnfc8iT3nPBfyUZQeEK1jy6XApGw0UP9MNMv8czw+ZbliYl+U5nrO5ik8OD39np5Hh5xwwyn3neV5mxUR9ZbGkcBETx+oPG40Lm7E9B3In9/IwL/C28eEIvs5klnohXMY+k8TB00zhsGnhaF7mdTaeEe5ivplhW87hIRZ/MbydVTmEi/kG3+ElNp8Vfsy2L1vPQ7zEsrPNIx/lCm7lz3xyjnrwd6Zw0Fz3nM3niYe5XtE3Ps4BHMKZPMxfeZ6X2OlVcXM0P+fPvM7sC/SF07iL/V4LzzLf6/rE6gut50buYvNF8mX1xb7j40vFyV95nZ2WiZ+537AvZ/Fj3rs8/IL3vml/fsHrrLYy3MD2q9SZWVaHeViNk/gSN3DIGv1gv3Xqx05vyYeD3jYfLLE+XMIBG91nnucFPv6uurHZJvedd2yWF9/iYQ7a4t7yIi8zYWt4lUW2hQvY70Nx8izPs8+O8DSH7bQ/G+wKG/FeDuN27uQXvMz2u8XBITzMs7zEez+2D7PuMx+cwuscsF8+zHEgnM60B53LcbzEuz4N3+egw+EIXmQy638WfsCEz9WfVY+G77DEMXPAJfyVQ447j5eYzJJfhkuZ9qQ8mPCN8znoW3Gz3nfO4TBu4yVeZr3v/c6sp+zLKbzIxB/DFI44LX5mO6P/nMqzLH9WfFzFS+zws315hMks8ktYjgu4knf9Grbg+9zHquf0i+X+sJ5Png9/501/hpP55F/qwN+ZwqZ/h3tY4FJYhSM4j28zmcP+0R+WuR6+ycf+C3/hsBTfc/T/3BcWTA1fYfU08ReFN7JV2vAAu6YLj/PJ9OHv7JopHMjjvMBcmcOirMshnMmF3MpLfPwm5/JX/s4MWcNhfJ6XmSNbmI9Pcjrn8HdWyh525loeZYGc8mFNzuNCbrrxXS714UJeYM2bw0RuYgpvzx2OYCLfYDJT+HgeeXIEf+XvTGbzW8KPOSyv+vGRfOFo/sTrrFUg3MxMhcKnOJF/cFhh+7NlEfXhCO7nBSbf+P1Wz2xaNNzDksWcz6X8g08WVwfeVVJcfJ+X2aV0eIxPlrGe2cqGU9myXJjE/Uxl0wri5VMVxceylZzLFTzGuyuHHfkhP2PLO5zLvFXcOyZwNrfyKgvc6RwmcB4/5FU+VjX8hSOq6Q+7VBc/89ZQH85mKlvUDvdxRB37sXA9cXE+r7L0XeFyFqsfLmKXu8XDkg2dw6VMZqbG6s2J/IyPNTM//IWX+eQ95oUJzcXL2q3CLazSOryTI/g21zOZT7U1Pxzdzv1kgQ7hEM7jJaa7NxzPqp3ty3f4Byt2ce/ZhElcw83czVTe2jV8jVXukx8T7lcfdnxAvVmlh7rwbSazyoPqwLd5nU16iotdeusTE3iMv/MqM/YRFyfwM5Z41Plcwusc8rj686kn1Jdtngw/YZX+8mD2p8JCfIzT+Cp/YYkB7jO7cBCX8EMe40XmeFp9OJ2/M2Gg+vC2Z8JqfJ0b2HSI/biHv3PYUPeHhYeFrTifBzhsuHngZSZz2EjvmT9B3TiXvzDtqHAcK4wOG3I1d3BQonqx9bPhQbYar048wKscNkF8HDFR/GzwQridhSeZE87nJ0ya4r4xzdRwLItPEzcX8yKrzTSn3MBL7PpieJzpZ4XPMftL8uU0XmWOl+XF6dzOPLPDl1hhjnnhav7OhLn2Z8J8z8z0WpiLEzmTrV43DxzEA/yFF1l+YbiKTyxyj/kb/7jxfrFn3rvUOn7B35j1jbAWO3EQEziFm/k5L/Iqmy0P9/KuN8P3WX9F+AGLrTQnXMTFfMI/LPQbH10d/sw2a80jS6wLK3MJ1/Gpt9wD/sHLfOpt7zlivfvBYRu852Um33i/0TMffVe8rL9JXfkBr7LGe2EdJvJdvscUNt0atuYeHuToD+zD60xh6+3i4UFeZYMPfcftPMiEHb5nuV3uM1fy0o33uz2z4N6wJV/hfjbaF+7k6P3yYIZD4fNMOCIeJnzmmY2O248VvwzXMNcJc8CZ/INPfm1f/s6rbHAyTOJ2pvKeb8OPeNcp94OtT6svS/xkrriEySx+1n1hzp/dK87gIo7+Vf2Y9Jv4OPp3v7PxH+EulrwQLuWQv/STiRfNFzNdCiey6j/hO2x82bncxes33id7Ztor6sZ72ZUjOI7v8wseZ/KNfa6GT/N9/snG18XF1iny5kFeYp3/hfdxBN/jl0zm06nO49MZMsYzm2UKE7mXKRx9U3idT2QLf+NtOcKn+Tr/ZNlc4Qo2vDlM4A5e5R15nMu3eJ1P32J/DstrP17mVWbJH07iUwXCP1izUJjETUzliMJhMu8o5ly+xcvMXTwszllczBIl1ZtLmMI2pcJPOOJ2+TGZV5mzdDiDncuFR5m9ou84jNM4g5f5RCV15m/8k4mVxcvRd+oPE6v6nTdVCyezc3X58CivMk2NcCzz1AxfYsNa6sIdTGalOuFapqmnzxzLTSxVP1zGexqEH7F0w3A5RzeSL8s3lg9X8SqLNwn7M4GLeY5X2b1Z+DWz3xOWYWVO45tcx1zNw25M4kyeYCo7t3Quj/JrZm0dTmGxturMIVzEtbzEZ9qFfzFLe3XgJF5l9w7OZ/qO8uJznMnu9/qeWbrYl2W7hitY7b5wA4t2Cxcyd3fzxVlMZroe4XhmfzC8m/2ZyGn8kOeYcmPdI35ntkfNGafyOrM/7ndO43U+9WT4B2/rHzbl43yar3MPf+WfrPVUuJl5B4Sz+fTTvufogeJh2WfMIVcwlXUGWcf3eJ1PD7Y/hw0NL7PscPtzBVP5zAjzx2wJYQ524VRO5zHWHhVuYY4x4uV0XmeBRH3gPP7B1s+GB3lnUrieT431PW8eH5ZjF77IlTzGss+pAws+H1ZmLbbmK1zHzTzIzBPCF3jvC+EXHDZJP9h6su94kF8wYYr7watMYdap+skpTGXrafZl/xfdK3abFZ5g5Zfkx0Ivh235Kg8xYba4+PQ8efJPXmaeV8KXWOlV67iWfzLHInPD/ovFz/5LzAHP8Q8WXh6O4HwmM/FN9WSuteFMVnkrfJtl3/aeiVzBt5nC+zepE7/iVVZ/Tx7cyHOsvtV79n/f77zpg3AyR293v3j7h2FLvsH97LLTfeDoXdYz427ncgLPsdqecAMTPpIfK34cruFT+/WH6Q+Ez/GWg+HLrPpJ+A4TD6svMx4JmzOJE/gxU1nks3ABS34RLmXCUfGz/3H5suwJ7/k0V/Ac/2TWr9xLjv5afVnuZLiSBb8xr0zgK0zmVZb+PlzOYT9Yz8tM5q0/hq+x9k/hFpY4Gy5hvl/COWz/u7x5mH8y8zl58gVeZ6ML8mISd/IqU1noL/VmEl/lOaYy+8VwGrP9EzbhVO5m9n/DFpzGfUxIFheTrjqPo6/Jj09fVw+2+y/8lGlS9IFPM4Fj+Rr/5FUW/J9z+Qqvs12aTHEeR6cLr7NxhnAXs2cMp7FxJuu4i9eZmDlMYeMsvmORbGFtLuAWNs7ue96WI3yduXOGrTiMs3iAlzk6l/iYmFt8LJ8nXMUOt4RHWCtvuJnt8tmXn/I6OxTwO4/wOvMXDOeyXKFwJRMKh1eZcKtnji5qP1YqEdbkWm5iQknrWalU2J1r+TWzlQ5rMIlT+S5TmVhGPdmyXLifNSpYxxoVPfOWyvbhy0xhwyphI+7gTta7M9zG0dXVhxlqhM+zUc3wGe7kX+xQW99YpE74NBfwT95UN6zGydxw4309z3zmLucxXf0wB/NwPKfzJd7XQJ78ktdZ6m7nsgGXcQO386aG8mYCJ/MIr7JtY/PDQ/yao5uJh4n36CdbNbeeB/g1n2mtPry5TTiAL/I8E9vbn6062Jc1Ooqb7/IQ83UKu3AOjzHXvWEFzuRq3tdZH5ixa5iFDdiYEziJ27mLdbuFW1nqfv3kMm5gUnf3kaMfUH+WeEi9+TSX8Gv+eeO7Xt4z4WH9Z/ve4WF27+N7jugbJnPEI55Ztp96cwWP8aFHw+948+P6xxd5iHc8IQ4m8C1+zau8vX/4Bhs9Fe7kbQPMG4fxdZ7nZWYZGNZmMw7gJG7hXp5n4UHmnvP5F5sN9jv38i+WGRo+wDd5koWH25fNRtqH9yXYn1/yL+Yepb6cxa+ZeXT4AluN0XceYDIrjHV/+MBz4mXSBPPMhCn6x2wzwqksNEu9+SrP85bZYSJfZgrbzzHPTJjnPJZ/JVzFjK+qGyfwSxae7zzOZwrLLjD3LPpaWJMLuYmDXw//5j0L3XN+xOscsEj+TFqsnsy8RL/Ycmn4NPfzT6ZZFo5lhjfUic/zKnO9qf+cyWRWWiEPruVHvG+VenLEGvkymdeZb21Ymg9wDpfzJBPX6Qfzv6UPbMkHOJcLuZ8nWWx9uIgFNobzOGCTfjD9lvA5Jm4TD4u/r+9M4mL+zVRm/SCcwgd2OJcneZ4jdqojE3c5n4m7PbPpnnAPS34ULuXNn4QvstlhcTCJe3mSqcz1hTlhzqNhQw7mDO7g32xzTNz8hCmsflw83MiTLPilcziar/BvXmeTE+FuZvrKvedEzmfFr8M6XMP32OQb+zHxB/Ez/xlzyDvPhus5+Gdx8vZfwmf4Bv/iiN/0n6V+t47L+AZbnQu78QBPsOqfvuM7PMBnLjif3S5Zz/SX3Qfe+a+54XqmsluyfXmCf3H0Ff1iq6vmhIk8wE+YwrzXwrocwNncyvMsez1cwcEp4uffTGXWNJn/zykskzZ8k23ShZ8wR/pwOitnDNexeKZwMBfzbxbOHHbiQ5zPz/kd62QJ32OabOFY3pkjXM96OcMkbmMqE3KFV5nz5nAGq+UWFzfwO3bIEx5ht3zhMxzME/yLf7NbAb/zmYK+48BC4QUmFglTmOXWcBIfuk18HFw0HM2/eZ0lSviOS/g3R5f0PauVUge2Lh2O4EEms1sZ+bFbWXXnCV5lmQrmjf0rheeYdIf+8fYq4Rt8prr6sXYt53ELr7J0nXA529cND7P93WEPHuY3rNZI3bmB3zFd47AcazKR47mSm5jC+k3CRvyAO9mmqfvI6s3Cjby1efgaa7UIN/O2VmFTDuTr3MMLvLtNOJQf8h/e1VZcfIaD+T4/4V/8m7nahTOZ0EG/2LSjuHjfveGXzNk5bMgZ3MHBXZzHgV3lx6b36Tv38DDzdwvnsml333HoA+rBxAf1k888JG8O7eV7Pvhw+C1H93b/2LCP+JjIHTzMFDZ4NNzOZx5zPp953DNHP+E8pn/Sez7H62zQX125nX+z2FP6xEX8koMH+J7NB4Yfs+Dg8BVWHeJcvsPtTDfUvWKPYerMb/gPk4aHqew4IvyMDUbqC7fzWw5OED9zjAqns2RieD+TuJRfMZUdng2PsMM463iEqTfej/fM9M/pEzNNCCey4AvqyeyTzAGn8S8+OFW+bDRNPhzInfyKF3jbdHHydaby/hn24Ve8wB4v6hvbzxYvD/MvFp4XVuR8rmGLV8IRTOA+JvMq874azubABeJj+tfUmyUWhkv44Bvqx/uX24cj+BUvMJkt3jRn3Me/2WOlc/gNv2W21eFUtlhrHz64zvfM/pY5YPX14UYmvKM+TLPBnHEox3Ii/2Hed+3L2dzIFlvFxX38liW2qS+zfSA/1t5u3riFqTfef+iZLXY4h0m7/c7Be7zn30zlg3vFxxYH7cfch8JZrPRpuJY3H1E3vsh/mOGLsAJr8Hmu5ru882g4mut5nZmP6SNf4FUmfSk/1joRbma6r8OaHM9NTHsy7MxxPMrEb8MUZv1e3ziFW9jpB+v4OVPY7FS4l3lOe8+XuJeDf9JH1vs53MZ6v3pm2t/lwdvOha/zmT/DvzjwsnvMLP+Gk5gm2f3g0Cv6zXuuiZMf8W9Wvx7ey438gsVTwsXMmCH+o+QkTmAqB2cM/2a6TOFtHM/XmfWm8B5O4UccmC28wKHZw384Omd4nYNvdj4z5g4fYhIn8DumclCe8CLb5fWen/IiB+YTHxMLhimsXChcx7JFwhWsf6t9+AEvsE5R9eF7/IiJxZ3H4SXDf9m6VHiQSWXky+xlw9ycxlksWz5syxU8xDwVwpc4uqJ+ML3/aHson+M/zH5H2ITTuJt5q1jP2fyHbe/UX47mIX7H6+xUNfycidXUjw2qi4dDuZ27+Q9z36VeTKxvPzZtEO5hmob25Vj+w6ZNfc+7m5s/fsiLLNciXMmEVvrMq0y98b6NZz7UTp3YsUP4GRt2DHcwXSf3mHW7yJtbmcJW94UH2LCb/Vj3/rARt3InC3cP57P+A+rBD3iRTXrKi7v5HSv38h3XcTezPBxOYpXe9uPb/I4Z+vidg/k8v+PfbPKI85ixX1ibE7iFRR8NF7LRY+rDTI+HhTiRr/KOJ8O3OLq/+8ChT5k/Nhhg3pn/aflxLr9juoF+53h+x7ufMbesNijcwOKDw8WsNsTvLDJUv7mAF/nAsPAk04wIq3Is32HjkeEulktwf1g0Ub059Fn1YcukcD8LjfWeSXyV+5nKBs/pA7fzOtNOCMex40T15Wf8juUmiZ83Tw5f5P1Twq9470x58Qv+wwwvWsfn+RVvneUecyhf4w7+w3Qvmws2nR3u4eh58me7V/SVg/kpT/Jv9n41HM4f+S9bztcH3rtA3sz4mnvHbIucz6k8yeqL9YODuJHf8SLvXeI7fsGLzLpcXTmFrzHHm/LgdO5noxVhAnfyKm9ZGd7Bl/kW710lLpZcF9bkUm7i8LfUlw+8rT5stF4czL0hnMWeG9WD3/MiR28xD7zjPfFy6DbzxZveFwcn8yTTbncux/F7ltgRLmGTXeFuDt8tXxbeoz6cz00st9d9ZMaP9ZkTOIU191vPiofCRK5hCut/qg78gP+w8eFwF+sdCbfx1qNhUb7GhWx8zHrWOB6+y9tOhK8z+0l9Zi1O4yxu5uhv9JeJ38mPZX8IV/ChU+4Ve/5ofjia3/MirzPtaX1nujPheJY6615zGTeyzC9hEt9kKgv/Ki7O5/es+lv4Dguf8x0HnRc3q/0ZbmCGC2Ex1mUSn+cibmUqS150PziaS3mS1znoknhY8EpYnG35ChfzECtdDRtwLbez0DX15nC+yo38lxmvuzfMmfamiIMzeIh50ocvsWeGcDi/578clDFM4kWm3nifyTNbZg4Hcz//Zp0sYSLfYwqr3RRuYPqcYXE+x8UcfrN42SBPuJ11C4RbmVjQeUy81TOb3xZ+zApFw9XMWNI6TmAKh5YK/2Hz2+3HxNK+Z75yYRHO4QJ2qRAeY8aKzmXOyvblDKbw1mr2YSJf4zGmsGatcBNb1QkPcGhd+fHm+uGLzNUwrMSZXMtBTcwRGzT1ntt5kRWb+Y5ruJ3t7gk/ZY8W4TdMbCU/NmgddmAit/MIU1i2TbiCPdrZn4Pai5fZO4TTmKajcziWKcx/r305l98wXeewBZM4nvuYymJd3SMu4gb2uN/+rN89/IBdHwiPs9RDYXku4ypW7Bl25hoeZZbeYVsO5yQe4r9s0cd+HM59PMp/WePJ8F127S9+lh9gbrmK/zDpafVjiYHhEvYYpF4cPtj5TDdUP5hhmDj4PI9z0HDzwuEj7MeuI8XF40xltlHhVJYZ7Xu+yVQ2HBO25w4eZsVE53INj994n+SZw8eKl5nHhS8w0/gwF2txImdyMxtNDHey5AthaS7lcrafHPbiYf7A7FPCopzGhbxnqrz4EY8zzTT3lF2n+535Z5gL9mYi5/Ib/sgUpn0pbM5x/Ji5XlYnZpodVmYiJ3IdU5hvTjiH7eeqE3u9ol4sM9/cMM1r8ufw1/Wb5ReGD3IVv2WRxeGdXMD1LL3MvHI5U5nljXAS0ywP7+ZYfsgOK9WFR5hy4/0qz2yzOvyEhdaEt/JVvsaya8MqvJcr+Da/YJ51+stEvsSPmXLj97c8s9h65zCJi/gFU3nHO+ETfIu/scuG8BiTNlrPZlvCvUzcJh42/CDcwbt2hEP4Pi+x107zxaG7w39YZ0/Ygl34Hvfx2I3vPvKeOT8OZ7DcgXAlWx8MR/Igr7DIIXPJmp/Km0ncxBSmsvzn8uQq/sA7vtAH3nHUMxseUz/u4CVmPW7uOIUrWPjLcD6bnlBH7uExVv7aPHIdP+YDJ8OTfOBbzyzzXViRo/km1/A6H/jBehY7FQ7nIv7LpqfVk3v4AzOcCfOyKBvzec7mQu7ig2fDb9n4Z7+z8S+e+eCv4uW3/JeVz4WduI6fM90f4XgOP289a/5pvnjrhbAaX+MGPvN3+BeLXwoXs9A/6sdX+QOH/uv+8cFk88dveYllrugvh161nvWvhR8w83Xv+QI/YJcU94vHmMIs/wsnsUuq75iUJuv/mcqR6cMrbJQh3MnOGcOhPMp/2DqT73iQR5nhpjA7O/N+Ps9pPMqv2CBr2IXbeYyNsomXtbKHm1kqh+/4IJdxJ79lUi714ZC84SU2yxfuZYH84Tz2KhD+wF4FPbNikXANH74tHM5T/JcVioarmaV4OIk5SoSJnM4UJt0uHyaW9ju7lFFPpq0QjmPaSmE6juN4Nq0iT+7hD8x0p/nhRP7DjFXDCaxWLdzA3DXCSnyYs7iWp/hgTf1k8drhYtaoF77LLHeF9diEk7iNu5mvQTiHN98dvsjcjezDWdzNbI39zqnczZFN3EfWbiZvJnILTzGF1ZuHG5muRZiL4zmTdVuFW9mmbfgJh7czj8zR3pyxYgfzzFId9YHL+C1LdJIHlzCFt96rDnyNVziks7nmJaayTRfzxk/4D4d3lQ8z3+c9X+C/fLC7+JnmwXAsK/YUD9cwldUf1g/W7BNuYoVH3G+2ekx9eIApzPa4+nAqrzDrE+EU3tk/XM9iT4WLmPUZ88UpPMXig8LSbMjFXM4drDtcn7iVl1h3pPfMlWge2frZ8CBbJ3lm1/HhcWaYED7PApOcz+Gcx0v8ly0nh/s5dKp5YfFp8mXZGeEKPjQz/I7ZZ6k7p3E9070kLo7nv3xotv2YaU44kZnmho04kTv50KvWM/tCcbDDkvAIH1rueyauMG9stUo9eYCX2Gd1eJpJa8w/H1prfzZcb1445B37sdKGcC1HbjTfLLBJP5ljczidnbaEn7PWe+FmPrRVnPyOqWy1TZ5M+tDvzLPHHPIlHmervX5nEg/wOFNZ9aOwL4fzHZ7hvxz2sbrwMq+w1L5wGUt+EtbmUm5huk/NIfMdDuew92fhj6z9uTy5hQdY+ItwPnMfDe/mLH7Imsfsy03cwru/DNvzQx5msa/CRaz1tT5zM79jt5Py4gn+yDTf6QfH8gxr/aCv3MzUG+9/9Mx0p/WL43mFJX/SVy7lZfb9zfnM9bu54SjO5Du8xjLnwjd55wX3jut5iTdfDF9k6b/Fx+W8zKRL8mXpy2FjDuNy7uLlG98l24e7mMq2V/SNh3iCda7ah335HnfxDMteC1cw03Xr2YLdOJG7uI8nmDtFnpzFy2ycJlvsx108wZvShpPZLV04kid4hT3Th98zY+ZwAptkDXezZzbfs2j2cCEr5ggrcQ3XslsucbD4zWEpLuYyps8dDudz/JdV8oT3sCPf5kf8jJXzioNJXMcTTGWmAmFdTuRW5i4sb47kLK7hFaYtIh+O479sd2v4KQvfFs5n3aLOZ/li4SqmKx5W43huYOYS8ucLPMGWpcL9HHx7+Ddzl5Yfi5UNhzGJi3iZqWxWPtzLqhW85zvcy5KVwqVsVtnvbH6HevJj/ss7q4TrWahaWJ/N+So/4MdsV0M/2LdmeIZN6sqfu3mZXe4Kj7Faff3ggw3Cb3lLw3AUX+Y1NmgSbmdSM/Vl0j2eWay5ueQiXuGdrd0fPsj1/Ijf3viujffs2Tb8nvU6hNuYtqM5Z+VO7htz3hu25QweYuEuYW/O548s2VVcXMqPmO0++XMqr7BnN/vze/544/39ntm2u/hYoYf9uZpXmP+hcC4r9XQu1/J7juxlPSs+LD/25Bp+xO95X+/wS1bs43um6RuOZcVH/M6mj4Z7WPgx+3I+v2S5x8NabMmV3Mz9bPFEuI+VnlQHth2gvhz2tPvETM+EE5ltkL7y9sHhcL7Bf5l5SPgCew5VP44crv7sNCLszM95lDlGhtM5MsF6lh+lflzFLzlytO85coxntkhUL+Z7Vr05h5tZJsk6tmUnvsl9PMTPb6wb63cOG6/+vG+iuvFLfs8cL4RdOZ3HmX2S9RzGafyel5lvsjxZcEr4CptMDXez4jTzzOwz7M88M8OyTOJLXMFUZn0xzMUpnMkms8JmHMnd3MsrHDZHfqw613r25Dvcy++Zcb4+cCQn8EteYbrXwoYczx2suEi92GdxeJq9l4Q/Ms3SsALHcjXTL3P/+Bz/Zc835MOSy8OlzPtmWIKzuYTNV4Qfs8CacB77vKUuPM0rrP62OLiR33P4evEy6zvqwCn8kcM3+p4F3hUP82wyXxy2Rf9Z+L3wTs7net63XV856sPwGnvuEj8z7ZYnJ3Ijc30kTs7kPBbeF97N+fyQdT4J32O7z9Wdn/L0jfdfiJuneY19jnvPuifCrUz62n1k3W/UkVt5maO+tT+H/6A/rHAqrMTVXMtaP9qHm3mNlX4KuzKJa3mcqex4NvyMxX8NS3Exl7HYb+Eilvk9fJN9/lAvjjxvvnnTxbAIJ3MBy1yyH/tcth/7/GtfnuYVlr4aLmfNa/rBJG7iZaYyXUo4no3+F+5kpVT1ZO002aOO3MLPWC9DuI03Zw5fZOGsYSlW4Hwu42qWzxau4qjs4TWOyuGZN+dxHjPfEr7AxvnCXUyXPxzPogXChSxUMHyV3QqFJ3h74bAO3+B7bHBr2JbbeejG+9s8c1RR+XBY8fAyM5cIs/IFTmGO28PpTFspHMfKlcP6XMcPmP/OcC57VQt/YK/qntmpTvg5b6obTmb5+vrHYQ3kw/sbhX34FU8ze2Nzw2mcz9uahKU5jK9zOS/f+K6pPrIXl/MEf2DuZuEspr0nzMtKHMfZXMuKrZ3HNTzBim30hWu4jiXbhWWYxKV8k6m8ub17xRe5mqM6mC/ef686cBS/4g+8xpL3hXdyKdezV3frWOyBcBEL9pAHX2EqCz7oPbs9pF4s0tN9Yy8u4Hv8gSV6h0uYvq/z+RxT2a2f85j5UfeMnR8L+/Eoz7L44+Fidn7Kdyz8dNiGozifn/AaiwyUHzs/Yz9WGmTe2G2wfXmC15h9qHliL07jm/yBnYc5j0nD1Yt3jAzfYucx4QgeZTK7JYqLNZLCd1llrPrwbX7CUePkwzbj/c5uz9mf6Z8Pu/M5fs00E8KxHDkxvMLML+g/808O57LOVHPHvtPMB8/wLLtNFx/zzgpnc+RL8uQVXuPtL4dvMMOc8Hl2m6tPPMFU3vtq+AVvXmAu+SLns9xrYVOu5B7mWOx3Tuce3rU0fJ/l3rAPcy8Pa3MWtzDn6rAee3EGt/EH5ltjP7bnHG7hYRZYGxblPC5k2nVhefbiOK7iD+z3tn5z5Hp9ZIYNYVk+zxXsvtF88t4t+sVW74UP8wBPserWsDXf4UH22yY+5nxfPVlye1iJS7mWDT60D7fzLLvv8ju/5lm22u2ecBQP8Ayv3fhuj/fsvs++rLI/fJutD6kbD/ILjjgcJrP7Efsx42dhEU7gAjb8ItzBDkfDI8x1LCzIyuzAmXyF63iE9xwPP2LFL8M1LHcibM7WXMmPeZBNvgp3s9hJc8lRXMQrvMaG3/qeO3iF5b8LK3AVV7PDqXAox/AI/+F/zPpjOIUjTodJTGYqO5y1H1v+HO5n/l/Cubzv1/BLlv9NPbmKH7P8Oe/58AX3jS3/ch6rXww3suUldeZ+XmP2q/rCaVzErNfVg23+F37CUWlzxH7Mmi6cwlrpw80skCGcx1oZ/c7MmcJRfIHXODJzeIUtsoT7WOim8FW2yRo+zE946sb7bJ45Mrv9WThHeAfbcyTn8y0e5hV2zhUeZc+bw+95Ux55cDL3Mect4QzmyhvOZNZ86s0m+cO+3M0zfLiQfPlwYc/MUCR8hM/zJ955a7ie9xUNv+SIYmEyx5QI/2OekuFL7FtWfMxXTp04h9/zlgri4Ai+zPVMZqmK8uEIjuIy/sRkXmOeSuETfIm/cdQdvmf6KuFzvKdq+BHTVZM3x/MMH66lH8xRO7yf0/kVc9TxnnnqipN17wq3Mkf9sCCn8xUWaiAOvspTzNnQ75zBU6zVSL25mT/xnsby4Uf8it2aqCtP8BoLNXMO2/NVbuZhFmseluPDXMSVPMVGLcOdbNUqPMBHWsuHI9qZG45qL15m62CumcSpnMNU3t9RPZimk7lnV47lyzzO2l3CLazRLXyXle8P17FZ93Av73/A+RzRQ36868HwfdZ8yDpu4ld8uKf68pE+6sebHwlfZLN+4mGxR8MSbMYRXMQl3Mtkjnky/I+3Pq0PfI2prDQwXMs8g9wnNh4sL+7iKaYb4l6z1FD14TJ+xSzDw+bsy0n8mGc4YqT8mDshnMWio8KFLDFafXjLGHPDRxL1g1WTwneYeby+8wW+z1ufU0feOSFczxEviJejJrkfbDE53MeHp6gnS0/zniO4nKeYzEYz7ctW3Ml9PMCML4bl2JATuJI7bvz+sni4g6fYYK66cjt/4ph55pE3vaIunMxrbDVfvKy3INzGRq/Jj60W+p63L/Ke97Iv3+ABfsEzrLg4XMNGS5zHDsv0hUeYzL7L7cfMb5ojdlitnjzCU2z+tnvAu94xfyyyIVzAUe+qH7NuCqfwkff0g1m26isncQfzbZMH5/AMG7wvD47kdh7hFXb7IDzBRz4UD3PtDPOyDmdyNt/jvbvUj1/wFO/d7T2z7bE/p3I2R+1TP47Zb17Z4VN5sfDhsCjncyGf+TL8izd9I29O5k9s9234KZt8H+7mnT+G61nqp7A0H+EyLudPrPRLuJZNfw33cMRv6sBkXmPV333HUXyHybzGLOfMFR/5Qzwc8ad1LHxBnJzPtcz/V1iTdTiXm/ges18M23MaD994f9kzCyeLk/P5E8dcNw9s+J/7wpFpcv6fV9g2bdieh3iYNdKF77Jh+nAHy2QIH+Gb/IkjMobJ7J4p/JrpM4fPMXuWcBoLZw3nM1O2cCLTZRc/x/Mwb8oRTmbbXPJk2tzi4gMcxXH8mid5jWPyhP9xxC3yZe8C4Y8sXDCswN6cz9X8kW0LiZfpCsuTWYqFk5j+dvVk59LhUWYvI06O4TQm8z9mLisOjuILPMRrzFs5nM0Od4RHmLFKOIF571R3zuZJZqkR5uCtLMdKrMVJnM7XuJJruZkj6sib5euKm6t4hAXrha8wb31xM9fd4Uw2bRTuYYfG9mORJuECjmmm/szdPGzHWfyUt7VQB7bj61zLT3lbS78zYyvrOYFrWa+NOeU2/siqncJ32KyzOLiXn7JSF/sz3/3hHI7qbp5Yu4f92Jtb+Cl/5GMPhr+wzkPW8z1+yvI99Zn9eoVnWbuP89juEevZtV94nBUfC9fwscfFw1xPOpe1OZOruIU1+4ebeMtT4cscNUC92P3p8GveOtA9YK1nzD+bDAp3s+tg+TDvEPPNZkP1ld2HyZNf8xf2GBF+w4Kj3B/mG+2ecBTncA+vsd4Yc8h6ic7hNn7DMUnuE8eM9cwq48TPt/kLe483X8zzfPgS004IM3Acn2f2ieEITmMyK00K72I/ruX7PMt8k8PbOYdv8J4p6siPeI2tp4UHOWaGerDibH3lGn7Ne+bZl2NesZ75FljPUZzDr3mNlReGbbmOh1hqcdiRy/gZ+ywNT7PRMn3jTv7Ix97QX+ZfLg/O5X+8ZUVYgHfyZc7jeo5aJT+2XScfjtlsf1bbEm5gjm3hdFZ5P2zPLnybh3mMmbaHE9nvQ/PDO3aGb7HIrnABx3wsPqbZFzbgWG5n1QPhO6x60DPLHwobcxV38f5Pw69Y5bA4+Db/Y5Uj3rPfF/Jjv2OeOea49az1pTngcG7mL/yXJU6ES1jxK/eCbb+WB/vwEL/iaWb4Jnye1b8NN7L+d+EHrPS990zgWn7Aq+zzg/OY51T4EiucDlfz1jPha2z9U3iQY86qH+//WV5s/qs68mP+y5Z/hvvZ9UJ4nIP/Cv9mzsvhDOb9N5zNgslhQ77CHWx3JfyUJa77nku4g7X+038WTZcr9uFCfsqcGcIqnMG3mSdT+BLvzhp+yLTZwiKsynFcwHfYNXvYh8d5mgVyhPdzHr+68T6nZ/bJFfbjaZ7lwzf7nad4lpVyh2tZ45bwXabLG45n/nzhA5zLk6ybP9zKZwuE/2OFguFq9ikkbiYUDq+yQpGwFR/lah7gzze+u02+PMCz7FjMufyMp9mquO/4KA/wM/7MfqXsz8K36zPn8zhvKRu+zOrlwo28qUI4mWOqhP+x453iYNXqYT2+w2033tfwzFE17ctr/I+5a5tntuXDnMUFPMRTrFfHecxbN5zNcvXClax7l75zK6+yUn1xcAzX8hD/Y/kG4Som3G0/PtDQPjzJ/9ixsfqyY1PP7NPMvLDFPeE+Zm4evsDqLfWXD7RyLvu1MT9M1y68iw9wPN/nSbbo6Hzm7aLeLHRf+CrH3C9fZuyuD5zArWzyYLibt/T0ni9zN/v1kg/TPxK24XP8hPX6mRfmfixsyVnczypPhG+z5ZPmmvt5jQn99Z8tn/Idmz/tPT/mflYabN5YdkhYk925gpv4NXMMC5tyOvew/hi/8wF+wD08ybSJYQ225ji+y4Psk2RemTBWPZh1XDiFhZ4zLyw7QX4sN9X9Zblp5pAruY+lXvQ7E7iM+3iVt83yHV/nVbaeLT+WmBMuYYN51nM7r7L1K9az1fzwABNe8z3zLgtr8VGO4Wxu5s/8j8XeCBex+pvhRnZaEX7OVqvEw1arPTPvWnPD2fyA5d8KG3MVd7H2237nFu5iq/XqwAO8yjEb5MdSG63jMh5gw03hDt6xOazGt7iBz24J/8dCW80jb9vmO77O/7Hadvux0w71ZcJO+bDULnEz8+7wBY7ZJ1+2PmB+2PJQuJ9jDvueYz7zzKafu9e84wt1YMNj6sUSJ803630fbmPjU+Ew7uJldvkxPMbsp8NpvPWX8DVm+9U6TuUx1vktfI/P/q7+rHdOfOzyp/XM93c4hwmX9IMl/pUvH012r1j1SvgOs10VJ7tccx6rXvc9c6aEM9jsf+Fedkh/8/95hB0yemamTGFhTuR8Vsoc1uRabuLNWcKibMQufJELuZPHOCp7eI0dcoTP8gj/x4Sc4VVmyhXW40RuY9E8zuc9ecNH+RF/ZsX84RpWKxBuYLWCnlmhkLy5mjvZsbB68DMeY4Yi4fN8tJh4WK5U+DhX8lemLxc+x2Llw0VMXzGswef4LotWUh9mrBJm5QROYcU71YcJ1fWDt9cI32COmmFfJnA6z/Aqq9VSX95SO3yZueuGs5iuXtieCRzPw7zKZ+8yT2xWP9zL6g3CHhzDjfyG/7HI3erIBVzIHo2sY4Em4Z1sz3lcz8PM0NR8sG8LdeNtLeXN13mVt7RXPz7awXyxaCfx8tl71Ye3d9FP3tw17MoXeZyVuoVrWfD+8EG+wm/57APOY8Ue5ou3P6gPfIP/MeEh+bJTT/GxLz/ncZ5h/V7hByz2sPvDHr31j2kfD8ex0JPhq+zx/9RM12FAVXcDgOmQ7ka6U6S7QbpVECSkEd1cKJNGQLq7kVRAQpDulBAQJUQFpBHd3PZtKH7Ps9/rX+9z7z33nF+dAdaz80DPzPG6ODmPn7PsH8KN7PBHcTPvW2ErLuZZZvxTWI3TuZNp/6IfnMxrrPPXcD+TvaOOfI83WeZv4QY+O1T92HBYeISthqsLz/Iqa4zQD749Un9ZY5TvzDzGvpzJs0wy1v1hw3Hy4RHeZMb3wwqczi1MNSmcwKRTwtwcw4VMNy2cwp4zzDNv8hdmnGk/TudCVp0dfsqqczyz4rxwG99eoH6sttA88O1FvrPE0nAdc60IG3ABD7PzSv3jK6vda37Npyy/JmzEj3mUFT5Ub7b/KOzHi7zHRhucx6N8yhIb5cO8H7s37LfFfmy/1Tlsv02e7MOLvMo7bP+J7+y1PbzFqjv0i51324cN95hLNtorL/bbJ1723O8c3uQt1joQ7mWGg+E0FjsSrmHlY+Lidl5l8eO+cy2vsuvJ8Dr7nRYvG58JjzHjWXPOt8+ZR5a9GG5kni/ky0W8xVSX3E9m/TIswuKczVVcywxfuZ+cxl9Y+3K4j8WuqB/7XZUvO14LLzHl9XA8k34bjmH178K23MXzv7+/4Zk5b5p/zudTZrgVDuE0PmHb7+3HErflz3X8hSXueM+Wd8MzzHw/nMkWD8wph/I0b/Ipsz0M63EOD7LWI/PMFD+E+TiOS9n2sfyY9Wf9Z/p/hVNZ79/O4/P/Cbey7n/DA+z3i7x4j0/5ytPwa+b9TT+4mE+YOkGG/zmRLZKEfXiad1gwabiCmVOEM/m3Z8JfODRV+JT90oT3mCt9mI99uIBLeYeZMziP7TKFHTmUF3iJT5kxczidVbOGn7J2tnAfm+YITzB77nAu6+QJ9zPRs+FINs0nLp7gUyYuEOblKC5m/YL24SGeYJ9C6s6hvMNf+JQJCoflOJybOLSI9axSNNzBbMX1kXN4j31KOJ+NSoa9eJS3mL2U/rMi53Ipt7FdGf1l9nLWs2x5eXAjn7LXc85n4UrhB6xaxTwwc1X7ciY38uXq4RX+raa6M22tcDKH1A6fMHG9MBlH8T1WbiAfbudTVm7oPWs2CutyDw+wQGPf2YTLeYDHmaZJOImdmoVfsm5z/zFlC3GxF8dzO28xfUvzz6lczGSt5M/3+JQ9WjuHN3iLhdvoL+u3dY/YpJ182bh9eIw5OjiHfTiPN3iHrTqGZ1mlk/vDXi+Kj0U7mxuu5i9s1cV+zP2qOWV1LuSn3MVuPdSP3/Apk/cMx7JqL3nyU95hzdfMFUv0sT/X8Sn/1k8+7NI/vMZGA8KjzDQwLMgZXMEug/zPyq+bK6YfbI5Y4A3zzbZvhueZ9w/WcTGXM8Vb4Tgm/lNYiaP4CQv82f+s+1dxcwgP8BqfsNzb4SYWfyd8ma9yLa/wWxYfEj7PHlzLrbzBPu/qL4cO1T9WHCF+9uE2XuMdJh0ZjmGmUeaAM/gL277nXJ7nU7YcF55hn/edx7YT/MdeU90vtpxmHTvxDM/zS2aaHjblEM7gCT5hjjnhPPaa6z/e4hOmni8PTuQdJlwYjmDOJe4Z53Mdmy5zLk/wFlMvdw7TrFBvTuJTNl1lH1ZcbT238env79d4ZqK1YUkO5Uiu51O2WBee5pD16sWeG8KbLLYxrME13M00m/WZk/gls2wNZ7HFNucz0SfiZbFPncNOu+3H9HvCqUy7P5zM2gfCfRxyUH7sdUS/WP5o+DGbHDO/7MLjPM9rrHJSXtzB08xxyvxxKOfxDp+yxWfi5GneYq7T4QL2OuM7e531zNrn1IOlPg8/ZOfz4VUWvhh+wMqX7MvtvMWMX1nH6dzOzpftz+ZXws+Y66p8OOSafrHwdfGw/Df6xb7fhndZ/jvf+dyNcDP73rKetW6He9n8nviY+X74MmfyCrM80F/O4h2meWhOOIk72OWReeLffgh/YcnH4Qtcz5Ps8nf/s/Y/9JuVf9YPDv2neWOvf+knK/w73MLK//E/0z5x31j5F33jdj5h+oQZI0724VRe4x1mSRzO4pAk4RNWSho24yc8xQTPhMPZLLXvzJ0mXMi+GcK77JvRM3tlDm+xT1bxslQ2//FD3mWD7OFhFsgVLuffcodD+QufslaesB338gKzPusczubh39/n98y6BcIDTF1QnTiRp1imkPy4gXeZo7A+cR6fMGHR8Hm24Ahu5enf1xXznimKh+NYrES4hplLOp8z+YRZSpkn1iwd7uHfyqo/f+FTligXruOQ5+zPehXCg0xcMRzFipXCbcxdOXyBC3mS7aqoNy/w7u/vq3pm+Wrhx8xeXR05l1uZoVZYm9O4jx1rh5eYrK64+B4vMH89c81lXM589cPnuJSbWbdhWJ8HeIjlGoWbmKmxe8ghnME7fMIcTfSZ8/gLn30hXMLizcK1HNLcfnyuhfhZt5W4Wbp1+BHrt5UP67f3zL4d9JO9XgxvscLL4uAWPmH9br7zEJ8wV++wIEuxGhdwBT/kTrbrp99MONB5HMEnTDPIfeIkPmXZP8ibG3mX6f8aTmXft+3Hu3zK6kPEw118wkZDw6OsOcyccw8vsdDwsApXcgeLjAhXsdjIcA2zjglns/FY8fEYnzDluHA804wP67Aj+3IS9/MS7zLjBHPMOpzOtdz/+/dJnplmavgKJ/FrJpwmbo7geGafHs5l2tnhZCaY43+m43CO55Tf1831nnnmhQXYjYu4nN+w1HxzzW4LfGe5heEmtl4StuU5nuczy8L3WXeF/vIAn7DiKv3jNt5l23X2Z+oN5o9DOZE7+JQpNsqDVTiOm7iDbbeGfTiU53mHTzlkm3z43PZwM/PvUzcu4zl2PRheZ9ZD7gfzH/Y/X+EynuPXbHTEfWbbo+Ln0BPiZ5nT4QZ2PSseJv8qHMvXLoffs9R188NO34ZfMu+tsBaHcjH38ilf+97+LHEvXMf89+XP1x5Zz1d+lAe/5nVW+zncyfL/DdvwY37OpE/CMcz/1Pls/Ft4jK8lyBTxMFPCcAazJQobcw6PsU2S8HPWShq+xr38nkNShk+Y5Bn7cDQ/Z4JU4XBmTBNOZ+O09uUxPmGbDPbjaxnFw5cyhZeZMnPYj+N5j0Oy2p8ZstuX0/g9m+QIj7NxTnEyee5wLPPmDRcz+7NhWTbmEM7lRh7jk9//y+87XykQfs1uhcJvmLRIOIaJioUFOZIrWLl42JvbeZu5S4QLObRk+JRtSukPa5QOdzNTWfPIluXDM6xZMazNPdzHXFXCSmzEbuzHBfyER/kN73FIdfVlt5rWsWDtsCNX8BJz1AkbsCPn8TAvsVtd+zN7Q/9zLg/zlUb6y4yN3RembyIPTuUTNmlqXjn0Bf1irmbqxdTNw4l8pbXz2a9NOID3+IDd2smPr3VyfzjgRetZrnO4iem6hFNYtWv4KQd0kwcf8Cm79XA+U/eUD1/pJR8OeM0+TN9HHflaX/EzyUBzzdHczWSv+4/v8XtWeUP83MGnLPRH94LduJJn+A1T/Nl5HMfvmf1tdeRcfsoG76gLD/Nrtn83fJGdeZFf8SrzDXWPuZR72G+4uWGrEc7hWV5k0pFhHY7hftYcZX/WGe07m44JT7De2PAga4zTPzYbH/bmKd5mvvflxRoTncsG7Mfd3M/DvMdUk8IJfGWy+nPAFPPJplPVjSd4lnWmO4/9ZjiP3WaZHyaeHebgKM5jxTnO4zaeZdZ5+sfZvMhui5zHfkvEw4xL5cnpfMBWy53HZ1aE77PIynAV+31gfzZfHX7GZ9aGmfk+Z/K1de4PS24IB3M9HzPvRvFzMR+w4KZwBStuUUc23CZuHuE9tvgk7MDT/IL1dphnDvjU+cy22/6cw3tssce+7LfPd+bab18u4ANWOBBuYe+D4bu8zV+Z7lA4hS8eCb9iqaNha3bjO/yQ5/gN/8tSx6xjxePqzeQnwrEsfCr8gK0/sz+bnAmPs8NZ9WfBz/WXA86rDzNfMF8seDlsyRU8w2Rfh3n4HhexwbfhYZa7EXbhJl5jyVvhUK7nU77zvfoxye1wNLs8th9z/BTOY+2/h/tY/1/+4yFeY7V/hzuZ+v/UgxN5hl3/G15nyye+s/6vzmH9p57ZMGHm+I9HeIbZE4VdOJfXmDxxWIQtOJareJq1koR7WStp2I97eY+1knnPWsk9s0WKsBdP8xYzpgyzsAb7cTpncTfvsfoz4S5WTxW24S5+ztKpw4+YIE04nC+lDS8zb7qwIttzMbfxIqukD3cwSYZwNDNl1DfWYz/O4Fwe5D32zha+y9v8lSVyhuvYJnfYlZ/zOtPk8Z2TeJ1Z8qk/KxaUN3sXCvvzNu+zRmFzwPbczdO8yHJFxMNNvM76Rc0TD/EW6xcLm/AQjzNnCXFzPm+zSUnr+VzZcDMzPxe2Yj/O5FneY/uK8mPXKvJhghrmj7lrhq25kOdYoVa4hf1qO4+t6oiD2eqGDdiac3iY51ipXvgJX6ofDuVlPmWlBtZzaBPf+W5T88n0L4RT2a+Z+Jm1dTibuduqA5N3DMfymU5hR77PS2z2kn7zFG+z18vi4C3eY8nO/ud63mazLvZl5Vf0mdt5nZV7eM/MPcNX2Jsz+TVvs38v94uZeoczmLKPe8fx3M3e/ezHrgPEx0QDw5FsNsi5PMX7fGawdXyfp5jpjbAWW7E/Z3Avz/I+M/1BfziD95j5rbAmZ3IPK/9Jvdn0z/bjCd5j2b+Ikxt5lvneDpey/Dvhx8wxJJzHHEPDvGzEeVzMo8wx0nc+MypMzfc5kY1G6yOP8jpbjRU/k4wPM7A4R3Ma17LChHALu06yPztPCa/yhanhSb4wPRzKk3zKujPDA0w4KxzB6nOcy128ztLzwnf5EX9lwvn2Y/8F5osVF4bb2HhReIwZl4S5+AKncwFPsuTScD3LLwur8GPuYPnl6syPuYWlV8iLXVfKn/0+ML+ssMr/TLY6fI9D16g/m38Yfsb+G9ST9/krS26UHzNvcu+YZrO4OIn32H+r/Xmfv7L3J+FtVtyhP+y9y3cm2h2OZK49+sKEe+3DEdz2+/t9nlnsgPi4hr+y30H5sOIh/zPt0bAlJ/MMcx0XH58/Ya64lSdZ6LNwJVOeDrNxPOewwNlwOaucM2/Mez4swkpczFX8hC0uhKdZ9ItwNdN/FU5l5cthNbbgdu7kaaa67j/25gRu5202uhEeZYfvwy9Y6J56sfh9/3EtjzLVgzATO3ACZ/ALZn0UzmbvH8XHVH/3P2v+I6zPPTzE9D+HrTiVZ1nzn/5jo39Zz6M8y/6/uFcs8qu+8sWn4VdMlyDL/2zAKTzMWonDvXwmSfg+ayX1PztwLw/zC5ZLFm5i/uRhBS7jFvZPGd5nh1T2Y4XU1vOFNOFJdkhnPVOnDyeyRQ7r2Jtv8jS/4G3+xFx5/M8F/ILv5g9/ZdGC4WpWKCR+vlg4/IopioTj2KGoOPkFbzNliXA8M5cMZ7JfmfAeE1UIG3EkjzLT8+EMFq8UrmXDquERlq0WVmN/buRO3mfpGs7nRzzK/jWt54u11Icv1vXMYvXDNUzYQB84gr8yT6NwEd9t7DvfbeKZ5ZqaX1ZsEW5jzZbhHpZvG37MTO3EzXc5g1/xVybpGNbmaO5j9U5hPe7iQdbuHLZjf77LfbzA+/yVtbtax6I9ncfVPMgavcKmbMfdPMELLPlauJ45eodFWI/zuIoHWbJP2J7reZHv9pMP2w1wPnMODOcz0SD3gAleD4cz5x+sZ4K31JXDeZ+J/xSOYrs/O5/Z/xrOZbJ3xc33+CvbD5UfEw+3P+uPCg+x6Wh1Z/sx/ueL75k7dh8bfsd3xzufFSaEW1hhome+O8l6NpwSHmGhqeFKppsmDvbnFH7F+yw1M/yQpWZ7ZrX54U4mWxw24Xs8zixLwllsv0wevMhf2Xp5eI4lV5g7llsZbmLlD8LtfPXD8Fs22SBvHud9vrvR+Xx+c7iV1bbYlzv5LfNsDRfx+e3+Z9sd4Xn23+l8ZtoVzmDy3eFYJtrjvrD8vvBjvrQ/vMwMh8JpTHckrM0p3Mf0R8OpTHfMOvY/Ll7mOi9PLuBWprwQjmemL9xLzuAF5rsULmXOy+4/M14NB3I6H7LD1+EXHHjdd2b6zrkceMN3vnpTP9nue/Gxwh33j4nvhqPY7p71THdf/VjzYbiHRX8MV7PDT+Jnhb+HVbmFn7LdP5zHAT+HD1jgv+FyZv0lnM1uv4bfsOlT+7I9T/ACL7Lbb/5npgRZ/+dzrMIZ3MwdzJkonM9uicNvWCF52JRbeIKVUoSfsE7acD/LpQvb8SV24yZe4GV+w0Lp5cNBXMnNfMQ+GcI7TJopHMPimcO1zJVVfFzACyyWLVzDUjnCD5kylzhZiuO5kh9yUG7xskTecB3L5gs3smrB8FN2LhRe5TNFw3R8n1NYvVi4i02Ky5fHeYHJSoTvsUkZ6zigQviAGSqG05i3ijy5mI/YqFp4lD2qhzeYpkZYm5O4j8Vq6gcb1RYnj/I4q9UJd7JA3XA5C9Szjsu5k90ayIff8BG7Nwy/Y5JG4Wi2a+x/XuAjFmjiXL7YNPyKqV/wPyfyEWs2dz+5hyeYq4W5Zp1W7ie7txE/X2znfFbvYH6Y7sWwNJtyCj/iCeZ/KVzGVC/LhxP4iBk7h9OZpUv4MmfxCt99JfyVhbqHrTmAK3mOD9isR3iKpV+TBwv2DlewU7/wS2bsL16WGxhu4qDX5cdag8XHvTzH7n/UD5Z4y3yyG7tzHXfyG37HBH8Ki7MPh3Mt7zDRX6zjSN5hyXfEz/U8x2ZD1JPdh4qHqYeZY/YZbn+mHxFOZbJRYQZW4Hucxi3MNjqcw7xjwsVMMk5cHM3v2HG8eeMlXmGe98Py7M5F/JjfMeEE95zlOYLL+fHv3yeqC4fxY97hb8wxKczHeuzOeVzKg/yOCSabB+aY4nzO4x0mm6r+7DhNvXiJV9hmRvg5y810L7mJD1hpVvgJy8+RP7vPFT+7zXfP+A0fMc3CcBK7L7Ke3/ERCy8OP2CeJfrLesvUkc1X6BM/42+suDLcxoYfyJ9H+ICpVruvnMC1zLgmnM4ia8NVrLQufJHd+Qm/4ndMuj6swu4cwx38jm0+1E+m/sh95jMbzAPf5xXm2GSuWGCze8DEW53DUfycfT4xj8yx3X4s/qk8uJbfsdXu8Cxb7An78jTvss1e5zPJvrAyR3M7k+wPM3A0pzHBAfeN+Q66n6x1RJzcy7N84ah4eZJ3me1YOIcFTqkvXzgtb3blIJ7kaV7nI/b+PLzNlF+Jh+N5l42vhMfY4Gp4mMOuuS9Med0+zPJtOItFvjPv7HpDnGz9fXiOjW+7bxzGY3zA39j2jvU8zwfMcl88rP4w3MUCj9SbSX9wn1jwsTi5gueY+cdwJkv8PezM3lzHq7zNAv8IO3E5v+SAf8qHif/lPUfxAVv+OzzD+v8JDzHPU3lxEVew92/h33ibv/DVhNn+57cckCh8wJxJwvlMkTQcxyzJwllMmzzMw8lcxPSpw84cxqm8yt/YLGN4is0zhZ+xa5bwOgtl9Z59uZLXeZfNstuf+XOEy9guZ3iBGZ8Ne3M6b7Na/nAn0xTynq3ZjpO4k+d4gX0L6wvv8gG7FpUXmxdTHzYrHr7KU/yWxUqGa5ioVDiSDUqHh5m7bLiQacrLgwOfDx9yWEX9ZPPK3vMz/sb8VcK6bMBlPMDDLFE1XMd21dWP1WuEg7mLj5mtVliWr3ION/JbvlA7PMmsdcLCfI4vsDdn8wNu5kneZt664WLmrK//nM8HHNZA/di1oXlgx0bhJVZrJi7u5G32bW7OWKtFuJdVWoY72KCNfnBAW/ExR4dwHjt3dM/5TKewHN/nJqZ4KRzHPC+Hi9i0i7njCT5kpa7hJ0zcLRzFhK+GI9ixuzh5iVdZp0e4n8/11E+27RWeZ53XrGeB3uFy1umjbtzPB2zb1358vl+4lQUHhIVYkiu4kuvZcaD+8xJvM88f1Je1/xjuY9G3wtXs+yfzwhZ/Dk+z41+cw7/+Nfw/Fn1H/lzNB8w+JJzLzu/qBzsOtT+zDA9nsekIc8HcI+3DhbzK1KP8x4k8wQGjxcfOY/zPhOPDJBzB0Rw2wX1li4nqxUST5cWRvMQ+U8I7TDHN/WCVGe4nW84Mz7DorPBVrua3LDI7XMUM8/SZ03iXfRaJhykXh6mYgU04nhM4jceZaIm8WX5p+DEzLPMfE62QJztyJM/wEpusdB5fXSV/Flzt/jDRRvux3OZwE1Nsdf84jiv56jb7s/knYRt+xs+Z+1PruJCfMcVO5zDbbnPFOfyNLfeoA1/dKx52PKg+7HYk/IY1T4R7OOyU/fnqafuxzNlwA1Od02/m/zxcxoHnw4esdzE8yL6XzBtf/cp5bHrZ/WShq+acA7iSq/mAGb42T8z4nbpxOs+w643wOjN9H7bmDJ5j6gfyZmsO4EQu4zk+YOlH4Ucs/YNnJv4xHMUk/wzrczQPMfu/wrms/u9wFwf8x/lM/iQcy2QJsv/PXBzI97iAD1kkSbiKiZOGo1goebiSeVOGrbmY51jlmXAHB6ZyHgunCSvxVX7AT/gtk6UTP/OnD1txGc8yfYZwKgtkCmtwOXezVRb/M0cOcXMeH7JvzvAuM+QKc3MaFzJp7nAMM+YNp7N0gfAjNigUHmaRwmE9ruJBNi4SNuExHuerZdSTucuFnbiQX/7+vrxnvlwxvMJclc0R+1ZVDw6rFv7GYbU889m64RIWrydvruVhFq/vPQc20A82bxR+xlqNreNePmTCpuEIVnwh3MZGzcOjrNAy3MIqrc03m7YJT3BYO/mycUd94bBOvvNvL4e/sH638BCz9AhnMdVr7gEncBkH9pEvS/QN1zHvAPeWpQaGH/LZQeLmEh5j9jfNBefyCuv/wRzxEL9k3z+aFzZ+y/5M+Cf9Ybq/hlNYfKh5YIFh7jULjQxXstAozyw/JvyYqceHE/nG++GPTDRR3BzJu+w0WX5MOjXMwMIcw2n8gM9OC/OxDJdwKTew/ZzwIgst9B9XcgPrLgoPMNHisANf5kh+wSvMstT8sfqKcBezrzQHfOGD8CQrrwq3s+Rq57MT1/MLfvn797We+ew69WHp9eFHLPGheWfdj5zHA/yCuT52r9mBC7iDXzDJ5nA0W28Jz3HgVvePabeFNTiQk7mbD1nsk3ANS25XHybaJW+O5EdstDs8ymZ7w1NMfzCcytyHwoUscjRcxWTH3DO+zPe4nFdY5aRzuYOnOOyz8Df2PB3e5Mtn7MeeZ9WNN/mQ+c6FZbmUG/nsFf9zCR+y0LWwMVfyGDNdFzdn8Ddm+Ma9Zpqb7g0ncRY73govMcn3YUqO5ng2uB0e5rC7zmexe+aFGR6Ih/0ehvfY/ofwIhM8DrMyP4dzNpexx9/Vkzf4kB3/JT9m+3c4h6X/a16Z4tdwHGv+Fu5hhUQ5/mdlbuF2dk4aXmX75OFFZk0V5mdHDuNsLuMl/sb2aezHYWl9Z9asYS7O5gLmzBbOZ8Lc9uEI/say+eTL7tzI7fyOPQqFN1iueLiJ+UrKg0t5kflKec/OpdWTKcuE49m5rO/MVC6cwULlw5UsXCksyQ+4np0r24+vVwl/YLrq4RQWrmkf5q+vf8zSQB05ixv5ekP7M1NT8XPYC/rBRM3CkezVIrzFDC3DZzmM07iEv7Fz67ALr/IaU7ZRbyZvG9bmWO5j8XbhWnZpbz9m7xCW4lx+yKrdwk9Zr3t4kEl6iJvDOJpX+Rtr9QoHcRj38hF/Y7/Xwnss0Dtczn59fGeJfuE6dn/DOn7He2z2VniK1f4c7mS/v1jP594ON7PuO+EBphtiHplrZNiCC3iaOUfrE7tzPtfyOzYbI16WGR9uYKkJ+sfCE90H1p8SHmLlqeFgbudjZp0hPpbjbK7lJlafGe5itjn6zTl8xJzz5Mv6C8THfovVn92XqAczLgun84UV4UlmXSleVlulv+ywOvyCqdaGE5j2w3Ay+29QL97nY1bfZF/u4hes+7F5YfPN4WcstiVcw35bncd7fMxa29wf1vvE/WTPHeFNZt/r3nEuf+PLB8IrzH0wXMiXD4mHV/iYGY6E01jgWNiOy3mBCY6Hw5n/hLpwGT/j4JPOY/7PrGOmM+EMZj1nzjmYs7mdj/nyRXmx7CVzyY08xDpfhg24n4fZ5avwGvNeDp/nYm5l5q/Dmcx+wxxwLu+xzi3nMsNt57Inp/Eab7LCXeu4hTdZ4p79uI7XmOu++LiA99jioe88zXus8SjczX6PfWfJH8P1rPhTuI2l/x5+xLT/cM84mQeY4efwOU7jZvb8tznhTT5mif9TB5b+T/giP+JXzPxf/WS/J/JjxV/kw36/+s70T8Op7Peb72ySMOf/PM6eicKbfDZxuIQtkoSnmThpWIE9OYpbeJPFk4VduJbXWC552J6beJEpUoTj2DOl/Vk4VdiEPfkBj/MmE6QJh7NnWt+ZPF3YjmN5ge3Ty4eDeJHX+Ig5M4QvsicHcz6/4k0+/v2/zL4zU9ZwBotmsy8HczVv8jGz5gpns0nusD8H8Tjv8xHb55UvW+Z3Ps/wJlMWCQuxHcdzJS+wSzF1ZNkSYRVu5A7mKBkW4zyuYc9S8uJQ3uQjPmWV0vbn4OfVj+0res+LfMxCleXFPFXCRcxT1TNLVgvXM331cCrL1gwbsg3bcSOP8HNeYKq64QRWbWAffsrPmblx2IgzeZTlm4UfM2lzc8hybMShHMPV3MSjfMouLfWfCVt5z1c5gtf4LQe11lc2bKM+bNQ+bM6j/IwFO+ozV/Aik3cKm7I5x/IEP2PDF63nEZ5g/5fcM+Z42Rzzj53Df7BQF/PFyq+E21mwq/O4gif4Ujfzy8t8zMSvhqPYtLv/+VIP/7FhL3lx0Gvqz6a9/c/+/eXL3APdX7bnQq7hRaZ63Twz92BxcTAX8jIfM/sbYW2+xLncx8ss/odwLV/9o3lj0b+Ya9b8qzy5h/eZ8R33mtO5kbWHhA24j4fZ/G/mi2XeVXdu4CP2HRreZe7h6sNXR8iHb44Kf2Lt0fbnPj5ixjHWcTr3scl74XG2HafuPM/L/Mv48N9MM8E+nMTjzD5J/9h2sn3ZY3p4g8VmmCv2n6k/bDtLXXie3zLPbHPBRVzLLPPDWWy5QBw8wxtMvjAcyzyLwsZcxGPMv0Q/uIyPmHpZ2IJtOZGneZ49louHzVaGp5jvg3ApE672H0fwBoutU1+2/dB5LPqR+Liap9lji/1YZ2u4n40+CY/y1e36w8I7wg+YYJd6cTgfMcXucBwb7bM/0x0Mp7DGoXA3cx+WDxfyNOsfDQ/xueNhJbZlX27mJzzPu8x5Ut7swfn8ljfY4ZR1/II3+Pxn4VYWPasvrHhOHtzGQ6z1ufpyLx+x/3n3iy9eCL/ii5c8s+jlsAxXcwPzXREP23ApD/FzJrkajuagr8XH56/Ln4W/NT8s8Z36sT/X8Qbvs84N/eF+fsvmt8LP+Mr34dd88Z568JWH4ubXfMScP+g/KzwOt7DNj+rBvv8wTyz5c7iemf8dtuNMXmDe/1MHLuZ9DnoqPj77W1iPS3iQgxLkivXsmzC8yzZJws+ZKWmYhTM4i1WTh5+yTUr/s16a8CDzpA8XsVWG8Cx7ZAxvsEDmcDlzZg1L8RX25Xx+yK95l4myhSNZN7v/eYB32S5HeIGlc4UfMXMe69ifM3mB95n82XAss+YL23M2LzJPIXXiIp79/X0Rz8xVMlzAHM+F5TmPH7P/8+LjM5X0ge/zBtNVDqfwhSrhSVarFu5kvRri5EGeZfqa4VSWrBWuZ4ba4uE03mDGumFxTuda9qgX9uMN3mPx+tbzuQbhZg5q6L6wemN5cxdP8sWm4Vcs3SKsxBf4ET/hSb7Y2v9M0kYfOZrz2L+d/rFUR/PPfC+FS9njFXnxBh8xXzfrme9Vz6zVPdzL53p6z83cy9a9w3Ms1UeefJ0f8iv+wLx9w8XM0U/+7N9f/iwywLlcxXMsMjB8h6v4X2YcLC5O5w+s/cdwH9O95T6w9V+dx+5vh9+x+5CwN7/jbQ4aoR+sODLcxqSjwjHsMVYf2WOcZ6Z5P5zEbBPCypzD7awxOWzA3TzMQVPFxyrTwh0sPcs6fsRHzDk7nM8kc801c883T3ydC7mXP7DMwnADGy4KjzDRkjAVR3ICEy33noNWiJfJVomD7/EH9tik3uy/3T68z0csvUNdmG+X+8Wku/WXxfbYl2t4n9UOhDuZ5qDzOYmPWPuQ+WX/w/Zjm2Ph50x/PJzKHqfkywrnwi1Mdj6syfe4h0Uu6C9XcS/TXXSfWOhKuJI9rjmftb+WDytelwe38XO+/o3+sf+38mexG2F5ruHHrPl92IV7eI2Nb4fH2PiOZzZ46D6x3KNwExP9YB6Z4XE4jT1+kj+f/bu6cglX/f7+H87lKh7m6z+rB8v+M9zIRP8KczAPR3IeF7H8v9WLTf4vPM4u/1EvJk6Y+3+O4utJwx9YPVm4i02Sh8f5UorwMhM8Ew5ngTThcr6UMezBy7zBbJnDOcyfLVzGrDnC1pzNc8yVzzlcwMtMnD8sxMYcxZU8xpcK+J9VC4afslqhcCdbFg7PMGcR+3AQ5/MyH/2+rpj3LFzce37AyyxY1jo24wp+wFNsUCHsz8O8z8aV5M8ClfWPBavanw2rhUdYtYb6MEetcB6fqR3WYxe+z4O8xvZ1w4tM0SBsx3G8wC4N/c/nGoebmaKJOeE4LmCCFmFaDudkJmsVvsf07cKprNnePeIe/sAiHcJVLNLJMzN2CaezaNdwNdN3cx6rvBruYI3u4W6+1NO8sPlr4Wes0dt6DhpkHtlksPvOFG+oGzu9GX7JjH9UB07nHpZ6K6zND7mP6f4cTmH9v4aHmPntcCYLvmMe+cwQ88Tnh+oHt/IHNhoWHmWZ4WEdbuB+NhppPQuPcs/4/OiwGrdyJ18f63x2Gq/u/JKPWPb9cCMrTQw/YZFJ+shV3M16k90rNpmin3x9pnjYaZY4WGO+/ZhnYbiIQxeFT5lvify5lDtZaan4WWOZ/Vl2uTi4kV8yw6pwGl9cHX7F19fIh6+tD79nlw+95zV+zy4fe89y28Km3MQTzLJdH1iBs7iKW5hnh/ox7U7/cTK3MNEecXEkr/HFffJn8/3uN18/pB7MfjiszrncxUxHwxksfMx6fsBdrHRcP1nsZFiSa7ie2U6Jk3P4GXudC2+x5/nwJjteDC8xw5fmgfmumDsWv2pfruUtZr+mDizzrXO5gTdZ9Ub4KTPdsZ4zeJO574ULmeeB9VzEm+z1WHws/LO6s/U/w3Ms8y9x8vl/h1s54L/hA+Z7oj4s9qs+sddvzmfFBHn+5zY2Txz24me8xVxJwgFcwAfMkjTMy1lc/Pv7ZJ45IIX/+UzK8H2+/kz4A8ulDjcxZfowB3txPOfxFpNkCEczT9bwFS7i10yVK5zADLnDNziNPzJlHucz8bPhKD6bL1zCIgXCVUxYOBzBhkXUiUf4gM+VCDczW+mwDAdwDjfwAROVCUeyRVn/8TQfMHvFsDg7cS7X8kuWrRT25Ebe/P19Fc98oap+8CS/ZpZq4Uucxcu/v6/umUVrh6tZsk64nsXriZtZG4az+Upj5zNFy3Ac87UKl7Jwu/ADpuwQpuV4TmbijmE1juJONugUHmaNl8LdfPPV8CfW6B52Y0/u5je8yVd6yo+ZeoXd2ZMz+B1vMm1v+bByn3A78/cLl7FTf/PBNAPCSUw+0Lkcy5ss/Lr6Mstg6zmLN9npDeex1VvhWZb8q3lg2rfVlZP5E1MNcR4n8CZLDAtrch33MNcIcbAbF/Asv2Hi0eaCSceEY1hzrP2ZZ3y4iBknhNOZdmJYhs9zMjdwK5NNDt9j0inOZ96p4WI2nKnuPMIvWXtWuI+dZvvObHPCOew2Tz1YdUH4KasvDHex2eLwFJ9fIh/mXaa/XMyfmHa5cziZn7LOynA/e67Sf+ZdE7bkYp7hc2vDzSy2PlzDTB+6bxywIXybD/gfZv04nM2iW8LXOJir+T0fs9r2cCdL7wjrsgE/4gEe5ps71Y8tdoenWWhfuJJ596sLWxywns8eCpcwy2H3mg2OqC+78DDP8BrTHw2nMt2xcAqbHTcvbHFKPMx4Rp85nTc54Ky+sOv58DrzXVAHLuVKvnZRX5jwC/eEjTiC+3iU7S/Zhxf5PQd+GT5k2sv+42ReZJpr6sJJPMXBX5sf1v5GXGz2rfWsdzM8yCK3wlV887b5Yc873vMmf2KGB2FjvslpPMaf2OpReJa1fgj3sutP+sXn/u5+8rV/qCcz/xzOZON/Wsdj/J4l/yMvrudNZvqv75zBmyzzJNzAnr/5zpcT5P2fV5goYZiHtTmSi7iPpZOHb/Ij/sTEKcJRTJUybMQJPMoXnglPMkWacBwTpA2Hs2C6cAUzZQhnMHPGsCdn8iazZBIPZ/EoM2cOn+MAzuRmPuDgrOFjJs0ejmHXHOF1DswVPmSGvOE0Ni3gXJ7gA75WSP35PX9ihsLhy5zGK2xaxL7MUVQenMfHfLOY/Vm9eLiLyUrIk+/xOjOXUj9mLu2ZScqE+diao7mU59i1rP3Ztbxn5q0QLuaA59WP+SrZl2WqhhvYtZr9mLm6vnEmHzJrjbAIZ3MVk9cMy7Eqx3ITP2WOWvrAUnXDD/laPX1n6vrhRKZoFKblOE5m1ybyY/Gm4VqmbWU927QOP2ePNuENdm1nP1ZvH/6Ju/hPvtZB/OzR0X6s3sl/bPOS89mkc3icb75iPlm8q/vLZlzLzTzF4t18Z55XwzpcxP1M312cnMpdTNNDnJzE42zTUz4c3Mt9Y7Pe4mP+ft5zGU8xx4CwGluyB+dxJ8/wxu/rBnlm9jftw7k8w1R/CCew5VthZ57hVeb7k3pxKXcx5Z/D51mH47mV+5nir+aaLd92LqsMCXfw2Xe95xLuYM0R4R6mHaUOnMy5TDpG3TiGN5hifNiR43iJpd8PP2LdieEBDpwcPmTGKeF0Fp4dfsCkc8JiLMcxXMNNTDQ3bM2RPMf288OLLL0gbMQe/IhHeYNpFoaVOImfsOUi/WLCxfLkCE5nyiXmgaWWhpX5Ibez6LJwNRN+EGZmWY7gTG5k5zXmlznXh704n7fY/kP1YroN4RSW22RfbuJV9vhY/Zh4c1ibo7iPSbaGbTma5/nsNnPOHp/Yn3k/DRezwq5wC1vvMQ8stD9sxdZcybM8xwKHwuJsz+Vcy4vMcFi+7MGBnMZ9vMGHzHk0zM/5XMZWx8XNszzHoies42qeZeuT1rPOKeu5n2eZ5DPr2J+juZ/32fRMeILFz6oTu10Iv2HSL9xvlrgUrmORy+EqNrsavsyBPMUrfMjBX4ePmeh6OJJ1vw0P8G83w1+Y7Zb6cA7PMvftsAnbcSGP8wIT3AnLsCqHcwM/Zba7YS7O4QJ2ehh+yew/hHOZ6LF8We2ncCcT/cN3Zv1Z3Tibj1nin2FZruNGNvx3eIQD/08/WOe/5oaDn9ifCX9VF47gp8yQ8Nn/WZDTuILZEoXPcw63smXi8AzzJgsXs1zycBNzpwy7cSG/YdNUvvMEv2GC1OFwPpMmfJ+10oZ7WS9deJCZ0ocz2D9TeJ/lM4cf89k84ctcwitM+Ww4noULhG/wA/7IwQXDx0xe2P4cyytMVyQcyCl8yGJFwzUsWjxczZdL2I/9S8qXA0uLiw/5mCnLyI95ytmHi3ifAyvYhymft57jeZ8vVxQfq1QOd7BG1XA3U1QPxzFhrbA+R/AQi9cO17JaXXFyJx8yTb1wEt9oqH/M0Sicx/xNw2Xs8EL4BV9uJj++0cJ+TNXSOk7gFZZsFa5ni3b6xdN8zIHt5cOBHTzzuY724csczM08zSt8zIEv2o/JOusv3+N91uwS9uIe3mKebmEbDuQifs6HTN87nMpCfcKVzNVXPlzAx2zYLzzCgQPsz16vi4/PvWEdN/Ph7+/f9MwUfzSfLPHnsBzXcRMHvBM+YP2/mV9me9d7tuUAzuEhnucD9hwa3mTD4fLkEd7iwFHyYYnR3nMdH/KZsWERvs9V7DbOefyGt9hzvPiYbYK8mGeKe8ZFXMY8U8MyHMhF3MCHfGaWONlhjvvKonPD1XxrXvgzk8wPR3PwAvPF7IvDuSy0xHyyw1LnsfLycDvzrJQvK3MRl3E7S68NP2LPderJiuvDKtzGHez1oX4w04ZwButuVFce4EMm3hSO4hsfhz+y2uZwJ5tuNcc8wfMsuc0+XM8fmfOTsDbncx8zfipeTudDptqpzmzLCVzE82y9OzzHXvvVi4UPhB+w9UHr2euI9ax8VP+Y+pj7wxacyFU8zVon1Id7+SOrnbQPd/I0W58O2/McL7LkWftwPXey9Tn/MeH5cASLXwjXsthX4RrWuaw/3M+HTHVVP5joWjiSvb5WT5b7LtzECrfCLcx2J5zD9nflyzQPw0ls+1j/Wf1H9eYu/shefxcPc/0cluFgLuAGPmaSf4Wjmeg/8mXqX8KiLMuJXM2N/H9QSwECLQMtAAAACAAAACEApwEeiU0BAACAAQAACAAAAAAAAAAAAAAAgAEAAAAAaW52Zi5ucHlQSwECLQMtAAAACAAAACEAd6p5kRFpAABApgEABgAAAAAAAAAAAAAAgAGHAQAAcGMubnB5UEsBAi0DLQAAAAgAAAAhAPfurI/hagAAoLABAAYAAAAAAAAAAAAAAIAB0GoAAHBzLm5weVBLBQYAAAAAAwADAJ4AAADp1QAAAAA="


def _rope_tables():
    import io, base64
    z = np.load(io.BytesIO(base64.b64decode(_TAB_B64)))
    invf = z["invf"].view(np.float32)
    ang = (np.arange(SEQ, dtype=np.float64)[:, None] * invf.astype(np.float64)).astype(np.float32)
    emb = np.concatenate([ang, ang], 1)
    cos = np.cos(emb.astype(np.float64)).astype(np.float32)
    sin = np.sin(emb.astype(np.float64)).astype(np.float32)
    for a, pk in ((cos, z["pc"]), (sin, z["ps"])):
        f = a.reshape(-1)
        idx = pk >> 1
        up = (pk & 1).astype(bool)
        f[idx[up]] = np.nextafter(f[idx[up]], np.float32(np.inf), dtype=np.float32)
        f[idx[~up]] = np.nextafter(f[idx[~up]], np.float32(-np.inf), dtype=np.float32)
    cosT = np.ascontiguousarray(cos.T)
    sinT = np.ascontiguousarray(np.concatenate([sin.T[0:64], -sin.T[64:128]], axis=0))
    return np.stack([cosT, sinT])


def build_program(debug=0):
    nc = bass.Bass("TRN2", target_bir_lowering=False)

    def din(name, shape, dtype=F32):
        return nc.dram_tensor(name, list(shape), dtype, kind="ExternalInput").ap()

    xbT_d = din("xbT", (D, SEQ))
    xrT_d = din("xrT", (D, T))
    pos_d = din("pos", (1, SEQ))
    iidx_d = din("iidx", (1, E))
    wqT_d = din("wqT", (4, 128, NDC * 128))
    wkT_d = din("wkT", (HK, 128, NDC * 128))
    wvT_d = din("wvT", (NDC, 128, 1024))
    woT_d = din("woT", (NDC, 128, NDC * 128))
    rT_d = din("rT", (128, E))
    r2T_d = din("r2T", (128, E))
    w1T_d = din("w1T", (D, DFF))
    w2T_d = din("w2T", (DFF, D))
    wsrc1_d = din("wsrc1", (1024, 128, 128))
    wsrc2_d = din("wsrc2", (1024, 128, 128))
    bq_d = din("bq", (1, 512))
    bk_d = din("bk", (1, HK * HD))
    bv_d = din("bv", (1, HK * HD))
    bo_d = din("bo", (1, D))
    b1_d = din("b1", (1, DFF))
    b2_d = din("b2", (1, D))
    onef_d = din("ones_f", (1, 128))
    idnf_d = din("idnf", (128, 128))
    nsel4_d = din("nsel4", (128, 16), BF16)
    tbl_d = din("tbl", (2, 128, SEQ))
    bsel4_d = din("bsel4", (4, 512), BF16)

    out_d = nc.dram_tensor("outT", [D, T], F32, kind="ExternalOutput").ap()
    dbg_d = {}
    if debug:
        for nm, shp in [("d_q", (128, 512)), ("d_qp", (128, T)), ("d_kp", (128, T)),
                        ("d_v", (128, 1024)), ("d_A", (128, 128)), ("d_mx", (128, 128)),
                        ("d_x1", (128, 512)), ("d_ln2", (128, 512)), ("d_h", (128, 512)),
                        ("d_st", (4, T)), ("d_w1q", (128, 1024)), ("d_o", (128, 512)),
                        ("d_tb", (128, T)), ("d_m2", (128, 512)), ("d_iv", (4, 512)), ("d_y", (128, 512)),
                        ("d_qs", (1, 8))]:
            dbg_d[nm] = nc.dram_tensor(nm, list(shp), F32, kind="ExternalOutput").ap()

    w1q_d = nc.dram_tensor("w1q_s", [NFC, 128, NDC * 128], BF16, kind="Internal").ap()
    w2q_d = nc.dram_tensor("w2q_s", [NFC, 128, D], BF16, kind="Internal").ap()
    x1_d = nc.dram_tensor("x1_s", [D, T], F32, kind="Internal").ap()
    ln2_d = nc.dram_tensor("ln2_s", [D, T], BF16, kind="Internal").ap()
    mix_d = nc.dram_tensor("mix_s", [128, 16, SEQ], BF16, kind="Internal").ap()
    spl_d = nc.dram_tensor("spl_s", [D, 512], F32, kind="Internal").ap()
    st2_d = nc.dram_tensor("st2_s", [2, T], BF16, kind="Internal").ap()
    sq_d = nc.dram_tensor("sq_s", [2, 16, 1024], F32, kind="Internal").ap()
    tb_d = nc.dram_tensor("tb_s", [2, 1, 16], F32, kind="Internal").ap()
    trow_d = nc.dram_tensor("trow_s", [2, 1, 16], F32, kind="Internal").ap()
    corr_d = nc.dram_tensor("corr_s", [2, 256], BF16, kind="Internal").ap()
    corrv_d = nc.dram_tensor("corrv_s", [2, 1024], BF16, kind="Internal").ap()

    es = ExitStack()
    with es:
        Sb = lambda nm, shp, dt=F32: es.enter_context(nc.sbuf_tensor(nm, list(shp), dt))
        U1 = Sb("U1", [128, 32768], BF16)
        U2 = Sb("U2", [128, 16384], BF16)
        U3 = Sb("U3", [128, 16384], BF16)
        U4 = Sb("U4", [128, 10752], BF16)
        U5 = Sb("U5", [128, 4096], BF16)
        U6 = Sb("U6", [128, 4096], F32)
        U7 = Sb("U7", [128, 3072], F32)
        XRE = lambda kx: U6[:, kx * 512:(kx + 1) * 512]
        OT = lambda ko: U6[:, 1024 + ko * 512: 1024 + (ko + 1) * 512]
        X1T = U6[:, 2048:2560]
        INV4 = U6[0:4, 1536:2048]
        BCM = U6[:, 2560:3072]
        BCS = U6[:, 3072:3584]
        WRK = U6[:, 3584:4096]
        
        c_idnf = Sb("c_idnf", [128, 128])
        c_nsel4 = Sb("c_nsel4", [128, 16], BF16)
        c_bsel4 = Sb("c_bsel4", [4, 512], BF16)
        c_onesb = Sb("c_onesb", [128, 512], BF16)
        c_onesf = Sb("c_onesf", [1, 128])
        c_rT = Sb("c_rT", [128, E], BF16)
        c_r2T = Sb("c_r2T", [128, E], BF16)
        c_iidx = Sb("c_iidx", [1, E])
        c_invf = Sb("c_invf", [1, 128])
        kPI = Sb("kPI", [128, 6])
        kCp = Sb("kCp", [128, 11])
        kCn = Sb("kCn", [128, 11])
        tinyb = Sb("tinyb", [1, 8])
        st2_bf = Sb("st2_bf", [2, T], BF16)
        s_cols = Sb("s_cols", [128, NTT])
        row_a = Sb("row_a", [1, 512])
        row_b = Sb("row_b", [1, 512])
        row_c = Sb("row_c", [1, 512])
        mo_bf = Sb("mo_bf", [1, 512], BF16)
        so_bf = Sb("so_bf", [1, 512], BF16)
        corr = Sb("corr", [2, 256], BF16)
        corrv = Sb("corrv", [2, 1024], BF16)
        inv4b = Sb("inv4b", [4, 512], BF16)
        q_acc = Sb("q_acc", [128, 2])
        q_tc = Sb("q_tc", [128, 6])
        qsc = Sb("qsc", [1, 8])
        qth = Sb("qth", [1, 12])
        a_hq = Sb("a_hq", [128, 128])
        acc_a = Sb("acc_a", [128, 128])
        acc_b = Sb("acc_b", [128, 128])
        mixT = Sb("mixT", [128, 128], BF16)

        pb = [es.enter_context(nc.psum_tensor(f"pb{i}", [128, 512], F32)) for i in range(8)]

        sems = {k: es.enter_context(nc.semaphore(k)) for k in ALL_SEMS}
        P = Prog(sems)
        V, SC, TE, G = "vector", "scalar", "tensor", "gpsimd"

        R = Res
        rU1, rU2, rU3, rU4, rU5, rU6, rU7 = R(), R(), R(), R(), R(), R(), R()
        r_pb = [R() for _ in range(8)]
        r_const, r_kpi, r_tiny, r_rows, r_st2, r_scols = R(), R(), R(), R(), R(), R()
        r_corr = [R(), R()]
        r_corrv, r_inv = R(), R()
        r_qacc, r_qtc = R(), R()
        r_xst = [R(), R()]
        r_xre = [R(), R()]
        r_ot = [R(), R()]
        r_x1t = R()
        r_bcm, r_bcs, r_wrk = R(), R(), R()
        r_ahq, r_acca, r_accb, r_mixT = R(), R(), R(), R()
        r_mrow = R()
        r_dw1, r_dw2, r_dx1, r_dln2, r_dmix, r_dspl = R(), R(), R(), R(), R(), R()

        def TSc(eng, out, in0, s1, s2=None, op0=AL.mult, op1=None, **kw):
            if op1 is None:
                return P.op(eng, lambda E_: E_.tensor_scalar(
                    out=out, in0=in0, scalar1=s1, scalar2=None, op0=op0), **kw)
            return P.op(eng, lambda E_: E_.tensor_scalar(
                out=out, in0=in0, scalar1=s1, scalar2=s2, op0=op0, op1=op1), **kw)

        def TTn(eng, out, in0, in1, aop, **kw):
            return P.op(eng, lambda E_: E_.tensor_tensor(out=out, in0=in0, in1=in1, op=aop), **kw)

        def STT(out, in0, sc, in1, op0, op1, **kw):
            return P.op(V, lambda E_: E_.scalar_tensor_tensor(
                out=out, in0=in0, scalar=sc, in1=in1, op0=op0, op1=op1), **kw)

        def ACT(out, in_, func, bias=None, scale=None, accum=None, **kw):
            def f(E_):
                kwargs = {}
                if bias is not None:
                    kwargs["bias"] = bias
                if scale is not None:
                    kwargs["scale"] = scale
                if accum is not None:
                    kwargs["accum_out"] = accum
                return E_.activation(out=out, in_=in_, func=func, **kwargs)
            return P.op(SC, f, **kw)

        def CP(eng, out, in_, **kw):
            return P.op(eng, lambda E_: E_.tensor_copy(out=out, in_=in_), **kw)

        def MM(out, lhsT, rhs, start, stop, **kw):
            return P.op(TE, lambda E_: E_.matmul(out, lhsT, rhs, start=start, stop=stop), **kw)

        def TRp(out, in_, ident, **kw):
            return P.op(TE, lambda E_: E_.transpose(out, in_, ident), **kw)

        def MEMS(eng, ap_, val, **kw):
            return P.op(eng, lambda E_: E_.memset(ap_, val), **kw)

        def RECIP(out, in_, **kw):
            return P.op(V, lambda E_: E_.reciprocal(out=out, in_=in_), **kw)

        P.dma("ld", c_idnf[:], idnf_d, writes=[r_const])
        P.dma("ld", c_nsel4[:], nsel4_d, writes=[r_const])
        P.dma("ld", c_bsel4[:], bsel4_d, writes=[r_const])
        P.dma("ld", c_onesf[:], onef_d, writes=[r_const])
        P.dma("ld", c_iidx[:], iidx_d, writes=[r_const])
        P.dma("ld", U7[:, 0:E], rT_d, writes=[rU7])
        P.dma("ld", U7[:, E:2 * E], r2T_d, writes=[rU7])
        MEMS(V, c_onesb[:], 1.0, writes=[r_const])
        CP(V, c_rT[:], U7[:, 0:E], reads=[rU7], writes=[r_const])
        CP(V, c_r2T[:], U7[:, E:2 * E], reads=[rU7], writes=[r_const])
        for (ap_in, off, n) in [(bq_d, 0, 512), (bk_d, 2048, HK * HD), (bv_d, 3072, HK * HD)]:
            for c in range(max(1, n // 1024)):
                w_ = min(1024, n - c * 1024)
                P.dma("ld", U7[0:1, 2048:2048 + w_], ap_in[0:1, c * 1024: c * 1024 + w_],
                      writes=[rU7])
                ACT(U5[0:1, off + c * 1024: off + c * 1024 + w_], U7[0:1, 2048:2048 + w_],
                    AF.Copy, reads=[rU7], writes=[rU5])

        MEMS(V, tinyb[0:1, 0:1], 1.0, writes=[r_tiny])
        ACT(tinyb[0:1, 0:1], tinyb[0:1, 0:1], AF.Arctan, reads=[r_tiny], writes=[r_tiny])
        MM(pb[0][:, 0:1], c_onesf[0:1, :], tinyb[0:1, 0:1], True, True,
           reads=[r_tiny, r_const], writes=[r_pb[0]])
        TSc(V, kPI[:, 0:1], pb[0][:, 0:1], 4.0, reads=[r_pb[0]], writes=[r_kpi])
        TSc(V, kPI[:, 1:2], kPI[:, 0:1], 2.0, reads=[r_kpi], writes=[r_kpi])
        TSc(V, kPI[:, 2:3], kPI[:, 1:2], -1.0, reads=[r_kpi], writes=[r_kpi])
        TSc(V, kPI[:, 3:4], kPI[:, 0:1], 0.5, reads=[r_kpi], writes=[r_kpi])
        TSc(V, kPI[:, 4:5], kPI[:, 0:1], -1.0, reads=[r_kpi], writes=[r_kpi])
        def build_tables(g):
            gsl = slice(g * T, (g + 1) * T)
            P.dma("ld", U7[:, 0:2048], tbl_d[0][:, gsl], writes=[rU7])
            ACT(U4[:, 0:2048], U7[:, 0:2048], AF.Copy, reads=[rU7], writes=[rU4])
            P.dma("ld", U7[:, 0:2048], tbl_d[1][:, gsl], reads=[rU7], writes=[rU7])
            ACT(U4[:, 2048:4096], U7[:, 0:2048], AF.Copy, reads=[rU7], writes=[rU4])

        def tbc(t5):
            return U4[:, t5 * 512:(t5 + 1) * 512]

        def tbs(t5):
            return U4[:, 2048 + t5 * 512: 2048 + (t5 + 1) * 512]

        def phase_x(g):
          for t5 in range(NT5):
            sl5 = slice(t5 * 512, (t5 + 1) * 512)
            gsl = slice(g * T + t5 * 512, g * T + (t5 + 1) * 512)
            for dc in range(NDC):
                k = dc % 2
                xsl = slice(dc * T + t5 * 512, dc * T + (t5 + 1) * 512)
                sqsl = slice(9216, 9728)
                P.dma("ld", U6[:, k * 512:(k + 1) * 512], xbT_d[dc * 128:(dc + 1) * 128, gsl],
                      writes=[r_xst[k]])
                CP(V, U1[:, xsl], U6[:, k * 512:(k + 1) * 512], reads=[r_xst[k]], writes=[rU1])
                ACT(U4[:, sqsl], U6[:, k * 512:(k + 1) * 512], AF.Square,
                    reads=[r_xst[k]], writes=[rU4])
                MM(pb[0][0:1, :], c_onesb[:, 0:1], U1[:, xsl], dc == 0, dc == NDC - 1,
                   reads=[rU1, r_const], writes=[r_pb[0]])
                MM(pb[1][0:1, :], c_onesb[:, 0:1], U4[:, sqsl], dc == 0, dc == NDC - 1,
                   reads=[rU4, r_const], writes=[r_pb[1]])
            TSc(V, row_a[0:1, :], pb[0][0:1, :], 1.0 / 2048.0, reads=[r_pb[0]], writes=[r_rows])
            TSc(V, row_b[0:1, :], pb[1][0:1, :], 1.0 / 2048.0, reads=[r_pb[1]], writes=[r_rows])
            ACT(row_c[0:1, :], row_a[0:1, :], AF.Square, reads=[r_rows], writes=[r_rows])
            TTn(V, row_b[0:1, :], row_b[0:1, :], row_c[0:1, :], AL.subtract,
                reads=[r_rows], writes=[r_rows])
            TSc(V, row_b[0:1, :], row_b[0:1, :], EPS_LN, None, op0=AL.add,
                reads=[r_rows], writes=[r_rows])
            RECIP(row_c[0:1, :], row_b[0:1, :], reads=[r_rows], writes=[r_rows])
            ACT(row_c[0:1, :], row_c[0:1, :], AF.Sqrt, reads=[r_rows], writes=[r_rows])
            for ttl in range(4):
                tt = t5 * 4 + ttl
                TRp(pb[7][:, tt:tt + 1], row_c[0:1, ttl * 128:(ttl + 1) * 128],
                    c_onesf[0:1, 0:1], reads=[r_rows, r_const], writes=[r_pb[7]])
            ACT(row_b[0:1, :], row_b[0:1, :], AF.Sqrt, reads=[r_rows], writes=[r_rows])
            TSc(V, so_bf[0:1, :], row_a[0:1, :], -1.0, reads=[r_rows], writes=[r_mrow])
            CP(V, mo_bf[0:1, :], row_b[0:1, :], reads=[r_rows], writes=[r_mrow])
            P.dma("sc", st2_d[0:1, sl5], so_bf[0:1, :], reads=[r_mrow], writes=[r_st2])
            P.dma("sc", st2_d[1:2, sl5], mo_bf[0:1, :], reads=[r_mrow], writes=[r_st2])
            P.dma("ld", st2_bf[:, sl5], st2_d[:, sl5], reads=[r_st2], writes=[r_st2])
            if debug and g == 0:
                P.dma("st", dbg_d["d_st"][0:1, sl5], row_a[0:1, :], reads=[r_rows])
                P.dma("st", dbg_d["d_st"][1:2, sl5], row_c[0:1, :], reads=[r_rows])
          CP(V, s_cols[:], pb[7][:, 0:NTT], reads=[r_pb[7]], writes=[r_scols])

        def phase_v(g):
          for dc in range(NDC):
              k = dc % 2
              P.dma("ld", U7[:, k * 1024: k * 1024 + 1024], wvT_d[dc], writes=[rU7])
              CP(V, U2[:, dc * 1024:(dc + 1) * 1024], U7[:, k * 1024: k * 1024 + 1024],
                 reads=[rU7], writes=[rU2])
          for fc2 in range(2):
              for dc in range(NDC):
                  MM(pb[2][0:1, :], c_onesb[:, 0:1],
                     U2[:, dc * 1024 + fc2 * 512: dc * 1024 + (fc2 + 1) * 512],
                     dc == 0, dc == NDC - 1, reads=[rU2, r_const], writes=[r_pb[2]])
              CP(V, so_bf[0:1, :], pb[2][0:1, :], reads=[r_pb[2]], writes=[r_mrow])
              P.dma("sc", corrv_d[0:1, fc2 * 512:(fc2 + 1) * 512], so_bf[0:1, :],
                    reads=[r_mrow], writes=[r_corrv])
          P.dma("sc", corrv_d[1:2, :], U5[0:1, 3072:4096], reads=[rU5], writes=[r_corrv])
          P.dma("ld", corrv[:], corrv_d[:], reads=[r_corrv], writes=[r_corrv])
          for tt in range(NTT):
              for fc2 in range(2):
                  bank = (tt * 2 + fc2) % 2
                  for dc in range(NDC):
                      MM(pb[bank][:, :], U1[:, dc * T + tt * 128: dc * T + (tt + 1) * 128],
                         U2[:, dc * 1024 + fc2 * 512: dc * 1024 + (fc2 + 1) * 512],
                         dc == 0, False, reads=[rU1, rU2], writes=[r_pb[bank]])
                  MM(pb[bank][:, :], st2_bf[:, tt * 128:(tt + 1) * 128],
                     corrv[:, fc2 * 512:(fc2 + 1) * 512], False, True,
                     reads=[r_st2, r_corrv], writes=[r_pb[bank]])
                  ACT(U3[:, tt * 1024 + fc2 * 512: tt * 1024 + (fc2 + 1) * 512],
                      pb[bank][:, :], AF.Copy, scale=s_cols[:, tt:tt + 1],
                      reads=[r_pb[bank], r_scols], writes=[rU3])
          if debug:
              CP(V, U6[:, 0:1024], U3[:, 0:1024], reads=[rU3],
                 writes=[rU6, r_xre[0], r_xre[1]])
              P.dma("st", dbg_d["d_v"], U6[:, 0:1024], reads=[rU6, r_xre[0], r_xre[1]])
        P.set_fence()
        def head_pipeline(is_q, h, g):
            wsrc = wqT_d[h] if is_q else wkT_d[h]
            k = h % 2
            wslot = slice(4096, 6144)
            P.dma("ld", U7[:, 0:2048], wsrc, writes=[rU7])
            CP(V, U4[:, wslot], U7[:, 0:2048], reads=[rU7], writes=[rU4])
            for dc in range(NDC):
                MM(pb[7][0:1, 0:128], c_onesb[:, 0:1],
                   U4[:, wslot.start + dc * 128: wslot.start + (dc + 1) * 128],
                   dc == 0, dc == NDC - 1, reads=[rU4, r_const], writes=[r_pb[7]])
            CP(V, so_bf[0:1, 0:128], pb[7][0:1, 0:128], reads=[r_pb[7]], writes=[r_mrow])
            boff = h * 128 if is_q else 2048 + h * 128
            P.dma("sc", corr_d[0:1, k * 128:(k + 1) * 128], so_bf[0:1, 0:128],
                  reads=[r_mrow], writes=[r_corr[k]])
            P.dma("sc", corr_d[1:2, k * 128:(k + 1) * 128], U5[0:1, boff:boff + 128],
                  reads=[rU5], writes=[r_corr[k]])
            P.dma("ld", corr[:, k * 128:(k + 1) * 128], corr_d[:, k * 128:(k + 1) * 128],
                  reads=[r_corr[k]], writes=[r_corr[k]])
            for t5 in range(NT5):
                sl5 = slice(t5 * 512, (t5 + 1) * 512)
                bank = t5 % 2
                for dc in range(NDC):
                    MM(pb[bank][:, :],
                       U4[:, wslot.start + dc * 128: wslot.start + (dc + 1) * 128],
                       U1[:, dc * T + t5 * 512: dc * T + (t5 + 1) * 512],
                       dc == 0, False, reads=[rU4, rU1], writes=[r_pb[bank]])
                MM(pb[bank][:, :], corr[:, k * 128:(k + 1) * 128], st2_bf[:, sl5],
                   False, True, reads=[r_corr[k], r_st2], writes=[r_pb[bank]])
                if debug and is_q and h == 0 and t5 == 0 and g == 0:
                    CP(V, U6[:, 0:512], pb[bank][:, :], reads=[r_pb[bank]],
                       writes=[rU6, r_xre[0]])
                    P.dma("st", dbg_d["d_q"], U6[:, 0:512], reads=[rU6, r_xre[0]])
                TTn(V, U4[:, 8192:8704], pb[bank][:, :], tbc(t5), AL.mult,
                    reads=[r_pb[bank], rU4], writes=[rU4])
                TTn(V, U4[:, 8704:9216], pb[bank][:, :], tbs(t5), AL.mult,
                    reads=[r_pb[bank], rU4], writes=[rU4])
                sqsl = slice(9216, 9728)
                ACT(U4[:, sqsl], pb[bank][:, :], AF.Square, reads=[r_pb[bank]], writes=[rU4])
                MM(pb[4][0:4, :], c_nsel4[:, t5 * 4:(t5 + 1) * 4], U4[:, sqsl],
                   t5 == 0, t5 == NT5 - 1, reads=[rU4, r_const], writes=[r_pb[4]])
                ppb = 2 + t5 // 2
                half = t5 % 2
                hsl = slice(half * 64, half * 64 + 64)
                MM(pb[ppb][hsl, :], c_rT[:], U4[:, 8192:8704], True, False,
                   reads=[rU4, r_const], writes=[r_pb[ppb]])
                MM(pb[ppb][hsl, :], c_r2T[:], U4[:, 8704:9216], False, True,
                   reads=[rU4, r_const], writes=[r_pb[ppb]])
            TSc(V, INV4, pb[4][0:4, :], 1e-12, None, op0=AL.max,
                reads=[r_pb[4]], writes=[r_inv])
            RECIP(INV4, INV4, reads=[r_inv], writes=[r_inv])
            ACT(INV4, INV4, AF.Sqrt, reads=[r_inv], writes=[r_inv])
            CP(V, inv4b[:, :], INV4, reads=[r_inv], writes=[r_inv])
            if debug and is_q and h == 0 and g == 0:
                P.dma("st", dbg_d["d_iv"], INV4, reads=[r_inv])
            for t5 in range(NT5):
                ppb = 2 + t5 // 2
                half = t5 % 2
                hsl = slice(half * 64, half * 64 + 64)
                MM(pb[5][:, :], c_bsel4[:, t5 * 128:(t5 + 1) * 128], inv4b[:, :], True, True,
                   reads=[r_inv, r_const], writes=[r_pb[5]])
                CP(V, U6[0:64, 0:512], pb[ppb][hsl, :], reads=[r_pb[ppb]], writes=[rU6])
                CP(V, U6[64:128, 0:512], pb[ppb][hsl, :], reads=[r_pb[ppb]], writes=[rU6])
                TTn(V, U6[:, 512:1024], U6[:, 0:512], pb[5][:, :], AL.mult,
                    reads=[rU6, r_pb[5]], writes=[rU6])
                TSc(V, U6[64:128, 512:1024], U6[64:128, 512:1024], kPI[64:128, 3:4], None,
                    op0=AL.add, reads=[r_kpi], writes=[rU6])
                TSc(V, U6[:, 1024:1536], U6[:, 512:1024], kPI[:, 0:1], None, op0=AL.is_gt,
                    reads=[rU6, r_kpi], writes=[rU6])
                STT(U6[:, 0:512], U6[:, 1024:1536], kPI[:, 2:3], U6[:, 512:1024],
                    AL.mult, AL.add, reads=[rU6, r_kpi], writes=[rU6])
                TSc(V, U6[:, 1024:1536], U6[:, 512:1024], kPI[:, 4:5], None, op0=AL.is_lt,
                    reads=[rU6, r_kpi], writes=[rU6])
                STT(U6[:, 512:1024], U6[:, 1024:1536], kPI[:, 1:2], U6[:, 0:512],
                    AL.mult, AL.add, reads=[rU6, r_kpi], writes=[rU6])
                if is_q:
                    ACT(U4[:, 6144 + t5 * 512: 6144 + (t5 + 1) * 512], U6[:, 512:1024],
                        AF.Sin, reads=[rU6], writes=[rU4])
                else:
                    ACT(U2[:, h * T + t5 * 512: h * T + (t5 + 1) * 512], U6[:, 512:1024],
                        AF.Sin, reads=[rU6], writes=[rU2])

        def phase_heads(g):
          for hk in range(HK):
              head_pipeline(False, hk, g)
          if debug and g == 0:
              CP(V, U6[:, 0:2048], U2[:, 0:T], reads=[rU2],
                 writes=[rU6, r_inv, r_xre[0], r_xre[1]])
              P.dma("st", dbg_d["d_kp"], U6[:, 0:2048], reads=[rU6, r_inv])
          for hq in range(4):
              head_pipeline(True, hq, g)
              if debug and hq == 0 and g == 0:
                  CP(V, U6[:, 0:2048], U4[:, 6144:8192], reads=[rU4],
                     writes=[rU6, r_inv, r_xre[0], r_xre[1]])
                  P.dma("st", dbg_d["d_qp"], U6[:, 0:2048], reads=[rU6, r_inv])
              for hk in range(HK):
                  for t5 in range(NT5):
                      TTn(V, U4[:, 9728:10240], U4[:, 6144 + t5 * 512: 6144 + (t5 + 1) * 512],
                          U2[:, hk * T + t5 * 512: hk * T + (t5 + 1) * 512],
                          AL.mult, reads=[rU4, rU2], writes=[rU4])
                      for ttl in range(4):
                          tt = t5 * 4 + ttl
                          MM(pb[6][:, tt * 8 + hk: tt * 8 + hk + 1],
                             U4[:, 9728 + ttl * 128: 9728 + (ttl + 1) * 128],
                             c_onesb[:, 0:1], True, True,
                             reads=[rU4, r_const], writes=[r_pb[6]])
              CP(V, a_hq[:], pb[6][:, 0:128], reads=[r_pb[6]], writes=[r_ahq])
              if debug and hq == 0 and g == 0:
                  P.dma("st", dbg_d["d_A"], a_hq[:], reads=[r_ahq])
              for tt in range(NTT):
                  TSc(V, acc_a[:], U3[:, tt * 1024: tt * 1024 + 128],
                      a_hq[:, tt * 8: tt * 8 + 1], None, op0=AL.mult,
                      reads=[rU3, r_ahq], writes=[r_acca])
                  src_, dst_, rs, rd = acc_a, acc_b, r_acca, r_accb
                  for hk in range(1, HK):
                      STT(dst_[:], U3[:, tt * 1024 + hk * 128: tt * 1024 + (hk + 1) * 128],
                          a_hq[:, tt * 8 + hk: tt * 8 + hk + 1], src_[:], AL.mult, AL.add,
                          reads=[rU3, r_ahq, rs], writes=[rd])
                      src_, dst_, rs, rd = dst_, src_, rd, rs
                  TRp(pb[7][:, 0:128], src_[:], c_idnf[:], reads=[rs, r_const], writes=[r_pb[7]])
                  CP(V, mixT[:], pb[7][:, 0:128].rearrange("p (b i) -> p i b", i=16),
                     reads=[r_pb[7]], writes=[r_mixT])
                  r0 = 512 * hq + (g * 16 + tt) * 8
                  P.dma("st", mix_d[:, :, r0:r0 + 8], mixT[:],
                        reads=[r_mixT], writes=[r_dmix])
                  if debug and hq == 0 and tt == 0 and g == 0:
                      CP(V, U6[:, 0:128], src_[:], reads=[rs], writes=[rU6, r_xre[0]])
                      P.dma("st", dbg_d["d_mx"], U6[:, 0:128], reads=[rU6, r_xre[0]])

        for g in range(4):
            P.set_fence()
            build_tables(g)
            P.set_fence()
            phase_x(g)
            P.set_fence()
            phase_v(g)
            P.set_fence()
            phase_heads(g)
        P.set_fence()
        r_stg = [R(), R()]
        r_abt = [R(), R()]
        r_acc1 = [R(), R()]
        r_sq = [R(), R()]
        r_arr2 = [R(), R()]
        r_tb = [R(), R()]
        r_trow = [R(), R()]
        r_qsc = R()
        ACC1 = lambda m: U7[:, m * 128:(m + 1) * 128]
        STG1 = lambda m: U6[:, m * 1024:(m + 1) * 1024]
        ABT1 = lambda m: U6[:, 2048 + m * 1024: 2048 + (m + 1) * 1024]
        ARR2 = lambda m: U7[0:16, 1024 + m * 1024: 2048 + m * 1024]
        AC2 = lambda m: U7[0:16, 512 + m: 513 + m]
        wsrcs = [wsrc1_d, wsrc2_d]
        MEMS(V, qsc[0:1, 0:8], 0.0, writes=[r_qsc])
        MEMS(V, qth[0:1, 6:7], 1.0, writes=[r_qsc])
        MEMS(V, qth[0:1, 7:8], float(2.0 ** -30), writes=[r_qsc])
        TTn(V, qth[0:1, 8:9], qth[0:1, 6:7], qth[0:1, 7:8], AL.add,
            reads=[r_qsc], writes=[r_qsc])
        TTn(V, qth[0:1, 8:9], qth[0:1, 8:9], qth[0:1, 6:7], AL.subtract,
            reads=[r_qsc], writes=[r_qsc])
        TSc(V, qth[0:1, 9:10], qth[0:1, 8:9], float(-(2.0 ** 30)), 1.0,
            op0=AL.mult, op1=AL.add, reads=[r_qsc], writes=[r_qsc])
        for m in range(2):
            MEMS(V, ACC1(m), 0.0, writes=[r_acc1[m]])
        for b in range(128):
            for m in range(2):
                P.dma("ld" if m == 0 else "sc", STG1(m),
                      wsrcs[m][8 * b: 8 * (b + 1)].rearrange("s p f -> p s f"),
                      writes=[r_stg[m]])
                ACT(ABT1(m), STG1(m), AF.Abs, reads=[r_stg[m]], writes=[r_abt[m]])
                for j in range(8):
                    TTn(V, ACC1(m), ACC1(m), ABT1(m)[:, j * 128:(j + 1) * 128], AL.add,
                        reads=[r_abt[m]], writes=[r_acc1[m]])
        for m in range(2):
            P.dma("st", sq_d[m], ACC1(m), reads=[r_acc1[m]], writes=[r_sq[m]])
            P.dma("ld", ARR2(m), sq_d[m], reads=[r_sq[m]], writes=[r_arr2[m]])
            MEMS(V, AC2(m), 0.0, writes=[r_acc1[m]])
        for s in range(1024):
            for m in range(2):
                TTn(V, AC2(m), AC2(m), ARR2(m)[:, s:s + 1], AL.add,
                    reads=[r_arr2[m]], writes=[r_acc1[m]])
        for m in range(2):
            P.dma("st", tb_d[m], AC2(m), reads=[r_acc1[m]], writes=[r_tb[m]])
        P.dma("ld", U7[0:8, 520:522], tb_d[0], reads=[r_tb[0]], writes=[r_tb[0]])
        TTn(V, U7[0:8, 524:525], U7[0:8, 520:521], U7[0:8, 521:522], AL.add,
            reads=[r_tb[0]], writes=[r_qsc])
        P.dma("st", trow_d[0][0:1, 0:8], U7[0:8, 524:525], reads=[r_qsc], writes=[r_trow[0]])
        P.dma("ld", U7[0:1, 528:536], trow_d[0][0:1, 0:8], reads=[r_trow[0]], writes=[r_trow[0]])
        TTn(V, U7[0:1, 536:540], U7[0:1, 528:532], U7[0:1, 532:536], AL.add,
            reads=[r_trow[0]], writes=[r_qsc])
        TTn(V, U7[0:1, 540:542], U7[0:1, 536:538], U7[0:1, 538:540], AL.add,
            reads=[r_qsc], writes=[r_qsc])
        TTn(V, qsc[0:1, 0:1], U7[0:1, 540:541], U7[0:1, 541:542], AL.add,
            reads=[r_qsc], writes=[r_qsc])
        P.dma("ld", U7[0:2, 544:552], tb_d[1], reads=[r_tb[1]], writes=[r_tb[1]])
        MEMS(V, U7[0:2, 552:553], 0.0, writes=[r_qsc])
        for i in range(8):
            TTn(V, U7[0:2, 552:553], U7[0:2, 552:553], U7[0:2, 544 + i:545 + i], AL.add,
                reads=[r_tb[1]], writes=[r_qsc])
        P.dma("st", trow_d[1][0:1, 0:2], U7[0:2, 552:553], reads=[r_qsc], writes=[r_trow[1]])
        P.dma("ld", U7[0:1, 556:558], trow_d[1][0:1, 0:2], reads=[r_trow[1]], writes=[r_trow[1]])
        TTn(V, qsc[0:1, 4:5], U7[0:1, 556:557], U7[0:1, 557:558], AL.add,
            reads=[r_trow[1]], writes=[r_qsc])
        TSc(V, qth[0:1, 11:12], qth[0:1, 9:10], float(2.0 ** -25 - 2.0 ** -54),
            float(2.0 ** -54), op0=AL.mult, op1=AL.add, reads=[r_qsc], writes=[r_qsc])
        for m in range(2):
            TSc(V, qsc[0:1, 4 * m + 1:4 * m + 2], qsc[0:1, 4 * m:4 * m + 1],
                float(2.0 ** -24), 1e-8, op0=AL.mult, op1=AL.max,
                reads=[r_qsc], writes=[r_qsc])
            TSc(V, qth[0:1, 3 * m:3 * m + 1], qsc[0:1, 4 * m + 1:4 * m + 2], 0.5,
                reads=[r_qsc], writes=[r_qsc])
            TSc(V, qsc[0:1, 4 * m + 3:4 * m + 4], qsc[0:1, 4 * m + 1:4 * m + 2],
                qth[0:1, 11:12], reads=[r_qsc], writes=[r_qsc])
            if m == 1:
                TSc(V, qth[0:1, 10:11], qsc[0:1, 4 * m + 1:4 * m + 2], 6e-7,
                    reads=[r_qsc], writes=[r_qsc])
                TTn(V, qth[0:1, 10:11], qth[0:1, 10:11], qth[0:1, 9:10], AL.mult,
                    reads=[r_qsc], writes=[r_qsc])
                TTn(V, qth[0:1, 4:5], qsc[0:1, 4 * m + 3:4 * m + 4], qth[0:1, 10:11],
                    AL.subtract, reads=[r_qsc], writes=[r_qsc])
                TTn(V, qth[0:1, 5:6], qsc[0:1, 4 * m + 3:4 * m + 4], qth[0:1, 10:11],
                    AL.add, reads=[r_qsc], writes=[r_qsc])
            else:
                CP(V, qth[0:1, 1:2], qsc[0:1, 4 * m + 3:4 * m + 4],
                   reads=[r_qsc], writes=[r_qsc])
                CP(V, qth[0:1, 2:3], qsc[0:1, 4 * m + 3:4 * m + 4],
                   reads=[r_qsc], writes=[r_qsc])
        MM(pb[0][:, 0:6], c_onesf[0:1, :], qth[0:1, 0:6], True, True,
           reads=[r_qsc, r_const], writes=[r_pb[0]])
        CP(V, q_tc[:, 0:6], pb[0][:, 0:6], reads=[r_pb[0]], writes=[r_qtc])
        if debug:
            P.dma("st", dbg_d["d_qs"], qsc[0:1, 0:8], reads=[r_qsc])

        P.set_fence()
        r_aw = [R(), R()]
        r_prd = [R(), R()]
        r_sgn = [R(), R()]
        r_q2 = [R(), R()]

        def w1_store(c, base):
            dc, f8 = c // 8, c % 8
            for j in range(8):
                P.dma("st", w1q_d[f8 * 8 + j, :, dc * 128:(dc + 1) * 128],
                      U3[:, base + j * 128: base + (j + 1) * 128],
                      reads=[r_q2[(c % 2)]], writes=[r_dw1])

        def w2_store(c, base):
            P.dma("st", w2q_d[c // 2, :, (c % 2) * 1024:(c % 2 + 1) * 1024],
                  U3[:, base:base + 1024], reads=[r_q2[(c % 2)]], writes=[r_dw2])

        def quant(nchunks, src_fn, store_fn, m, dbg_tap):
            for c in range(nchunks):
                k = c % 2
                wslot = U6[:, k * 1024:(k + 1) * 1024]
                awslot = U6[:, 2048 + k * 1024: 2048 + (k + 1) * 1024]
                prdslot = U3[:, 4096 + k * 1024: 4096 + (k + 1) * 1024]
                sgnslot = U3[:, 6144 + k * 1024: 6144 + (k + 1) * 1024]
                qmslot = U3[:, 8192 + k * 1024: 8192 + (k + 1) * 1024]
                base = 2048 + k * 1024
                P.dma("ld", wslot, src_fn(c), writes=[r_stg[k]])
                ACT(awslot, wslot, AF.Abs, reads=[r_stg[k]], writes=[r_aw[k]])
                TSc(V, awslot, awslot, q_tc[:, 3 * m:3 * m + 1], None,
                    op0=AL.subtract, reads=[r_aw[k], r_qtc], writes=[r_aw[k]])
                TSc(V, prdslot, awslot, q_tc[:, 3 * m + 1:3 * m + 2], None,
                    op0=AL.is_gt, reads=[r_aw[k], r_qtc], writes=[r_prd[k]])
                P.op(V, lambda E_, aws=awslot, prs=prdslot, qms=qmslot, mm_=m:
                     E_.scalar_tensor_tensor(out=qms, in0=aws,
                                             scalar=q_tc[:, 3 * mm_ + 2:3 * mm_ + 3],
                                             in1=prs, op0=AL.is_gt, op1=AL.add),
                     reads=[r_aw[k], r_prd[k], r_qtc], writes=[r_prd[k]])
                ACT(sgnslot, wslot, AF.Sign, reads=[r_stg[k]], writes=[r_sgn[k]])
                TTn(V, U3[:, base:base + 1024], qmslot, sgnslot, AL.mult,
                    reads=[r_prd[k], r_sgn[k]], writes=[r_q2[k]])
                store_fn(c, base)
                if debug and dbg_tap and c == 0:
                    CP(V, U6[:, 2048:3072], U3[:, base:base + 1024], reads=[r_q2[k]],
                       writes=[r_aw[0]])
                    P.dma("st", dbg_d["d_w1q"], U6[:, 2048:3072], reads=[r_aw[0]])

        quant(128, lambda c: w1T_d[(c // 8) * 128:(c // 8 + 1) * 128,
                                   (c % 8) * 1024:(c % 8 + 1) * 1024], w1_store, 0, True)
        quant(128, lambda c: w2T_d[(c // 2) * 128:(c // 2 + 1) * 128,
                                   (c % 2) * 1024:(c % 2 + 1) * 1024], w2_store, 1, False)

        P.set_fence()
        if debug:
            P.dma("st", U6[:, 0:512], mix_d[0:128, 0:1, 0:512], reads=[r_dmix],
                  writes=[rU6, r_xre[0]])
            P.dma("st", dbg_d["d_m2"], U6[:, 0:512], reads=[rU6, r_xre[0]])
        for dco in range(NDC):
            P.dma("ld", U7[:, 0:2048], woT_d[dco], writes=[rU7])
            ACT(U1[:, dco * 2048:(dco + 1) * 2048], U7[:, 0:2048],
                AF.Copy, scale=1.0 / 64.0, reads=[rU7], writes=[rU1])
        for c in range(2):
            P.dma("ld", U7[0:1, 2048:3072], bo_d[0:1, c * 1024:(c + 1) * 1024],
                  writes=[rU7])
            ACT(U4[0:1, c * 1024:(c + 1) * 1024], U7[0:1, 2048:3072], AF.Copy,
                reads=[rU7], writes=[rU4])

        def ln_rows(sscale):
            TSc(V, row_a[0:1, :], pb[2][0:1, :], 1.0 / 2048.0, reads=[r_pb[2]], writes=[r_rows])
            TSc(V, row_b[0:1, :], pb[3][0:1, :], 1.0 / 2048.0, reads=[r_pb[3]], writes=[r_rows])
            ACT(row_c[0:1, :], row_a[0:1, :], AF.Square, reads=[r_rows], writes=[r_rows])
            TTn(V, row_b[0:1, :], row_b[0:1, :], row_c[0:1, :], AL.subtract,
                reads=[r_rows], writes=[r_rows])
            TSc(V, row_b[0:1, :], row_b[0:1, :], EPS_LN, None, op0=AL.add,
                reads=[r_rows], writes=[r_rows])
            RECIP(row_b[0:1, :], row_b[0:1, :], reads=[r_rows], writes=[r_rows])
            ACT(row_b[0:1, :], row_b[0:1, :], AF.Sqrt, reads=[r_rows], writes=[r_rows])
            CP(V, mo_bf[0:1, :], row_a[0:1, :], reads=[r_rows], writes=[r_mrow])
            TSc(V, so_bf[0:1, :], row_b[0:1, :], sscale, reads=[r_rows], writes=[r_mrow])

        def bcast_ms():
            MM(pb[4][:, :], c_onesb[0:1, 0:128], mo_bf[0:1, :], True, True,
               reads=[r_mrow, r_const], writes=[r_pb[4]])
            MM(pb[5][:, :], c_onesb[0:1, 0:128], so_bf[0:1, :], True, True,
               reads=[r_mrow, r_const], writes=[r_pb[5]])
            CP(V, BCM, pb[4][:, :], reads=[r_pb[4]], writes=[r_bcm])
            CP(V, BCS, pb[5][:, :], reads=[r_pb[5]], writes=[r_bcs])

        for t5 in range(NT5):
            sl5 = slice(t5 * 512, (t5 + 1) * 512)
            km = t5 % 2
            for dci in range(NDC):
                P.dma("ld", U2[:, km * 8192 + dci * 512: km * 8192 + (dci + 1) * 512],
                      mix_d[0:128, dci:dci + 1, sl5], reads=[r_dmix], writes=[rU2])
            for dco in range(NDC):
                bank = dco % 2
                for dci in range(NDC):
                    MM(pb[bank][:, :],
                       U1[:, dco * 2048 + dci * 128: dco * 2048 + (dci + 1) * 128],
                       U2[:, km * 8192 + dci * 512: km * 8192 + (dci + 1) * 512],
                       dci == 0, False, reads=[rU1, rU2], writes=[r_pb[bank]])
                MM(pb[bank][:, :], U4[0:1, dco * 128:(dco + 1) * 128], c_onesb[0:1, 0:512],
                   False, True, reads=[rU4, r_const], writes=[r_pb[bank]])
                ko2 = dco % 2
                CP(V, OT(ko2), pb[bank][:, :], reads=[r_pb[bank]], writes=[r_ot[ko2]])
                P.dma("st", spl_d[dco * 128:(dco + 1) * 128, :], OT(ko2),
                      reads=[r_ot[ko2]], writes=[r_dspl])
                CP(V, U4[:, 10240:10752], OT(ko2), reads=[r_ot[ko2]], writes=[rU4])
                ACT(U4[:, 9216:9728], OT(ko2), AF.Square, reads=[r_ot[ko2]], writes=[rU4])
                MM(pb[2][0:1, :], c_onesb[:, 0:1], U4[:, 10240:10752], dco == 0, dco == NDC - 1,
                   reads=[rU4, r_const], writes=[r_pb[2]])
                MM(pb[3][0:1, :], c_onesb[:, 0:1], U4[:, 9216:9728], dco == 0, dco == NDC - 1,
                   reads=[rU4, r_const], writes=[r_pb[3]])
            ln_rows(1.0)
            bcast_ms()
            for dc in range(NDC):
                kx = dc % 2
                P.dma("ld", XRE(kx), xrT_d[dc * 128:(dc + 1) * 128, sl5], writes=[r_xre[kx]])
                P.dma("ld", OT(kx), spl_d[dc * 128:(dc + 1) * 128, :],
                      reads=[r_dspl], writes=[r_ot[kx]])
                TTn(G, WRK, OT(kx), BCM, AL.subtract,
                    reads=[r_ot[kx], r_bcm], writes=[r_wrk])
                TTn(G, WRK, WRK, BCS, AL.mult, reads=[r_bcs], writes=[r_wrk])
                TTn(G, X1T, WRK, XRE(kx), AL.add,
                    reads=[r_wrk, r_xre[kx]], writes=[r_x1t])
                P.dma("st", x1_d[dc * 128:(dc + 1) * 128, sl5], X1T,
                      reads=[r_x1t], writes=[r_dx1])
                CP(V, U4[:, 10240:10752], X1T, reads=[r_x1t], writes=[rU4])
                ACT(U4[:, 9216:9728], X1T, AF.Square, reads=[r_x1t], writes=[rU4])
                MM(pb[2][0:1, :], c_onesb[:, 0:1], U4[:, 10240:10752], dc == 0, dc == NDC - 1,
                   reads=[rU4, r_const], writes=[r_pb[2]])
                MM(pb[3][0:1, :], c_onesb[:, 0:1], U4[:, 9216:9728], dc == 0, dc == NDC - 1,
                   reads=[rU4, r_const], writes=[r_pb[3]])
                if debug and t5 == 0 and dc == 0:
                    P.dma("st", dbg_d["d_x1"], X1T, reads=[r_x1t])
            ln_rows(0.5)
            bcast_ms()
            for dc in range(NDC):
                kx = dc % 2
                P.dma("ld", OT(kx), x1_d[dc * 128:(dc + 1) * 128, sl5],
                      reads=[r_dx1], writes=[r_ot[kx]])
                TTn(G, WRK, OT(kx), BCM, AL.subtract,
                    reads=[r_ot[kx], r_bcm], writes=[r_wrk])
                TTn(G, U4[:, 10240:10752], WRK, BCS, AL.mult,
                    reads=[r_wrk, r_bcs], writes=[rU4])
                P.dma("st", ln2_d[dc * 128:(dc + 1) * 128, sl5], U4[:, 10240:10752],
                      reads=[rU4], writes=[r_dln2])
                if debug and t5 == 0 and dc == 0:
                    CP(V, U6[:, 0:512], U4[:, 10240:10752], reads=[rU4],
                       writes=[rU6, r_xre[0]])
                    P.dma("st", dbg_d["d_ln2"], U6[:, 0:512], reads=[rU6, r_xre[0]])
            if debug and t5 == 0:
                P.dma("ld", U6[:, 0:512], spl_d[0:128, :], reads=[r_dspl],
                      writes=[rU6, r_xre[0]])
                P.dma("st", dbg_d["d_o"], U6[:, 0:512], reads=[rU6, r_xre[0]])

        P.set_fence()
        for c in range(8):
            P.dma("ld", U7[0:1, 2048:3072], b1_d[0:1, c * 1024:(c + 1) * 1024], writes=[rU7])
            ACT(U2[0:1, 8192 + c * 1024: 8192 + (c + 1) * 1024], U7[0:1, 2048:3072],
                AF.Copy, reads=[rU7], writes=[rU2])
        for c in range(2):
            P.dma("ld", U7[0:1, 2048:3072], b2_d[0:1, c * 1024:(c + 1) * 1024], writes=[rU7])
            ACT(U3[0:1, 6144 + c * 1024: 6144 + (c + 1) * 1024], U7[0:1, 2048:3072],
                AF.Copy, scale=2.0, reads=[rU7], writes=[rU3])

        for tq in range(4):
            sl5 = slice(tq * 512, (tq + 1) * 512)
            for dc in range(NDC):
                P.dma("ld", U2[:, dc * 512:(dc + 1) * 512],
                      ln2_d[dc * 128:(dc + 1) * 128, sl5], reads=[r_dln2], writes=[rU2])
            for fc in range(NFC):
                k = fc % 2
                P.dma("ld", U3[:, k * 2048:(k + 1) * 2048], w1q_d[fc], reads=[r_dw1], writes=[rU3])
                bank = fc % 2
                for dc in range(NDC):
                    MM(pb[bank][:, :], U3[:, k * 2048 + dc * 128: k * 2048 + (dc + 1) * 128],
                       U2[:, dc * 512:(dc + 1) * 512], dc == 0, False,
                       reads=[rU3, rU2], writes=[r_pb[bank]])
                MM(pb[bank][:, :], U2[0:1, 8192 + fc * 128: 8192 + (fc + 1) * 128],
                   c_onesb[0:1, 0:512], False, True, reads=[rU2, r_const], writes=[r_pb[bank]])
                ACT(U1[:, fc * 512:(fc + 1) * 512], pb[bank][:, :], AF.Gelu,
                    reads=[r_pb[bank]], writes=[rU1])
            if debug and tq == 0:
                CP(V, U6[:, 0:512], U1[:, 0:512], reads=[rU1], writes=[rU6, r_xre[0]])
                P.dma("st", dbg_d["d_h"], U6[:, 0:512], reads=[rU6, r_xre[0]])
            for dg in range(2):
                for fc in range(NFC):
                    k = fc % 2
                    P.dma("ld", U3[:, 4096 + k * 1024: 4096 + (k + 1) * 1024],
                          w2q_d[fc, :, dg * 1024:(dg + 1) * 1024], reads=[r_dw2], writes=[rU3])
                    for j in range(8):
                        MM(pb[j][:, :],
                           U3[:, 4096 + k * 1024 + j * 128: 4096 + k * 1024 + (j + 1) * 128],
                           U1[:, fc * 512:(fc + 1) * 512], fc == 0, False,
                           reads=[rU3, rU1], writes=[r_pb[j]])
                for j in range(8):
                    dco = dg * 8 + j
                    MM(pb[j][:, :], U3[0:1, 6144 + dco * 128: 6144 + (dco + 1) * 128],
                       c_onesb[0:1, 0:512], False, True, reads=[rU3, r_const], writes=[r_pb[j]])
                    ko2 = dco % 2
                    CP(V, OT(ko2), pb[j][:, :], reads=[r_pb[j]], writes=[r_ot[ko2]])
                    P.dma("st", spl_d[dco * 128:(dco + 1) * 128, :], OT(ko2),
                          reads=[r_ot[ko2]], writes=[r_dspl])
                    CP(V, U4[:, dco * 512:(dco + 1) * 512], OT(ko2),
                       reads=[r_ot[ko2]], writes=[rU4])
            for dco in range(NDC):
                ACT(U4[:, 8192:8704], U4[:, dco * 512:(dco + 1) * 512], AF.Square,
                    reads=[rU4], writes=[rU4])
                MM(pb[2][0:1, :], c_onesb[:, 0:1], U4[:, dco * 512:(dco + 1) * 512],
                   dco == 0, dco == NDC - 1, reads=[rU4, r_const], writes=[r_pb[2]])
                MM(pb[3][0:1, :], c_onesb[:, 0:1], U4[:, 8192:8704],
                   dco == 0, dco == NDC - 1, reads=[rU4, r_const], writes=[r_pb[3]])
            ln_rows(1.0)
            bcast_ms()
            for dco in range(NDC):
                kx = dco % 2
                P.dma("ld", XRE(kx), x1_d[dco * 128:(dco + 1) * 128, sl5],
                      reads=[r_dx1], writes=[r_xre[kx]])
                P.dma("ld", OT(kx), spl_d[dco * 128:(dco + 1) * 128, :],
                      reads=[r_dspl], writes=[r_ot[kx]])
                TTn(G, WRK, OT(kx), BCM, AL.subtract,
                    reads=[r_ot[kx], r_bcm], writes=[r_wrk])
                TTn(G, WRK, WRK, BCS, AL.mult, reads=[r_bcs], writes=[r_wrk])
                TTn(G, XRE(kx), WRK, XRE(kx), AL.add,
                    reads=[r_wrk, r_xre[kx]], writes=[r_xre[kx]])
                P.dma("st", out_d[dco * 128:(dco + 1) * 128, sl5], XRE(kx),
                      reads=[r_xre[kx]], writes=[r_dspl])
                if debug and tq == 0 and dco == 0:
                    P.dma("st", dbg_d["d_y"], OT(kx), reads=[r_ot[kx]])

        with nc.Block() as block:
            @block.sync
            def _(sync):
                P.replay("sync", sync)
                for q in ("st", "sc", "ld"):
                    for s_ in QSEMS[q]:
                        if P.cnt[s_] > 0:
                            sync.wait_ge(sems[s_], P.cnt[s_])

            @block.tensor
            def _(tensor):
                P.replay("tensor", tensor)

            @block.vector
            def _(vector):
                P.replay("vector", vector)

            @block.scalar
            def _(scalar):
                P.replay("scalar", scalar)

            @block.gpsimd
            def _(gpsimd):
                P.replay("gpsimd", gpsimd)

    return nc


def host_inputs(inputs, core):
    x = np.asarray(inputs["L0_attn_hidden_states"], np.float32)
    beta, hg = core // 4, core % 4
    xb = x[beta]
    xbT = np.ascontiguousarray(xb.T)
    xr = xb[hg * T:(hg + 1) * T]
    xrT = np.ascontiguousarray(xr.T)
    Wq = np.asarray(inputs["L0_attn_Wq"], np.float32)
    wq4 = Wq[hg * 512:(hg + 1) * 512]
    wqT = np.ascontiguousarray(
        wq4.reshape(4, 128, NDC, 128).transpose(0, 3, 2, 1).reshape(4, 128, NDC * 128))
    bq4 = np.asarray(inputs["L0_attn_bq"], np.float32)[hg * 512:(hg + 1) * 512].reshape(1, 512)
    return xbT, xrT, wqT, bq4


_SHARED = {}


def shared_inputs(inputs):
    if _SHARED.get("done"):
        return _SHARED
    f32 = np.float32
    bf = ml_dtypes.bfloat16
    Wk = np.asarray(inputs["L0_attn_Wk"], f32)
    Wv = np.asarray(inputs["L0_attn_Wv"], f32)
    Wo = np.asarray(inputs["L0_attn_Wo"], f32)
    R = np.asarray(inputs["L0_attn_rand_matrix"], f32)
    w1 = np.asarray(inputs["L0_ffn_w1"], f32)
    w2 = np.asarray(inputs["L0_ffn_w2"], f32)

    sh = {}
    sh["wkT"] = np.ascontiguousarray(
        Wk.reshape(HK, 128, NDC, 128).transpose(0, 3, 2, 1).reshape(HK, 128, NDC * 128))
    sh["wvT"] = np.ascontiguousarray(Wv.T.reshape(NDC, 128, 1024))
    sh["woT"] = np.ascontiguousarray(
        Wo.reshape(NDC, 128, NDC, 128).transpose(0, 3, 2, 1).reshape(NDC, 128, NDC * 128))
    RT = np.ascontiguousarray(R.T)
    sh["rT"] = RT
    sh["r2T"] = np.ascontiguousarray(np.concatenate([RT[64:], RT[:64]], axis=0))
    sh["w1T"] = np.ascontiguousarray(w1.T)
    sh["w2T"] = np.ascontiguousarray(w2.T)

    def make_wsrc(w):
        R_, C_ = w.shape
        WR, WC = R_ // 32, C_ // 32
        QC = WC // 32
        arr = w.reshape(WR, 32, WC, 32).transpose(1, 3, 0, 2).reshape(32, 32, WR * WC)
        wi = np.arange(WR)[:, None]
        wj = np.arange(WC)[None, :]
        Lp = (((wi // 32) * QC + wj // 32) * 32 + wi % 32) * 32 + wj % 32
        perm = np.empty(WR * WC, np.int64)
        perm[Lp.reshape(-1)] = np.arange(WR * WC)
        return np.ascontiguousarray(arr[:, :, perm].reshape(1024, 128, 128))

    sh["wsrc1"] = make_wsrc(w1)
    sh["wsrc2"] = make_wsrc(w2)
    sh["bk"] = np.asarray(inputs["L0_attn_bk"], f32).reshape(1, HK * HD)
    sh["bv"] = np.asarray(inputs["L0_attn_bv"], f32).reshape(1, HK * HD)
    sh["bo"] = np.asarray(inputs["L0_attn_bo"], f32).reshape(1, D)
    sh["b1"] = np.asarray(inputs["L0_ffn_b1"], f32).reshape(1, DFF)
    sh["b2"] = np.asarray(inputs["L0_ffn_b2"], f32).reshape(1, D)
    sh["pos"] = np.arange(SEQ, dtype=f32).reshape(1, SEQ)
    sh["ones_f"] = np.ones((1, 128), f32)
    sh["idnf"] = np.eye(128, dtype=f32)
    sh["iidx"] = np.arange(E, dtype=f32).reshape(1, E)
    nsel4 = np.zeros((128, 16), bf)
    for t5 in range(4):
        nsel4[:, t5 * 4 + t5] = 1
    sh["nsel4"] = nsel4
    bsel4 = np.zeros((4, 512), bf)
    for t5 in range(4):
        bsel4[t5, t5 * 128:(t5 + 1) * 128] = 1
    sh["bsel4"] = bsel4
    sh["tbl"] = _rope_tables()
    _SHARED.update(sh)
    _SHARED["done"] = True
    return _SHARED


def kernel(**inputs):
    nc = build_program(debug=0)
    sh = shared_inputs(inputs)
    in_maps = []
    for core in range(NCORES):
        xbT, xrT, wqT, bq4 = host_inputs(inputs, core)
        m = {k: v for k, v in sh.items() if k != "done"}
        m["xbT"] = xbT
        m["xrT"] = xrT
        m["wqT"] = wqT
        m["bq"] = bq4
        in_maps.append(m)
    res = run_bass_kernel_spmd(nc, in_maps, core_ids=list(range(NCORES)))
    outs = [res.results[c]["outT"] for c in range(NCORES)]
    full = np.concatenate([o.T for o in outs], axis=0)
    return np.ascontiguousarray(full.reshape(2, SEQ, D).astype(np.float32, copy=False))
